# version 13
# baseline (speedup 1.0000x reference)
import numpy as np
import ml_dtypes
import concourse.bass as bass
import concourse.mybir as mybir
from concourse.bass_utils import run_bass_kernel_spmd

F32 = mybir.dt.float32
BF16 = mybir.dt.bfloat16
I32 = mybir.dt.int32
AF = mybir.ActivationFunctionType
ALU = mybir.AluOpType
AX = mybir.AxisListType.X

B, S, D = 2, 4096, 4096
NCORE = 8
TOK = 1024
NDCH = 32
DFF = 11008
NFT = 86
NH, QR, KVR, RD, ND, VD = 32, 1536, 512, 64, 128, 128
QK = RD + ND
NQRCH = QR // 128
EPS_LN = 1e-5
EPS_RMS = 1e-6
OFFS = 60.0
H1T = 29
DSUB = [(0, 29), (29, 58), (58, 86)]

DPC = D // NCORE
QRPC = QR // NCORE
KVRPC = KVR // NCORE
DFFP = 11264
DFPC = DFFP // NCORE
GU_RANGES = [(0, 2816), (2816, 5632), (5632, 8448), (8448, 11008)]
CCH = 512

ENGS = ("sync", "tensor", "vector", "scalar", "gpsimd")


class Plan:
    def __init__(self):
        self.ops = {e: [] for e in ENGS}
        self.ctr = {}
        self.seen = {e: {} for e in ENGS}
        self.sems = {}
        self.bg = []

    def emit(self, eng, fn):
        self.ops[eng].append(fn)

    def bump(self, name, amt):
        self.ctr[name] = self.ctr.get(name, 0) + amt
        return self.ctr[name]

    def cur(self, name):
        return self.ctr.get(name, 0)

    def wait(self, eng, name, val):
        if val is None or val <= 0:
            return
        if self.seen[eng].get(name, 0) >= val:
            return
        self.seen[eng][name] = val
        sem = self.sems[name]
        self.emit(eng, lambda e, sem=sem, val=val: e.wait_ge(sem, val))

    ENG_SEM = {"vector": "s_ve", "scalar": "s_ac", "gpsimd": "s_gpc"}

    def op(self, eng, fn, inc=None):
        if inc is None and eng in self.ENG_SEM:
            name = self.ENG_SEM[eng]
            self.wait(eng, name, self.cur(name))
            inc = (name, 1)
        elif inc is not None and eng in self.ENG_SEM and inc[0] == self.ENG_SEM[eng]:
            self.wait(eng, inc[0], self.cur(inc[0]))
        if inc is None:
            self.emit(eng, fn)
            return None
        name, amt = inc
        val = self.bump(name, amt)
        sem = self.sems[name]
        self.emit(eng, lambda e, fn=fn, sem=sem, amt=amt: fn(e).then_inc(sem, amt))
        return val

    def dma(self, eng, sem, out, in_, **kw):
        return self.op(eng, lambda e, out=out, in_=in_, kw=kw:
                       e.dma_start(out=out, in_=in_, **kw), inc=(sem, 16))

    def wait_tot(self, eng, *names):
        for n in names:
            self.wait(eng, n, self.cur(n))

    def cc(self, kind, groups, ins, outs):
        val = self.bump("s_cc", 1)
        sem = self.sems["s_cc"]
        self.emit("gpsimd", lambda e, kind=kind, groups=groups, ins=ins, outs=outs,
                  sem=sem: e.collective_compute(kind, ALU.bypass,
                                                replica_groups=groups,
                                                ins=ins, outs=outs).then_inc(sem))
        return val

    def add_bg(self, thunk):
        self.bg.append(thunk)

    def drain_bg(self, k=1):
        for _ in range(k):
            if not self.bg:
                return
            self.bg.pop(0)(self)

    def drain_all_bg(self):
        while self.bg:
            self.bg.pop(0)(self)

    def psum_fence(self):
        self.wait("tensor", "s_ve", self.cur("s_ve"))
        self.wait("tensor", "s_ac", self.cur("s_ac"))

    def sync_fence(self):
        self.wait("sync", "s_pe", self.cur("s_pe"))
        self.wait("sync", "s_ve", self.cur("s_ve"))
        self.wait("sync", "s_ac", self.cur("s_ac"))


def build(debug=False):
    nc = bass.Bass(target_bir_lowering=False)
    P = Plan()

    def param(name, shape, dtype, out=False):
        return nc.declare_dram_parameter(name, list(shape), dtype, isOutput=out).ap()

    def idram(name, shape, dtype):
        return nc.dram_tensor(name, list(shape), dtype, kind="Internal").ap()

    def sdram(name, shape, dtype):
        return nc.dram_tensor(name, list(shape), dtype, kind="Internal",
                              addr_space="Shared").ap()

    xT_in = param("xT_in", (D, TOK), F32)
    gate_sl = param("gate_sl", (DFF, DPC), I32)
    up_sl = param("up_sl", (DFF, DPC), I32)
    down_sl = param("down_sl", (D, DFPC), I32)
    gate_s = param("gate_s", (128, NFT), F32)
    up_s = param("up_s", (128, NFT), F32)
    down_s = param("down_s", (128, NDCH), F32)
    qa_sl = param("qa_sl", (DPC, QR), F32)
    qb_sl = param("qb_sl", (QRPC, NH * QK), F32)
    g_qa_sl = param("g_qa_sl", (QRPC, 1), F32)
    kva_sl = param("kva_sl", (DPC, KVR + RD), F32)
    kvbN_sl = param("kvbN_sl", (KVRPC, NH * ND), F32)
    kvbV_sl = param("kvbV_sl", (KVRPC, NH * VD), F32)
    g_kva_sl = param("g_kva_sl", (KVRPC, 1), F32)
    wo_sl = param("wo_sl", (DPC, D), F32)
    cosq = param("cosq", (RD, TOK), BF16)
    sinq = param("sinq", (RD, TOK), BF16)
    cosk = param("cosk", (RD, S), BF16)
    sink = param("sink", (RD, S), BF16)
    out_xT = param("out_xT", (D, TOK), F32, out=True)

    dbg = {}
    if debug:
        for name, shape, dt_ in [("d_xn0", (D, TOK), BF16), ("d_x1", (D, TOK), F32),
                                 ("d_qc", (QR, TOK), BF16), ("d_ckc", (KVR, TOK), BF16),
                                 ("d_x2", (D, TOK), F32), ("d_key", (4 * NH * ND, TOK), BF16),
                                 ("d_vv", (4 * TOK, NH * VD), BF16), ("d_oT", (D, TOK), BF16)]:
            dbg[name] = param(name, shape, dt_, out=True)

    gate_nb = idram("gate_nb", (DFF, DPC), BF16)
    up_nb = idram("up_nb", (DFF, DPC), BF16)
    down_nb = idram("down_nb", (D, DFPC), BF16)
    gu_pcA = idram("gu_pcA", (DPC, DFF), BF16)
    gu_pcB = idram("gu_pcB", (DPC, DFF), BF16)
    down_pc = idram("down_pc", (DFPC, D), BF16)
    qa_pc = idram("qa_pc", (DPC, QR), BF16)
    qb_pc = idram("qb_pc", (QRPC, NH * QK), BF16)
    kva_pc = idram("kva_pc", (DPC, KVR + RD), BF16)
    kvbN_pc = idram("kvbN_pc", (KVRPC, NH * ND), BF16)
    kvbV_pc = idram("kvbV_pc", (KVRPC, NH * VD), BF16)
    wo_pc = idram("wo_pc", (DPC, D), BF16)
    gu_TA = sdram("gu_TA", (D, DFF), BF16)
    gu_TB = sdram("gu_TB", (D, DFF), BF16)
    down_T = sdram("down_T", (DFFP, D), BF16)
    qa_T = sdram("qa_T", (D, QR), BF16)
    qb_T = sdram("qb_T", (QR, NH * QK), BF16)
    kva_T = sdram("kva_T", (D, KVR + RD), BF16)
    kvbN_T = sdram("kvbN_T", (KVR, NH * ND), BF16)
    kvbV_T = sdram("kvbV_T", (KVR, NH * VD), BF16)
    wo_T = sdram("wo_T", (D, D), BF16)
    h2_d = idram("h2_d", ((NFT - H1T) * 128, TOK), BF16)
    part_a = idram("part_a", (D, TOK), F32)
    part_b = idram("part_b", (D, TOK), F32)
    xres1 = idram("xres1", (D, TOK), F32)
    xres2 = idram("xres2", (D, TOK), F32)
    k_in0 = idram("k_in0", (NH * ND // 2, TOK), BF16)
    k_in1 = idram("k_in1", (NH * ND // 2, TOK), BF16)
    v_in0 = idram("v_in0", (TOK, NH * VD // 2), BF16)
    v_in1 = idram("v_in1", (TOK, NH * VD // 2), BF16)
    kr_in = idram("kr_in", (RD, TOK), BF16)
    km_in = idram("km_in", (1, NH), F32)
    km_T = idram("km_T", (4, NH), F32)
    k_T0 = idram("k_T0", (4 * NH * ND // 2, TOK), BF16)
    k_T1 = idram("k_T1", (4 * NH * ND // 2, TOK), BF16)
    v_T0 = idram("v_T0", (4 * TOK, NH * VD // 2), BF16)
    v_T1 = idram("v_T1", (4 * TOK, NH * VD // 2), BF16)
    kr_T = idram("kr_T", (4 * RD, TOK), BF16)

    groups8 = [list(range(8))]
    groups4 = [[0, 1, 2, 3], [4, 5, 6, 7]]

    H1C = H1T * TOK
    WG = [H1C, H1C + 4096, H1C + 8192, H1C + 12288]
    GEV = [H1C + 16384, H1C + 17408]
    HOV = [H1C + 18432, H1C + 19456]
    ST_X = [37120, 38144]
    ST_SQ = [39168, 40192]
    AT_QC = 0
    AT_KB = [12288, 16384]
    AT_VB = [20480, 24576]
    AT_SQK = 28672
    AT_ATB = [32768, 33792]
    AT_QW = 34816
    AT_QN = 37120
    AT_QA = 38144
    AT_R1 = 39168
    AT_R2 = 40192
    AT_CSQ = 41216
    AT_SSQ = 42240
    AT_KRA = 43264
    AB_COLS = 50176

    from contextlib import ExitStack
    with ExitStack() as _stack:
        xnT = _stack.enter_context(nc.sbuf_tensor("xnT", [128, NDCH * TOK], BF16))
        bc0 = _stack.enter_context(nc.sbuf_tensor("bc0", [128, TOK], F32))
        bc1 = _stack.enter_context(nc.sbuf_tensor("bc1", [128, TOK], F32))
        vrow1 = _stack.enter_context(nc.sbuf_tensor("vrow1", [128, TOK], F32))
        vrow2 = _stack.enter_context(nc.sbuf_tensor("vrow2", [128, TOK], F32))
        ones_b = _stack.enter_context(nc.sbuf_tensor("ones_b", [128, 1], BF16))
        onesrow = _stack.enter_context(nc.sbuf_tensor("onesrow", [1, 128], F32))
        scal = _stack.enter_context(nc.sbuf_tensor("scal", [32, 8], F32))
        tbuf = _stack.enter_context(nc.sbuf_tensor("tbuf", [128, 2048], BF16))
        cio_i = _stack.enter_context(nc.sbuf_tensor("cio_i", [128, 2 * CCH], I32))
        cio_o = _stack.enter_context(nc.sbuf_tensor("cio_o", [128, 2 * CCH], BF16))
        cio_f = _stack.enter_context(nc.sbuf_tensor("cio_f", [128, 2 * CCH], F32))
        sv_g = _stack.enter_context(nc.sbuf_tensor("sv_g", [128, NFT], F32))
        sv_u = _stack.enter_context(nc.sbuf_tensor("sv_u", [128, NFT], F32))
        sv_d = _stack.enter_context(nc.sbuf_tensor("sv_d", [128, NDCH], F32))
        sv_q = _stack.enter_context(nc.sbuf_tensor("sv_q", [128, 2], F32))
        sv_k = _stack.enter_context(nc.sbuf_tensor("sv_k", [128, 1], F32))
        AB = _stack.enter_context(nc.sbuf_tensor("AB", [128, AB_COLS], BF16))
        AFX = _stack.enter_context(nc.sbuf_tensor("AFX", [128, 2 * TOK], F32))
        xfb = _stack.enter_context(nc.sbuf_tensor("xfb", [128, TOK], F32))
        ps0 = _stack.enter_context(nc.psum_tensor("ps0", [128, TOK], F32))
        ps1 = _stack.enter_context(nc.psum_tensor("ps1", [128, TOK], F32))
        ps2 = _stack.enter_context(nc.psum_tensor("ps2", [128, TOK], F32))
        ps3 = _stack.enter_context(nc.psum_tensor("ps3", [128, TOK], F32))
        SEM_NAMES = ["s_cc", "s_pe", "s_ac", "s_ve",
                     "svl", "cvi", "cvo0", "cvo1", "tr", "trs",
                     "fb0", "fb1", "xfb", "w0", "w1",
                     "hst0", "hst1", "hld", "pst0", "pst1", "xw0", "xw1",
                     "kvw0", "kvw1", "vst", "krw", "tab",
                     "kv0", "kv1", "qw", "qsw", "aug", "s_gpc"]
        for _sn in SEM_NAMES:
            P.sems[_sn] = _stack.enter_context(nc.semaphore(_sn))
        block = _stack.enter_context(nc.Block())
        PS = [ps0, ps1, ps2, ps3]

        P.op("vector", lambda e: e.memset(ones_b[:, :], 1.0), inc=("s_ve", 1))
        P.op("vector", lambda e: e.memset(onesrow[:, :], 1.0), inc=("s_ve", 1))
        P.dma("gpsimd", "svl", sv_g[:, :], gate_s)
        P.dma("gpsimd", "svl", sv_u[:, :], up_s)
        P.dma("gpsimd", "svl", sv_d[:, :], down_s)
        P.dma("gpsimd", "svl", sv_q[:, 0:1], g_qa_sl[0:128, :])
        P.dma("gpsimd", "svl", sv_q[0:64, 1:2], g_qa_sl[128:192, :])
        P.dma("gpsimd", "svl", sv_k[0:64, 0:1], g_kva_sl[:, :])
        P.wait("vector", "svl", P.cur("svl"))

        conv = {"n": 0, "cast": {}, "store": {}, "tr": None}

        def conv_chain(src, dst, r0, nr, c0, cw, is_f32, scale_ap, const,
                       eng_ld="gpsimd"):
            n = conv["n"]
            conv["n"] = n + 1
            par = n % 2
            if is_f32:
                ib = cio_f[0:nr, par * CCH: par * CCH + cw]
            else:
                ib = cio_i[0:nr, par * CCH: par * CCH + cw]
            ob = cio_o[0:nr, par * CCH: par * CCH + cw]
            if n - 2 in conv["cast"]:
                P.wait(eng_ld, "s_ve", conv["cast"][n - 2])
                P.wait("vector", "cvo%d" % par, conv["store"][n - 2])
                del conv["cast"][n - 2], conv["store"][n - 2]
            ld = P.dma(eng_ld, "cvi", ib, src[r0:r0 + nr, c0:c0 + cw])
            P.wait("vector", "cvi", ld)
            if scale_ap is None and const is None:
                cv = P.op("vector", lambda e, o=ob, i=ib: e.tensor_copy(o, i),
                          inc=("s_ve", 1))
            elif scale_ap is None:
                cv = P.op("vector", lambda e, o=ob, i=ib, k=const:
                          e.tensor_scalar_mul(o, i, k), inc=("s_ve", 1))
            elif const is None:
                cv = P.op("vector", lambda e, o=ob, i=ib, s=scale_ap:
                          e.tensor_scalar_mul(o, i, s), inc=("s_ve", 1))
            else:
                cv = P.op("vector", lambda e, o=ob, i=ib, s=scale_ap, k=const:
                          e.tensor_scalar(o, i, s, k, ALU.mult, ALU.mult),
                          inc=("s_ve", 1))
            P.wait("gpsimd", "s_ve", cv)
            st = P.dma("gpsimd", "cvo%d" % par, dst[r0:r0 + nr, c0:c0 + cw], ob)
            conv["cast"][n] = cv
            conv["store"][n] = st

        def conv_transp(nb, pc, c, dstc0, rw, srcr0=None):
            if srcr0 is None:
                srcr0 = dstc0
            if conv["tr"] is not None:
                P.wait("sync", "trs", conv["tr"])
            tr = P.dma("sync", "tr", tbuf[:, 0:rw],
                       nb[srcr0:srcr0 + rw, c * 128:(c + 1) * 128], transpose=True)
            P.wait("gpsimd", "tr", tr)
            st = P.dma("gpsimd", "trs", pc[c * 128:(c + 1) * 128, dstc0:dstc0 + rw],
                       tbuf[:, 0:rw])
            conv["tr"] = st

        def cast_matrix_i32(src, dst, nrows, sv, hook=None):
            for t in range(nrows // 128):
                for c0 in range(0, src.shape[1], CCH):
                    cw = min(CCH, src.shape[1] - c0)
                    conv_chain(src, dst, t * 128, 128, c0, cw, False,
                               sv[:, t:t + 1], None, eng_ld="gpsimd")
                    if hook is not None:
                        hook()

        HDFF = (NFT // 2) * 128

        def gu_ready(t):
            return gu_ag[0] if t < NFT // 2 else gu_ag[1]

        gu_ag = []
        down_ag_v, qa_ag_v, qb_ag_v = [], [], []
        kva_ag_v, kvbN_ag_v, kvbV_ag_v, wo_ag_v = [], [], [], []

        def bg_chain(src, dst, r0, nr, c0, cw, is_f32, scale_fn, const, fin):
            def run(_):
                sc = scale_fn() if scale_fn else None
                conv_chain(src, dst, r0, nr, c0, cw, is_f32, sc, const)
                if fin is not None:
                    fin()
            return run

        def add_f32_matrix(src, dst, nrows, ncols, aglist, out_T, scale_fn=None,
                           const=None):
            tiles = []
            nrt = (nrows + 127) // 128
            for t in range(nrt):
                r0 = t * 128
                nr = min(128, nrows - r0)
                for c0 in range(0, ncols, CCH):
                    cw = min(CCH, ncols - c0)
                    tiles.append((r0, nr, c0, cw, t))
            for i, (r0, nr, c0, cw, t) in enumerate(tiles):
                fin = None
                if i == len(tiles) - 1:
                    def fin(agl=aglist, dd=dst, oT=out_T):
                        P.wait_tot("gpsimd", "cvo0", "cvo1")
                        agl.append(P.cc("AllGather", groups8,
                                        [dd.opt()], [oT.opt()]))
                sf = (lambda t=t, nr=nr: scale_fn(t, nr)) if scale_fn else None
                P.add_bg(bg_chain(src, dst, r0, nr, c0, cw, True, sf, const, fin))

        qk_scale = float(QK) ** -0.5
        add_f32_matrix(qa_sl, qa_pc, DPC, QR, qa_ag_v, qa_T)
        add_f32_matrix(kva_sl, kva_pc, DPC, KVR + RD, kva_ag_v, kva_T)
        add_f32_matrix(qb_sl, qb_pc, QRPC, NH * QK, qb_ag_v, qb_T,
                       scale_fn=lambda t, nr: sv_q[0:nr, t:t + 1], const=qk_scale)
        add_f32_matrix(kvbN_sl, kvbN_pc, KVRPC, NH * ND, kvbN_ag_v, kvbN_T,
                       scale_fn=lambda t, nr: sv_k[0:nr, 0:1])
        add_f32_matrix(kvbV_sl, kvbV_pc, KVRPC, NH * VD, kvbV_ag_v, kvbV_T,
                       scale_fn=lambda t, nr: sv_k[0:nr, 0:1])
        add_f32_matrix(wo_sl, wo_pc, DPC, D, wo_ag_v, wo_T)

        stats_state = {"pe": 0}

        def stats_tile(xsrc_f32_sb, dt):
            par = dt % 2
            xbb = AB[:, ST_X[par]:ST_X[par] + TOK]
            sq = AB[:, ST_SQ[par]:ST_SQ[par] + TOK]
            P.wait("vector", "s_pe", stats_state["pe"] if dt >= 2 else 0)
            cv = P.op("vector", lambda e, o=xbb, i=xsrc_f32_sb:
                      e.tensor_copy(o, i), inc=("s_ve", 1))
            P.wait("scalar", "s_ve", cv)
            sv = P.op("scalar", lambda e, o=sq, i=xbb:
                      e.activation(o, i, AF.Square), inc=("s_ac", 1))
            P.wait("tensor", "s_ac", sv)
            f, l = (dt == 0), (dt == NDCH - 1)
            P.op("tensor", lambda e, i=xbb, f=f, l=l:
                 e.matmul(ps2[0:1, 0:512], ones_b[:, :], i[:, 0:512], start=f, stop=l))
            P.op("tensor", lambda e, i=xbb, f=f, l=l:
                 e.matmul(ps2[0:1, 512:1024], ones_b[:, :], i[:, 512:1024],
                          start=f, stop=l))
            P.op("tensor", lambda e, i=sq, f=f, l=l:
                 e.matmul(ps3[0:1, 0:512], ones_b[:, :], i[:, 0:512], start=f, stop=l))
            pe = P.op("tensor", lambda e, i=sq, f=f, l=l:
                      e.matmul(ps3[0:1, 512:1024], ones_b[:, :], i[:, 512:1024],
                               start=f, stop=l), inc=("s_pe", 1))
            stats_state["pe"] = pe
            return cv

        def bcast_row(src_row, psd, dst):
            P.op("tensor", lambda e: e.matmul(psd[:, 0:512], onesrow[:, :],
                                              src_row[:, 0:512], start=True, stop=True))
            pe = P.op("tensor", lambda e: e.matmul(psd[:, 512:1024], onesrow[:, :],
                                                   src_row[:, 512:1024],
                                                   start=True, stop=True),
                      inc=("s_pe", 1))
            P.wait("vector", "s_pe", pe)
            return P.op("vector", lambda e: e.tensor_copy(dst[:, :], psd[:, :]),
                        inc=("s_ve", 1))

        def finalize_ln():
            mean = bc0[0:1, :]
            var = vrow1[0:1, :]
            rstd = bc1[0:1, :]
            t = vrow2[0:1, :]
            P.wait("vector", "s_pe", P.cur("s_pe"))
            P.op("vector", lambda e: e.tensor_scalar_mul(mean, ps2[0:1, :], 1.0 / D))
            v2 = P.op("vector", lambda e: e.tensor_scalar_mul(var, ps3[0:1, :], 1.0 / D),
                      inc=("s_ve", 1))
            P.wait("vector", "s_ve", v2)
            P.op("vector", lambda e: e.tensor_mul(t, mean, mean))
            P.op("vector", lambda e: e.tensor_sub(var, var, t))
            v3 = P.op("vector", lambda e: e.tensor_scalar_add(var, var, EPS_LN),
                      inc=("s_ve", 1))
            P.wait("scalar", "s_ve", v3)
            a1 = P.op("scalar", lambda e: e.activation(rstd, var, AF.Sqrt),
                      inc=("s_ac", 1))
            P.wait("vector", "s_ac", a1)
            v4 = P.op("vector", lambda e: e.reciprocal(rstd, rstd), inc=("s_ve", 1))
            P.wait("tensor", "s_ve", v4)
            P.psum_fence()
            bcast_row(mean, ps0, bc0)
            return bcast_row(rstd, ps1, bc1)

        def ln_apply(xsrc_d):
            bdone = finalize_ln()
            P.wait_tot("sync", "xw0", "xw1")
            P.wait("sync", "s_ve", P.cur("s_ve"))
            applied = {}
            for c in range(NDCH):
                par = c % 2
                xb = AFX[:, par * TOK:(par + 1) * TOK]
                if c >= 2:
                    P.wait("sync", "s_ve", applied[c - 2])
                ld = P.dma("sync", "fb%d" % par, xb, xsrc_d[c * 128:(c + 1) * 128, :])
                P.wait("vector", "fb%d" % par, ld)
                P.wait("vector", "s_ve", bdone)
                P.op("vector", lambda e, xb=xb: e.tensor_sub(xb, xb, bc0[:, :]))
                applied[c] = P.op("vector", lambda e, xb=xb, c=c:
                                  e.tensor_mul(xnT[:, c * TOK:(c + 1) * TOK], xb,
                                               bc1[:, :]), inc=("s_ve", 1))
            return P.cur("s_ve")

        P.psum_fence()
        ln0 = {"c": 0, "cv": {}}

        def ln0_step():
            c = ln0["c"]
            if c >= NDCH:
                return
            ln0["c"] = c + 1
            par = c % 2
            xb = AFX[:, par * TOK:(par + 1) * TOK]
            if c >= 2:
                P.wait("sync", "s_ve", ln0["cv"][c - 2])
            ld = P.dma("sync", "fb%d" % par, xb, xT_in[c * 128:(c + 1) * 128, :])
            P.wait("vector", "fb%d" % par, ld)
            ln0["cv"][c] = stats_tile(xb, c)

        NHT = NFT // 2
        cast_matrix_i32(gate_sl[0:HDFF, :], gate_nb[0:HDFF, :], HDFF,
                        sv_g[:, 0:NHT], hook=ln0_step)
        while ln0["c"] < NDCH:
            ln0_step()
        g_cvo = (P.cur("cvo0"), P.cur("cvo1"))
        xn_ready = ln_apply(xT_in)
        cast_matrix_i32(up_sl[0:HDFF, :], up_nb[0:HDFF, :], HDFF, sv_u[:, 0:NHT])
        P.wait("sync", "cvo0", g_cvo[0])
        P.wait("sync", "cvo1", g_cvo[1])
        for c in range(DPC // 128):
            for r0 in range(0, HDFF, 2048):
                conv_transp(gate_nb, gu_pcA, c, r0, min(2048, HDFF - r0))
        P.wait_tot("sync", "cvo0", "cvo1")
        for c in range(DPC // 128):
            for r0 in range(0, HDFF, 2048):
                conv_transp(up_nb, gu_pcA, c, HDFF + r0, min(2048, HDFF - r0),
                            srcr0=r0)
        P.wait_tot("gpsimd", "trs")
        gu_ag.append(P.cc("AllGather", groups8, [gu_pcA.opt()], [gu_TA.opt()]))

        cast_matrix_i32(gate_sl[HDFF:DFF, :], gate_nb[HDFF:DFF, :], DFF - HDFF,
                        sv_g[:, NHT:NFT])
        cast_matrix_i32(up_sl[HDFF:DFF, :], up_nb[HDFF:DFF, :], DFF - HDFF,
                        sv_u[:, NHT:NFT])
        P.wait_tot("sync", "cvo0", "cvo1")
        for c in range(DPC // 128):
            for r0 in range(0, DFF - HDFF, 2048):
                conv_transp(gate_nb, gu_pcB, c, r0,
                            min(2048, DFF - HDFF - r0), srcr0=HDFF + r0)
        for c in range(DPC // 128):
            for r0 in range(0, DFF - HDFF, 2048):
                conv_transp(up_nb, gu_pcB, c, HDFF + r0,
                            min(2048, DFF - HDFF - r0), srcr0=HDFF + r0)
        P.wait_tot("gpsimd", "trs")
        gu_ag.append(P.cc("AllGather", groups8, [gu_pcB.opt()], [gu_TB.opt()]))

        cast_matrix_i32(down_sl, down_nb, D, sv_d)
        P.wait_tot("sync", "cvo0", "cvo1")
        for c in range(DFPC // 128):
            conv_transp(down_nb, down_pc, c, 0, 2048)
            conv_transp(down_nb, down_pc, c, 2048, 2048)
        P.wait_tot("gpsimd", "trs")
        down_ag_v.append(P.cc("AllGather", groups8,
                              [down_pc.opt()], [down_T.opt()]))
        if debug:
            P.wait("gpsimd", "s_ve", xn_ready)
            for c in range(NDCH):
                P.dma("gpsimd", "krw", dbg["d_xn0"][c * 128:(c + 1) * 128, :],
                      xnT[:, c * TOK:(c + 1) * TOK])

        def emit_ffn_A(first_pass, xn_ready):
            P.psum_fence()
            P.sync_fence()
            mm_of, ve_of, ac_of, gp_of = {}, {}, {}, {}
            for t in range(NFT):
                par = t % 2
                gbuf = AB[:, WG[par]:WG[par] + 4096]
                ubuf = AB[:, WG[2 + par]:WG[2 + par] + 4096]
                if first_pass:
                    P.wait("sync", "s_cc", gu_ready(t))
                if t >= 2:
                    P.wait("sync", "s_pe", mm_of[t - 2])
                gu_Th = gu_TA if t < NFT // 2 else gu_TB
                th = t if t < NFT // 2 else t - NFT // 2
                P.dma("sync", "w%d" % par, gbuf.rearrange("p (c f) -> p c f", c=NDCH),
                      gu_Th[:, th * 128:(th + 1) * 128].rearrange(
                          "(c p) f -> p c f", p=128))
                lu = P.dma("sync", "w%d" % par, ubuf.rearrange("p (c f) -> p c f", c=NDCH),
                           gu_Th[:, HDFF + th * 128:HDFF + (th + 1) * 128].rearrange(
                               "(c p) f -> p c f", p=128))
                psg, psu = PS[par], PS[2 + par]
                P.wait("tensor", "w%d" % par, lu)
                P.wait("tensor", "s_ve", xn_ready)
                if t >= 2:
                    P.wait("tensor", "s_ve", ve_of[t - 2])
                    P.wait("tensor", "s_ac", ac_of[t - 2])
                for c in range(NDCH):
                    lt = gbuf[:, c * 128:(c + 1) * 128]
                    f, l = (c == 0), (c == NDCH - 1)
                    P.op("tensor", lambda e, lt=lt, c=c, psg=psg, f=f, l=l:
                         e.matmul(psg[:, 0:512], lt, xnT[:, c * TOK:c * TOK + 512],
                                  start=f, stop=l))
                    P.op("tensor", lambda e, lt=lt, c=c, psg=psg, f=f, l=l:
                         e.matmul(psg[:, 512:1024], lt,
                                  xnT[:, c * TOK + 512:(c + 1) * TOK], start=f, stop=l))
                for c in range(NDCH):
                    lt = ubuf[:, c * 128:(c + 1) * 128]
                    f, l = (c == 0), (c == NDCH - 1)
                    P.op("tensor", lambda e, lt=lt, c=c, psu=psu, f=f, l=l:
                         e.matmul(psu[:, 0:512], lt, xnT[:, c * TOK:c * TOK + 512],
                                  start=f, stop=l))
                    mm = P.op("tensor", lambda e, lt=lt, c=c, psu=psu, f=f, l=l:
                              e.matmul(psu[:, 512:1024], lt,
                                       xnT[:, c * TOK + 512:(c + 1) * TOK],
                                       start=f, stop=l),
                              inc=("s_pe", 1) if l else None)
                mm_of[t] = mm
                gev = AB[:, GEV[par]:GEV[par] + TOK]
                P.wait("scalar", "s_pe", mm)
                if t >= 2:
                    P.wait("scalar", "s_ve", ve_of[t - 2])
                av = P.op("scalar", lambda e, o=gev, psg=psg:
                          e.activation(o, psg[:, :], AF.Silu), inc=("s_ac", 1))
                ac_of[t] = av
                if t < H1T:
                    hdst = AB[:, t * TOK:(t + 1) * TOK]
                else:
                    hdst = AB[:, HOV[par]:HOV[par] + TOK]
                    if t - 2 >= H1T:
                        P.wait("vector", "hst%d" % par, gp_of[t - 2])
                P.wait("vector", "s_ac", av)
                hv = P.op("vector", lambda e, o=hdst, psu=psu, g=gev:
                          e.tensor_tensor(o, psu[:, :], g, ALU.mult), inc=("s_ve", 1))
                ve_of[t] = hv
                if t >= H1T:
                    P.wait("gpsimd", "s_ve", hv)
                    gp_of[t] = P.dma("gpsimd", "hst%d" % par,
                                     h2_d[(t - H1T) * 128:(t - H1T + 1) * 128, :],
                                     hdst)
                P.drain_bg(2)
            return P.cur("s_ve")

        def emit_ffn_B(xbase_d, xdst_d, final):
            assert down_ag_v, "down AG not emitted yet"
            P.wait("sync", "s_cc", down_ag_v[0])
            P.sync_fence()
            P.psum_fence()
            SB = [H1C, H1C + 3712]
            gpev = {}
            xbv = {}
            for sub, (ca, cb) in enumerate(DSUB):
                nch = cb - ca
                if sub > 0:
                    P.wait("gpsimd", "s_pe", P.cur("s_pe"))
                    P.wait("gpsimd", "s_ve", P.cur("s_ve"))
                    P.wait("gpsimd", "s_ac", P.cur("s_ac"))
                    P.wait_tot("gpsimd", "hst0", "hst1")
                    for i in range(nch):
                        P.dma("gpsimd", "hld", AB[:, i * TOK:(i + 1) * TOK],
                              h2_d[(ca - H1T + i) * 128:(ca - H1T + i + 1) * 128, :])
                    P.wait_tot("tensor", "hld")
                    P.wait_tot("sync", "pst0", "pst1")
                mm_of, ve_of = {}, {}
                for dt in range(NDCH):
                    par = dt % 2
                    sbv = AB[:, SB[par]:SB[par] + nch * 128]
                    if dt >= 2:
                        P.wait("sync", "s_pe", mm_of[dt - 2])
                    lw = P.dma("sync", "w%d" % par, sbv.rearrange("p (c f) -> p c f", c=nch),
                               down_T[ca * 128:cb * 128,
                                      dt * 128:(dt + 1) * 128].rearrange(
                                          "(c p) f -> p c f", p=128))
                    psd = PS[par]
                    P.wait("tensor", "w%d" % par, lw)
                    if dt >= 2:
                        P.wait("tensor", "s_ve", ve_of[dt - 2])
                    for c in range(nch):
                        lt = sbv[:, c * 128:(c + 1) * 128]
                        f, l = (c == 0), (c == nch - 1)
                        P.op("tensor", lambda e, lt=lt, c=c, psd=psd, f=f, l=l:
                             e.matmul(psd[:, 0:512], lt, AB[:, c * TOK:c * TOK + 512],
                                      start=f, stop=l))
                        mm = P.op("tensor", lambda e, lt=lt, c=c, psd=psd, f=f, l=l:
                                  e.matmul(psd[:, 512:1024], lt,
                                           AB[:, c * TOK + 512:(c + 1) * TOK],
                                           start=f, stop=l),
                                  inc=("s_pe", 1) if l else None)
                    mm_of[dt] = mm
                    fb = AFX[:, par * TOK:(par + 1) * TOK]
                    if sub == 0:
                        P.wait("vector", "s_pe", mm)
                        if dt >= 2:
                            P.wait("vector", "pst%d" % par, gpev[dt - 2])
                        vv = P.op("vector", lambda e, o=fb, psd=psd:
                                  e.tensor_copy(o, psd[:, :]), inc=("s_ve", 1))
                        ve_of[dt] = vv
                        P.wait("gpsimd", "s_ve", vv)
                        gpev[dt] = P.dma("gpsimd", "pst%d" % par,
                                         part_a[dt * 128:(dt + 1) * 128, :], fb)
                    elif sub == 1:
                        if dt >= 2:
                            P.wait("sync", "s_ve", ve_of[dt - 2])
                        if dt >= 2:
                            P.wait("sync", "pst%d" % par, gpev[dt - 2])
                        ld = P.dma("sync", "fb%d" % par, fb,
                                   part_a[dt * 128:(dt + 1) * 128, :])
                        P.wait("vector", "s_pe", mm)
                        P.wait("vector", "fb%d" % par, ld)
                        if dt >= 2:
                            P.wait("vector", "pst%d" % par, gpev[dt - 2])
                        vv = P.op("vector", lambda e, o=fb, psd=psd:
                                  e.tensor_add(o, psd[:, :], o), inc=("s_ve", 1))
                        ve_of[dt] = vv
                        P.wait("gpsimd", "s_ve", vv)
                        gpev[dt] = P.dma("gpsimd", "pst%d" % par,
                                         part_b[dt * 128:(dt + 1) * 128, :], fb)
                    else:
                        if dt >= 2:
                            P.wait("sync", "s_ve", ve_of[dt - 2])
                        if dt >= 2:
                            P.wait("sync", "xw%d" % par, gpev[dt - 2])
                        ld1 = P.dma("sync", "fb%d" % par, fb,
                                    part_b[dt * 128:(dt + 1) * 128, :])
                        P.wait("sync", "s_ve", xbv.get(dt - 1, 0))
                        ld2 = P.dma("sync", "xfb", xfb[:, :],
                                    xbase_d[dt * 128:(dt + 1) * 128, :])
                        P.wait("vector", "s_pe", mm)
                        P.wait("vector", "fb%d" % par, ld1)
                        P.wait("vector", "xfb", ld2)
                        if dt >= 2:
                            P.wait("vector", "xw%d" % par, gpev[dt - 2])
                        P.op("vector", lambda e, o=fb, psd=psd:
                             e.tensor_add(o, psd[:, :], o))
                        vv = P.op("vector", lambda e, o=fb:
                                  e.scalar_tensor_tensor(o, o, 0.5, xfb[:, :],
                                                         ALU.mult, ALU.add),
                                  inc=("s_ve", 1))
                        xbv[dt] = vv
                        ve_of[dt] = vv
                        P.wait("gpsimd", "s_ve", vv)
                        dst = out_xT if final else xdst_d
                        gpev[dt] = P.dma("gpsimd", "xw%d" % par,
                                         dst[dt * 128:(dt + 1) * 128, :], fb)
                        if not final:
                            ve_of[dt] = stats_tile(fb, dt)
                    P.drain_bg(2)
            if final:
                return None
            return ln_apply(xdst_d)

        emit_ffn_A(first_pass=True, xn_ready=xn_ready)
        xn_ready = emit_ffn_B(xbase_d=xT_in, xdst_d=xres1, final=False)
        if debug:
            P.wait("gpsimd", "s_ve", xn_ready)
            for c in range(NDCH):
                P.dma("gpsimd", "krw", dbg["d_x1"][c * 128:(c + 1) * 128, :],
                      xres1[c * 128:(c + 1) * 128, :])

        P.drain_all_bg()
        assert qa_ag_v and qb_ag_v and kva_ag_v and kvbN_ag_v and kvbV_ag_v \
            and wo_ag_v

        P.psum_fence()
        P.sync_fence()
        CK = AT_VB[0]
        nrm = bc0[0:1, :]
        km_ve = {}

        P.wait("sync", "s_cc", kva_ag_v[0])
        P.wait("sync", "s_pe", P.cur("s_pe"))
        P.psum_fence()
        P.wait("tensor", "s_ve", xn_ready)
        mm_of, kv_cv = {}, {}
        for kt in range(5):
            ncols = 128 if kt < 4 else 64
            par = kt % 2
            wb = AB[:, AT_KB[0] + par * 4096: AT_KB[0] + par * 4096 + NDCH * ncols]
            if kt >= 2:
                P.wait("sync", "s_pe", mm_of[kt - 2])
            lw = P.dma("sync", "w%d" % par, wb.rearrange("p (c f) -> p c f", c=NDCH),
                       kva_T[:, kt * 128:kt * 128 + ncols].rearrange(
                           "(c p) f -> p c f", p=128))
            psq = PS[par]
            P.wait("tensor", "w%d" % par, lw)
            if kt >= 2:
                P.wait("tensor", "s_ve", kv_cv[kt - 2])
            for c in range(NDCH):
                lt = wb[:, c * ncols:(c + 1) * ncols]
                f, l = (c == 0), (c == NDCH - 1)
                P.op("tensor", lambda e, lt=lt, c=c, psq=psq, nr=ncols, f=f, l=l:
                     e.matmul(psq[0:nr, 0:512], lt, xnT[:, c * TOK:c * TOK + 512],
                              start=f, stop=l))
                mm = P.op("tensor", lambda e, lt=lt, c=c, psq=psq, nr=ncols, f=f, l=l:
                          e.matmul(psq[0:nr, 512:1024], lt,
                                   xnT[:, c * TOK + 512:(c + 1) * TOK],
                                   start=f, stop=l), inc=("s_pe", 1) if l else None)
            mm_of[kt] = mm
            P.wait("vector", "s_pe", mm)
            if kt < 4:
                kdst = AB[:, CK + kt * TOK: CK + (kt + 1) * TOK]
                cv = P.op("vector", lambda e, o=kdst, psq=psq:
                          e.tensor_copy(o, psq[:, :]), inc=("s_ve", 1))
                kv_cv[kt] = cv
                sq = AB[:, AT_R1:AT_R1 + TOK]
                P.wait("scalar", "s_ve", cv)
                svv = P.op("scalar", lambda e, o=sq, i=kdst:
                           e.activation(o, i, AF.Square), inc=("s_ac", 1))
                P.wait("tensor", "s_ac", svv)
                f, l = (kt == 0), (kt == 3)
                P.op("tensor", lambda e, i=sq, f=f, l=l:
                     e.matmul(ps3[0:1, 0:512], ones_b[:, :], i[:, 0:512],
                              start=f, stop=l))
                P.op("tensor", lambda e, i=sq, f=f, l=l:
                     e.matmul(ps3[0:1, 512:1024], ones_b[:, :], i[:, 512:1024],
                              start=f, stop=l), inc=("s_pe", 1))
            else:
                krb = AB[:, AT_R2:AT_R2 + TOK]
                cv = P.op("vector", lambda e, o=krb, psq=psq:
                          e.tensor_copy(o[0:64, :], psq[0:64, :]), inc=("s_ve", 1))
                kv_cv[kt] = cv
                P.wait("gpsimd", "s_ve", cv)
                krst = P.dma("gpsimd", "krw", kr_in, krb[0:64, :])
        P.wait("gpsimd", "krw", krst)
        ag_r = P.cc("AllGather", groups4, [kr_in.opt()], [kr_T.opt()])
        P.wait("scalar", "s_pe", P.cur("s_pe"))
        a1 = P.op("scalar", lambda e: e.activation(nrm, ps3[0:1, :], AF.Sqrt,
                                                   scale=1.0 / KVR), inc=("s_ac", 1))
        P.wait("vector", "s_ac", a1)
        P.op("vector", lambda e: e.tensor_scalar_add(nrm, nrm, EPS_RMS))
        v1 = P.op("vector", lambda e: e.reciprocal(nrm, nrm), inc=("s_ve", 1))
        P.wait("tensor", "s_ve", v1)
        P.psum_fence()
        bv = bcast_row(nrm, ps0, bc0)
        for kt in range(4):
            kd = AB[:, CK + kt * TOK: CK + (kt + 1) * TOK]
            P.wait("vector", "s_ve", bv)
            P.op("vector", lambda e, k=kd: e.tensor_mul(k, k, bc0[:, :]),
                 inc=("s_ve", 1) if kt == 3 else None)
        ck_done = P.cur("s_ve")
        if debug:
            P.wait("gpsimd", "s_ve", ck_done)
            for kt in range(4):
                P.dma("gpsimd", "krw", dbg["d_ckc"][kt * 128:(kt + 1) * 128, :],
                      AB[:, CK + kt * TOK: CK + (kt + 1) * TOK])

        krb = AB[:, AT_R2:AT_R2 + TOK]
        sqk_own = AB[:, AT_R1:AT_R1 + TOK]
        P.wait("scalar", "s_ve", kv_cv[4])
        kra_own = P.op("scalar", lambda e: e.activation(sqk_own[0:64, :], krb[0:64, :],
                                                        AF.Square), inc=("s_ac", 1))
        P.wait("tensor", "s_ac", kra_own)
        P.psum_fence()
        P.op("tensor", lambda e: e.matmul(ps2[0:1, 0:512], ones_b[0:64, :],
                                          sqk_own[0:64, 0:512], start=True, stop=True))
        krpe = P.op("tensor", lambda e:
                    e.matmul(ps2[0:1, 512:1024], ones_b[0:64, :],
                             sqk_own[0:64, 512:1024], start=True, stop=True),
                    inc=("s_pe", 1))
        P.wait("vector", "s_pe", krpe)
        kr2v = P.op("vector", lambda e: e.tensor_copy(bc1[0:1, :], ps2[0:1, :]),
                    inc=("s_ve", 1))

        P.wait("sync", "s_cc", kvbN_ag_v[0])
        P.wait("sync", "s_pe", P.cur("s_pe"))
        P.psum_fence()
        kwb = AB[:, AT_KB[0]:AT_KB[0] + 512]
        knv, kng, kn_mm = {}, {}, {}
        for h in range(NH):
            if h >= 1:
                P.wait("sync", "s_pe", kn_mm[h - 1])
            lw = P.dma("sync", "w0", kwb.rearrange("p (c f) -> p c f", c=4),
                       kvbN_T[:, h * ND:(h + 1) * ND].rearrange(
                           "(c p) f -> p c f", p=128))
            psk = PS[h % 2]
            P.wait("tensor", "w0", lw)
            P.wait("tensor", "s_ve", ck_done)
            if h >= 2:
                P.wait("tensor", "s_ve", knv[h - 2])
            for c in range(4):
                lt = kwb[:, c * 128:(c + 1) * 128]
                f, l = (c == 0), (c == 3)
                P.op("tensor", lambda e, lt=lt, c=c, psk=psk, f=f, l=l:
                     e.matmul(psk[:, 0:512], lt,
                              AB[:, CK + c * TOK: CK + c * TOK + 512], start=f, stop=l))
                mm = P.op("tensor", lambda e, lt=lt, c=c, psk=psk, f=f, l=l:
                          e.matmul(psk[:, 512:1024], lt,
                                   AB[:, CK + c * TOK + 512: CK + (c + 1) * TOK],
                                   start=f, stop=l), inc=("s_pe", 1) if l else None)
            kn_mm[h] = mm
            kout = AB[:, AT_ATB[h % 2]:AT_ATB[h % 2] + TOK]
            P.wait("vector", "s_pe", mm)
            if h >= 2:
                P.wait("vector", "kvw%d" % (h % 2), kng[h - 2])
            cv = P.op("vector", lambda e, o=kout, psk=psk:
                      e.tensor_copy(o, psk[:, :]), inc=("s_ve", 1))
            knv[h] = cv
            P.wait("gpsimd", "s_ve", cv)
            k_inh = k_in0 if h < 16 else k_in1
            kng[h] = P.dma("gpsimd", "kvw%d" % (h % 2),
                           k_inh[(h % 16) * ND:(h % 16 + 1) * ND, :], kout)
            if h == 15:
                P.wait("gpsimd", "kvw0", kng[14])
                P.wait("gpsimd", "kvw1", kng[15])
                ag_k0 = P.cc("AllGather", groups4, [k_in0.opt()], [k_T0.opt()])
            sqh = AB[:, 30976:30976 + TOK]
            P.wait("scalar", "s_ve", cv)
            if h >= 1:
                P.wait("scalar", "s_ve", km_ve[h - 1])
            sks = P.op("scalar", lambda e, o=sqh, i=kout:
                       e.activation(o, i, AF.Square), inc=("s_ac", 1))
            P.wait("tensor", "s_ac", sks)
            if h >= 1:
                P.wait("tensor", "s_ve", km_ve[h - 1])
            P.op("tensor", lambda e: e.matmul(ps2[0:1, 0:512], ones_b[:, :],
                                              sqh[:, 0:512], start=True, stop=True))
            kpe = P.op("tensor", lambda e:
                       e.matmul(ps2[0:1, 512:1024], ones_b[:, :],
                                sqh[:, 512:1024], start=True, stop=True),
                       inc=("s_pe", 1))
            P.wait("vector", "s_pe", kpe)
            P.op("vector", lambda e: e.tensor_add(vrow1[0:1, :], ps2[0:1, :],
                                                  bc1[0:1, :]))
            km_ve[h] = P.op("vector", lambda e, h=h:
                            e.reduce_max(vrow2[0:1, h:h + 1], vrow1[0:1, :],
                                         axis=AX), inc=("s_ve", 1))
            P.drain_bg(1)
        P.wait("gpsimd", "s_ve", km_ve[NH - 1])
        kmst = P.dma("gpsimd", "krw", km_in, vrow2[0:1, 0:NH])
        P.wait("gpsimd", "krw", kmst)
        ag_m = P.cc("AllGather", groups4, [km_in.opt()], [km_T.opt()])
        P.wait("gpsimd", "kvw0", kng[30])
        P.wait("gpsimd", "kvw1", kng[31])
        ag_k1 = P.cc("AllGather", groups4, [k_in1.opt()], [k_T1.opt()])

        P.wait("sync", "s_cc", kvbV_ag_v[0])
        P.wait("sync", "s_pe", P.cur("s_pe"))
        P.psum_fence()
        spots = [AT_KB[0], AT_KB[0] + 4096, AT_SQK, AT_KRA]
        vld = []
        for c in range(4):
            vld.append(P.dma("sync", "tab", AB[:, spots[c]:spots[c] + 4096],
                             kvbV_T[c * 128:(c + 1) * 128, :]))
        vev, vtg = {}, {}
        for tt in range(8):
            P.wait("tensor", "tab", vld[3])
            if tt >= 1:
                P.wait("tensor", "s_ve", vev[tt - 1])
            for c in range(4):
                lt = AB[:, CK + c * TOK + tt * 128: CK + c * TOK + (tt + 1) * 128]
                mv_ = AB[:, spots[c]:spots[c] + 4096]
                for vb in range(4):
                    f, l = (c == 0), (c == 3)
                    P.op("tensor", lambda e, lt=lt, mv=mv_, vb=vb, psx=PS[vb], f=f, l=l:
                         e.matmul(psx[:, 0:512], lt,
                                  mv[:, vb * 1024:vb * 1024 + 512], start=f, stop=l))
                    mm = P.op("tensor", lambda e, lt=lt, mv=mv_, vb=vb, psx=PS[vb],
                              f=f, l=l:
                              e.matmul(psx[:, 512:1024], lt,
                                       mv[:, vb * 1024 + 512:(vb + 1) * 1024],
                                       start=f, stop=l),
                              inc=("s_pe", 1) if (l and vb == 3) else None)
            vout = AB[:, AT_VB[1]:AT_VB[1] + 4096]
            P.wait("vector", "s_pe", mm)
            if tt >= 1:
                P.wait("vector", "vst", vtg[tt - 1])
            cvl = None
            for vb in range(4):
                cvl = P.op("vector", lambda e, o=vout[:, vb * 1024:(vb + 1) * 1024],
                           psx=PS[vb]: e.tensor_copy(o, psx[:, :]), inc=("s_ve", 1))
            vev[tt] = cvl
            P.wait("gpsimd", "s_ve", cvl)
            P.dma("gpsimd", "vst", v_in0[tt * 128:(tt + 1) * 128, :],
                  vout[:, 0:NH * VD // 2])
            vtg[tt] = P.dma("gpsimd", "vst", v_in1[tt * 128:(tt + 1) * 128, :],
                            vout[:, NH * VD // 2:])
            P.wait("tensor", "vst", vtg[tt])
            P.drain_bg(1)
        P.wait("gpsimd", "vst", vtg[7])
        ag_v0 = P.cc("AllGather", groups4, [v_in0.opt()], [v_T0.opt()])
        ag_v1 = P.cc("AllGather", groups4, [v_in1.opt()], [v_T1.opt()])
        if debug:
            P.wait("gpsimd", "s_cc", ag_v1)
            P.dma("gpsimd", "krw", dbg["d_key"][0:4 * NH * ND // 2, :], k_T0)
            P.dma("gpsimd", "krw", dbg["d_key"][4 * NH * ND // 2:, :], k_T1)
            P.dma("gpsimd", "krw", dbg["d_vv"][:, 0:NH * VD // 2], v_T0)
            P.dma("gpsimd", "krw", dbg["d_vv"][:, NH * VD // 2:], v_T1)

        P.wait("sync", "s_cc", qa_ag_v[0])
        P.wait("sync", "s_pe", P.cur("s_pe"))
        P.psum_fence()
        mm_of, cv_of = {}, {}
        for qt in range(NQRCH):
            par = qt % 2
            wb = AB[:, AT_KB[0] + par * 4096: AT_KB[0] + par * 4096 + 4096]
            if qt >= 2:
                P.wait("sync", "s_pe", mm_of[qt - 2])
            lw = P.dma("sync", "w%d" % par, wb.rearrange("p (c f) -> p c f", c=NDCH),
                       qa_T[:, qt * 128:(qt + 1) * 128].rearrange(
                           "(c p) f -> p c f", p=128))
            psq = PS[par]
            P.wait("tensor", "w%d" % par, lw)
            P.wait("tensor", "s_ve", xn_ready)
            if qt >= 2:
                P.wait("tensor", "s_ve", cv_of[qt - 2])
            for c in range(NDCH):
                lt = wb[:, c * 128:(c + 1) * 128]
                f, l = (c == 0), (c == NDCH - 1)
                P.op("tensor", lambda e, lt=lt, c=c, psq=psq, f=f, l=l:
                     e.matmul(psq[:, 0:512], lt, xnT[:, c * TOK:c * TOK + 512],
                              start=f, stop=l))
                mm = P.op("tensor", lambda e, lt=lt, c=c, psq=psq, f=f, l=l:
                          e.matmul(psq[:, 512:1024], lt,
                                   xnT[:, c * TOK + 512:(c + 1) * TOK],
                                   start=f, stop=l), inc=("s_pe", 1) if l else None)
            mm_of[qt] = mm
            qdst = AB[:, AT_QC + qt * TOK: AT_QC + (qt + 1) * TOK]
            P.wait("vector", "s_pe", mm)
            cv = P.op("vector", lambda e, o=qdst, psq=psq:
                      e.tensor_copy(o, psq[:, :]), inc=("s_ve", 1))
            cv_of[qt] = cv
            sq = AB[:, AT_R1:AT_R1 + TOK]
            P.wait("scalar", "s_ve", cv)
            svv = P.op("scalar", lambda e, o=sq, i=qdst:
                       e.activation(o, i, AF.Square), inc=("s_ac", 1))
            P.wait("tensor", "s_ac", svv)
            f, l = (qt == 0), (qt == NQRCH - 1)
            P.op("tensor", lambda e, i=sq, f=f, l=l:
                 e.matmul(ps2[0:1, 0:512], ones_b[:, :], i[:, 0:512], start=f, stop=l))
            P.op("tensor", lambda e, i=sq, f=f, l=l:
                 e.matmul(ps2[0:1, 512:1024], ones_b[:, :], i[:, 512:1024],
                          start=f, stop=l), inc=("s_pe", 1))
        P.wait("scalar", "s_pe", P.cur("s_pe"))
        nrm = bc0[0:1, :]
        a1 = P.op("scalar", lambda e: e.activation(nrm, ps2[0:1, :], AF.Sqrt,
                                                   scale=1.0 / QR), inc=("s_ac", 1))
        P.wait("vector", "s_ac", a1)
        P.op("vector", lambda e: e.tensor_scalar_add(nrm, nrm, EPS_RMS))
        v1 = P.op("vector", lambda e: e.reciprocal(nrm, nrm), inc=("s_ve", 1))
        P.wait("tensor", "s_ve", v1)
        P.psum_fence()
        bv = bcast_row(nrm, ps0, bc0)
        for qt in range(NQRCH):
            qd = AB[:, AT_QC + qt * TOK: AT_QC + (qt + 1) * TOK]
            P.wait("vector", "s_ve", bv)
            P.op("vector", lambda e, q=qd: e.tensor_mul(q, q, bc0[:, :]),
                 inc=("s_ve", 1) if qt == NQRCH - 1 else None)
        qc_done = P.cur("s_ve")

        if debug:
            P.wait("gpsimd", "s_ve", qc_done)
            for qt in range(NQRCH):
                P.dma("gpsimd", "krw", dbg["d_qc"][qt * 128:(qt + 1) * 128, :],
                      AB[:, AT_QC + qt * TOK: AT_QC + (qt + 1) * TOK])

        P.sync_fence()
        P.dma("sync", "tab", AB[0:64, AT_KB[0]:AT_KB[0] + S], cosk)
        csk_ld = P.dma("sync", "tab", AB[0:64, AT_KB[1]:AT_KB[1] + S], sink)
        P.dma("sync", "tab", AB[0:64, AT_CSQ:AT_CSQ + TOK], cosq)
        csq_ld = P.dma("sync", "tab", AB[0:64, AT_SSQ:AT_SSQ + TOK], sinq)
        P.wait("sync", "s_cc", ag_r)
        for r in range(4):
            P.dma("sync", "tab", AB[0:64, AT_KRA + r * TOK: AT_KRA + (r + 1) * TOK],
                  kr_T[r * RD:(r + 1) * RD, :])
        kr_raw_done = P.cur("tab")
        P.wait("sync", "tab", kr_raw_done)
        P.dma("sync", "qsw", AB[0:32, AT_SQK:AT_SQK + S], AB[32:64, AT_KRA:AT_KRA + S])
        krsw_ld = P.dma("sync", "qsw", AB[32:64, AT_SQK:AT_SQK + S],
                        AB[0:32, AT_KRA:AT_KRA + S])
        P.wait("vector", "qsw", krsw_ld)
        P.wait("vector", "tab", kr_raw_done)
        P.wait("vector", "s_pe", P.cur("s_pe"))
        KRG = AB[0:64, AT_KRA:AT_KRA + S]
        KRSW = AB[0:64, AT_SQK:AT_SQK + S]
        v1 = P.op("vector", lambda e: e.tensor_mul(KRG, KRG,
                                                   AB[0:64, AT_KB[0]:AT_KB[0] + S]),
                  inc=("s_ve", 1))
        v2 = P.op("vector", lambda e: e.tensor_mul(KRSW, KRSW,
                                                   AB[0:64, AT_KB[1]:AT_KB[1] + S]),
                  inc=("s_ve", 1))
        P.wait("vector", "s_ve", v2)
        v3 = P.op("vector", lambda e: e.tensor_add(KRG, KRG, KRSW), inc=("s_ve", 1))
        v4 = P.op("vector", lambda e: e.memset(AB[64:65, AT_KRA:AT_KRA + S], 1.0),
                  inc=("s_ve", 1))
        kr_ready = v4
        kr_ready = v4
        P.wait("sync", "s_cc", ag_m)
        P.dma("sync", "tab", vrow2[0:1, 0:NH], km_T[0:1, :])
        P.dma("sync", "tab", vrow1[0:1, 0:NH], km_T[1:2, :])
        P.dma("sync", "tab", vrow1[0:1, NH:2 * NH], km_T[2:3, :])
        km_rows = P.dma("sync", "tab", vrow1[0:1, 2 * NH:3 * NH], km_T[3:4, :])
        P.wait("vector", "tab", km_rows)
        P.op("vector", lambda e: e.tensor_max(vrow2[0:1, 0:NH], vrow2[0:1, 0:NH],
                                              vrow1[0:1, 0:NH]))
        P.op("vector", lambda e: e.tensor_max(vrow2[0:1, 0:NH], vrow2[0:1, 0:NH],
                                              vrow1[0:1, NH:2 * NH]))
        kmx = P.op("vector", lambda e: e.tensor_max(vrow2[0:1, 0:NH], vrow2[0:1, 0:NH],
                                                    vrow1[0:1, 2 * NH:3 * NH]),
                   inc=("s_ve", 1))

        out_allT = xnT
        head_entry_pe = P.cur("s_pe")
        AT_QN2 = 47360
        AT_QA2 = 48384
        QNs = [AT_QN, AT_QN2]
        QAs = [AT_QA, AT_QA2]
        HB = NH * ND // 2
        st = {}

        def keyb_of(h):
            return AB[:, AT_KB[h % 2]:AT_KB[h % 2] + 4096]

        def vbuf_of(h):
            return AB[:, AT_VB[h % 2]:AT_VB[h % 2] + 4096]

        def qwb_of(h):
            base = AT_QW if h % 2 == 0 else AT_SQK
            return AB[:, base:base + NQRCH * QK]

        def emit_loads(h):
            parb = h % 2
            k_Th, v_Th = (k_T0, v_T0) if h < 16 else (k_T1, v_T1)
            hh = h % 16
            st.setdefault(h, {})
            P.wait("sync", "s_cc", max(ag_k0, ag_v0) if h < 16
                   else max(ag_k1, ag_v1))
            P.wait("sync", "s_pe",
                   st[h - 2]["av_last"] if h >= 2 else head_entry_pe)
            if h < 2:
                P.wait("sync", "s_ve", kr_ready)
            for r in range(4):
                P.dma("sync", "kv%d" % parb, keyb_of(h)[:, r * TOK:(r + 1) * TOK],
                      k_Th[r * HB + hh * ND: r * HB + (hh + 1) * ND, :])
            st[h]["kv_ld"] = P.dma(
                "sync", "kv%d" % parb,
                vbuf_of(h).rearrange("p (c f) -> p c f", c=32),
                v_Th[:, hh * VD:(hh + 1) * VD].rearrange("(c p) f -> p c f", p=128))
            P.wait("sync", "s_cc", qb_ag_v[0])
            st[h]["lqw"] = P.dma(
                "sync", "qw", qwb_of(h).rearrange("p (c f) -> p c f", c=NQRCH),
                qb_T[:, h * QK:(h + 1) * QK].rearrange("(c p) f -> p c f", p=128))

        def emit_qbn(h):
            qwb = qwb_of(h)
            P.wait("tensor", "qw", st[h]["lqw"])
            P.wait("tensor", "s_ve", qc_done)
            if h >= 1:
                P.wait("tensor", "s_ac", st[h - 1]["exp"][30])
            for c in range(NQRCH):
                lt = qwb[:, c * QK:c * QK + 128]
                f, l = (c == 0), (c == NQRCH - 1)
                P.op("tensor", lambda e, lt=lt, c=c, f=f, l=l:
                     e.matmul(ps0[:, 0:512], lt,
                              AB[:, AT_QC + c * TOK:AT_QC + c * TOK + 512],
                              start=f, stop=l))
                mm = P.op("tensor", lambda e, lt=lt, c=c, f=f, l=l:
                          e.matmul(ps0[:, 512:1024], lt,
                                   AB[:, AT_QC + c * TOK + 512:AT_QC + (c + 1) * TOK],
                                   start=f, stop=l), inc=("s_pe", 1) if l else None)
            st[h]["qbn_pe"] = mm

        def emit_qbr(h):
            qwb = qwb_of(h)
            if h >= 1:
                P.wait("tensor", "s_ac", st[h - 1]["exp"][31])
            for c in range(NQRCH):
                lt = qwb[:, c * QK + 128:(c + 1) * QK]
                f, l = (c == 0), (c == NQRCH - 1)
                P.op("tensor", lambda e, lt=lt, c=c, f=f, l=l:
                     e.matmul(ps1[0:64, 0:512], lt,
                              AB[:, AT_QC + c * TOK:AT_QC + c * TOK + 512],
                              start=f, stop=l))
                mm = P.op("tensor", lambda e, lt=lt, c=c, f=f, l=l:
                          e.matmul(ps1[0:64, 512:1024], lt,
                                   AB[:, AT_QC + c * TOK + 512:AT_QC + (c + 1) * TOK],
                                   start=f, stop=l), inc=("s_pe", 1) if l else None)
            st[h]["qbr_pe"] = mm

        def emit_preA3(h):
            qn_sb = AB[:, QNs[h % 2]:QNs[h % 2] + TOK]
            QRA = AB[0:64, QAs[h % 2]:QAs[h % 2] + TOK]
            QRR = AB[0:64, AT_R1:AT_R1 + TOK]
            QSW = AB[0:64, AT_R2:AT_R2 + TOK]
            P.wait("vector", "s_pe", st[h]["qbn_pe"])
            if h >= 2:
                P.wait("vector", "s_pe", st[h - 2]["sc_last"])
            qnv = P.op("vector", lambda e, o=qn_sb: e.tensor_copy(o, ps0[:, :]),
                       inc=("s_ve", 1))
            st[h]["qnv"] = qnv
            P.wait("vector", "s_pe", st[h]["qbr_pe"])
            if h >= 1:
                P.wait("vector", "s_pe", st[h - 1]["qsq_pe"])
            P.wait("vector", "tab", csq_ld)
            rv1 = P.op("vector", lambda e: e.tensor_copy(QRR, ps1[0:64, :]),
                       inc=("s_ve", 1))
            st[h]["rv1"] = rv1
            P.wait("sync", "s_ve", rv1)
            if h >= 1:
                P.wait("sync", "qsw", st[h - 1]["swap"])
            P.dma("sync", "qsw", AB[0:32, AT_R2:AT_R2 + TOK],
                  AB[32:64, AT_R1:AT_R1 + TOK])
            d2 = P.dma("sync", "qsw", AB[32:64, AT_R2:AT_R2 + TOK],
                       AB[0:32, AT_R1:AT_R1 + TOK])
            st[h]["swap"] = d2
            P.wait("vector", "qsw", d2)
            P.op("vector", lambda e: e.tensor_mul(QSW, QSW,
                                                  AB[0:64, AT_SSQ:AT_SSQ + TOK]),
                 inc=("s_ve", 1))
            rv3 = P.op("vector", lambda e: e.tensor_mul(QRR, QRR,
                                                        AB[0:64, AT_CSQ:AT_CSQ + TOK]),
                       inc=("s_ve", 1))
            P.wait("vector", "s_ve", rv3)
            rv4 = P.op("vector", lambda e: e.tensor_add(QRA, QRR, QSW),
                       inc=("s_ve", 1))
            sqq = AB[:, AT_R1:AT_R1 + TOK]
            sqr = AB[0:64, AT_R2:AT_R2 + TOK]
            P.wait("scalar", "s_ve", rv4)
            P.op("scalar", lambda e, o=sqq, i=qn_sb: e.activation(o, i, AF.Square),
                 inc=("s_ac", 1))
            st[h]["sq_ac"] = P.op("scalar", lambda e, o=sqr, i=QRA:
                                  e.activation(o, i, AF.Square), inc=("s_ac", 1))

        def emit_partB(h):
            sqq = AB[:, AT_R1:AT_R1 + TOK]
            sqr = AB[0:64, AT_R2:AT_R2 + TOK]
            P.wait("tensor", "s_ac", st[h]["sq_ac"])
            P.wait("tensor", "s_ve", st[h]["qnv"])
            P.op("tensor", lambda e: e.matmul(ps0[0:1, 0:512], ones_b[:, :],
                                              sqq[:, 0:512], start=True, stop=False))
            P.op("tensor", lambda e: e.matmul(ps0[0:1, 512:1024], ones_b[:, :],
                                              sqq[:, 512:1024], start=True, stop=False))
            P.op("tensor", lambda e: e.matmul(ps0[0:1, 0:512], ones_b[0:64, :],
                                              sqr[:, 0:512], start=False, stop=True))
            qsq_pe = P.op("tensor", lambda e:
                          e.matmul(ps0[0:1, 512:1024], ones_b[0:64, :],
                                   sqr[:, 512:1024], start=False, stop=True),
                          inc=("s_pe", 1))
            st[h]["qsq_pe"] = qsq_pe
            P.wait("scalar", "s_pe", qsq_pe)
            P.wait("scalar", "s_ve", kmx)
            if h >= 1:
                P.wait("scalar", "aug", st[h - 1]["augd"])
            qn_a = P.op("scalar", lambda e, h=h:
                        e.activation(vrow1[0:1, :], ps0[0:1, :], AF.Sqrt,
                                     scale=vrow2[0:1, h:h + 1]),
                        inc=("s_ac", 1))
            st[h]["qn_a"] = qn_a
            P.wait("vector", "s_ac", qn_a)
            augv = P.op("vector", lambda e:
                        e.tensor_scalar(vrow1[0:1, :], vrow1[0:1, :],
                                        -1.0, OFFS, ALU.mult, ALU.add),
                        inc=("s_ve", 1))
            P.wait("gpsimd", "s_ve", augv)
            if h >= 2:
                P.wait("gpsimd", "s_pe", st[h - 2]["sc_last"])
            st[h]["augd"] = P.dma("gpsimd", "aug",
                                  AB[64:65, QAs[h % 2]:QAs[h % 2] + TOK],
                                  vrow1[0:1, :])

        def emit_pipe(h):
            parb = h % 2
            keyb = keyb_of(h)
            vbuf = vbuf_of(h)
            qn_sb = AB[:, QNs[h % 2]:QNs[h % 2] + TOK]
            qac = QAs[h % 2]
            kt_pe, exp_of, av_pe = {}, {}, {}
            st[h]["exp"] = exp_of

            def emit_score(kt):
                pss = PS[kt % 2]
                lkn = keyb[:, kt * 128:(kt + 1) * 128]
                lkr = AB[0:65, AT_KRA + kt * 128: AT_KRA + (kt + 1) * 128]
                if kt == 0:
                    P.wait("tensor", "aug", st[h]["augd"])
                    P.wait("tensor", "s_ve", kr_ready)
                    P.wait("tensor", "kv%d" % parb, st[h]["kv_ld"])
                    P.wait("tensor", "s_ac", st[h]["qn_a"])
                if kt == 1 and h >= 1:
                    P.wait("tensor", "s_ve", st[h - 1]["bc_cv"])
                if kt >= 2:
                    P.wait("tensor", "s_ac", exp_of[kt - 2])
                P.op("tensor", lambda e, lkn=lkn, pss=pss:
                     e.matmul(pss[:, 0:512], lkn, qn_sb[:, 0:512],
                              start=True, stop=False))
                P.op("tensor", lambda e, lkn=lkn, pss=pss:
                     e.matmul(pss[:, 512:1024], lkn, qn_sb[:, 512:1024],
                              start=True, stop=False))
                P.op("tensor", lambda e, lkr=lkr, pss=pss:
                     e.matmul(pss[:, 0:512], lkr, AB[0:65, qac:qac + 512],
                              start=False, stop=True))
                return P.op("tensor", lambda e, lkr=lkr, pss=pss:
                            e.matmul(pss[:, 512:1024], lkr,
                                     AB[0:65, qac + 512:qac + TOK],
                                     start=False, stop=True), inc=("s_pe", 1))

            def emit_exp(kt):
                pss = PS[kt % 2]
                atb = AB[:, AT_ATB[kt % 2]:AT_ATB[kt % 2] + TOK]
                P.wait("scalar", "s_pe", kt_pe[kt])
                if kt >= 2:
                    P.wait("scalar", "s_pe", av_pe[kt - 2])
                return P.op("scalar", lambda e, o=atb, pss=pss:
                            e.activation(o, pss[:, :], AF.Exp), inc=("s_ac", 1))

            def emit_av(kt):
                atb = AB[:, AT_ATB[kt % 2]:AT_ATB[kt % 2] + TOK]
                lv = vbuf[:, kt * 128:(kt + 1) * 128]
                f, l = (kt == 0), (kt == 31)
                P.wait("tensor", "s_ac", exp_of[kt])
                if kt == 0 and h >= 1:
                    P.wait("tensor", "s_ve", st[h - 1]["hv"])
                    P.wait("tensor", "s_ve", st[h - 1]["recip"])
                P.op("tensor", lambda e, lv=lv, atb=atb, f=f, l=l:
                     e.matmul(ps2[:, 0:512], lv, atb[:, 0:512], start=f, stop=l))
                P.op("tensor", lambda e, lv=lv, atb=atb, f=f, l=l:
                     e.matmul(ps2[:, 512:1024], lv, atb[:, 512:1024], start=f, stop=l))
                P.op("tensor", lambda e, atb=atb, f=f, l=l:
                     e.matmul(ps3[0:1, 0:512], ones_b[:, :], atb[:, 0:512],
                              start=f, stop=l))
                return P.op("tensor", lambda e, atb=atb, f=f, l=l:
                            e.matmul(ps3[0:1, 512:1024], ones_b[:, :],
                                     atb[:, 512:1024], start=f, stop=l),
                            inc=("s_pe", 1))

            if h + 1 < NH:
                emit_loads(h + 1)
            kt_pe[0] = emit_score(0)
            kt_pe[1] = emit_score(1)
            exp_of[0] = emit_exp(0)
            for kt in range(32):
                if kt + 2 < 32:
                    kt_pe[kt + 2] = emit_score(kt + 2)
                if kt + 1 < 32:
                    exp_of[kt + 1] = emit_exp(kt + 1)
                if kt == 29 and h + 1 < NH:
                    emit_qbn(h + 1)
                if kt == 30 and h + 1 < NH:
                    emit_qbr(h + 1)
                av_pe[kt] = emit_av(kt)
            st[h]["sc_last"] = kt_pe[31]
            st[h]["av_last"] = av_pe[31]
            if h + 1 < NH:
                emit_preA3(h + 1)
            P.wait("vector", "s_pe", av_pe[31])
            rv = P.op("vector", lambda e: e.reciprocal(bc1[0:1, :], ps3[0:1, :]),
                      inc=("s_ve", 1))
            st[h]["recip"] = rv
            P.wait("tensor", "s_ve", rv)
            if h + 1 < NH:
                P.wait("tensor", "s_ve", st[h + 1]["rv1"])
            else:
                P.wait("tensor", "s_ac", exp_of[31])
            bvh = bcast_row(bc1[0:1, :], ps1, bc1)
            st[h]["bc_cv"] = bvh
            P.wait("vector", "s_ve", bvh)
            st[h]["hv"] = P.op("vector", lambda e, h=h:
                               e.tensor_tensor(out_allT[:, h * TOK:(h + 1) * TOK],
                                               ps2[:, :], bc1[:, :], ALU.mult),
                               inc=("s_ve", 1))

        emit_loads(0)
        P.psum_fence()
        emit_qbn(0)
        emit_qbr(0)
        emit_preA3(0)
        for h in range(NH):
            emit_partB(h)
            emit_pipe(h)

        if debug:
            P.wait("gpsimd", "s_ve", st[NH - 1]["hv"])
            for c in range(NDCH):
                P.dma("gpsimd", "krw", dbg["d_oT"][c * 128:(c + 1) * 128, :],
                      out_allT[:, c * TOK:(c + 1) * TOK])

        P.wait("sync", "s_cc", wo_ag_v[0])
        P.wait("sync", "s_pe", P.cur("s_pe"))
        P.wait("sync", "s_ve", P.cur("s_ve"))
        P.wait_tot("sync", "xw0", "xw1")
        P.psum_fence()
        oall_done = st[NH - 1]["hv"]
        wo_mm, wo_ve, gpev2, xbv2 = {}, {}, {}, {}
        for dt in range(NDCH):
            par = dt % 2
            swb = AB[:, AT_VB[par]:AT_VB[par] + 4096]
            if dt >= 2:
                P.wait("sync", "s_pe", wo_mm[dt - 2])
            lw = P.dma("sync", "w%d" % par, swb.rearrange("p (c f) -> p c f", c=NDCH),
                       wo_T[:, dt * 128:(dt + 1) * 128].rearrange(
                           "(c p) f -> p c f", p=128))
            psd = PS[par]
            P.wait("tensor", "w%d" % par, lw)
            P.wait("tensor", "s_ve", oall_done)
            if dt >= 2:
                P.wait("tensor", "s_ve", wo_ve[dt - 2])
            for c in range(NDCH):
                lt = swb[:, c * 128:(c + 1) * 128]
                f, l = (c == 0), (c == NDCH - 1)
                P.op("tensor", lambda e, lt=lt, c=c, psd=psd, f=f, l=l:
                     e.matmul(psd[:, 0:512], lt,
                              out_allT[:, c * TOK:c * TOK + 512], start=f, stop=l))
                mm = P.op("tensor", lambda e, lt=lt, c=c, psd=psd, f=f, l=l:
                          e.matmul(psd[:, 512:1024], lt,
                                   out_allT[:, c * TOK + 512:(c + 1) * TOK],
                                   start=f, stop=l), inc=("s_pe", 1) if l else None)
            wo_mm[dt] = mm
            fb = AFX[:, par * TOK:(par + 1) * TOK]
            if dt >= 2:
                P.wait("sync", "xw%d" % par, gpev2[dt - 2])
            P.wait("sync", "s_ve", xbv2.get(dt - 1, 0))
            ld2 = P.dma("sync", "xfb", xfb[:, :], xres1[dt * 128:(dt + 1) * 128, :])
            P.wait("vector", "s_pe", mm)
            P.wait("vector", "xfb", ld2)
            if dt >= 2:
                P.wait("vector", "xw%d" % par, gpev2[dt - 2])
            vv = P.op("vector", lambda e, o=fb, psd=psd:
                      e.tensor_add(o, psd[:, :], xfb[:, :]), inc=("s_ve", 1))
            xbv2[dt] = vv
            wo_ve[dt] = vv
            P.wait("gpsimd", "s_ve", vv)
            gpev2[dt] = P.dma("gpsimd", "xw%d" % par,
                              xres2[dt * 128:(dt + 1) * 128, :], fb)
            stats_tile(fb, dt)
        xn_ready = ln_apply(xres2)
        if debug:
            P.wait("gpsimd", "s_ve", xn_ready)
            for c in range(NDCH):
                P.dma("gpsimd", "krw", dbg["d_x2"][c * 128:(c + 1) * 128, :],
                      xres2[c * 128:(c + 1) * 128, :])

        emit_ffn_A(first_pass=False, xn_ready=xn_ready)
        emit_ffn_B(xbase_d=xres2, xdst_d=None, final=True)
        P.wait_tot("sync", "xw0", "xw1")
        P.wait_tot("tensor", "xw0", "xw1")

        @block.sync
        def _(e):
            for fn in P.ops["sync"]:
                fn(e)

        @block.tensor
        def _(e):
            for fn in P.ops["tensor"]:
                fn(e)

        @block.vector
        def _(e):
            for fn in P.ops["vector"]:
                fn(e)

        @block.scalar
        def _(e):
            for fn in P.ops["scalar"]:
                fn(e)

        @block.gpsimd
        def _(e):
            for fn in P.ops["gpsimd"]:
                fn(e)

    return nc, dbg


def _rope_tables():
    inv = 1.0 / (10000.0 ** (np.arange(0, RD, 2, dtype=np.float64) / RD))
    t = np.arange(S, dtype=np.float64)
    fr = np.outer(t, inv)
    emb = np.concatenate([fr, fr], axis=1)
    cos = np.ascontiguousarray(np.cos(emb).T).astype(ml_dtypes.bfloat16)
    sin = np.sin(emb).T
    sinS = sin.copy()
    sinS[:RD // 2] *= -1.0
    sinS = np.ascontiguousarray(sinS).astype(ml_dtypes.bfloat16)
    return cos, sinS


def prepare_inputs(inputs):
    x = np.asarray(inputs["L0_attn_hidden_states"])
    w_qa = np.asarray(inputs["L0_attn_w_qa"])
    g_qa = np.asarray(inputs["L0_attn_g_qa"]).reshape(QR, 1)
    w_qb = np.asarray(inputs["L0_attn_w_qb"])
    w_kva = np.asarray(inputs["L0_attn_w_kva"])
    g_kva = np.asarray(inputs["L0_attn_g_kva"]).reshape(KVR, 1)
    w_kvb = np.asarray(inputs["L0_attn_w_kvb"])
    w_o = np.asarray(inputs["L0_attn_w_o"])
    gate_w = np.asarray(inputs["L0_ffn_gate_w"])
    gate_s = np.asarray(inputs["L0_ffn_gate_s"]).reshape(DFF, 1)
    up_w = np.asarray(inputs["L0_ffn_up_w"])
    up_s = np.asarray(inputs["L0_ffn_up_s"]).reshape(DFF, 1)
    down_w = np.asarray(inputs["L0_ffn_down_w"])
    down_s = np.asarray(inputs["L0_ffn_down_s"]).reshape(D, 1)

    kvb4 = w_kvb.reshape(KVR, NH, ND + VD)
    kvbN = np.ascontiguousarray(kvb4[:, :, :ND].reshape(KVR, NH * ND))
    kvbV = np.ascontiguousarray(kvb4[:, :, ND:].reshape(KVR, NH * VD))
    down_pad = np.zeros((D, DFFP), dtype=np.int32)
    down_pad[:, :DFF] = down_w
    cos, sinS = _rope_tables()

    in_maps = []
    for c in range(NCORE):
        b = c // 4
        r0 = (c % 4) * TOK
        r = c
        m = {
            "xT_in": np.ascontiguousarray(x[b, r0:r0 + TOK, :].T).astype(np.float32),
            "gate_sl": np.ascontiguousarray(gate_w[:, r * DPC:(r + 1) * DPC]),
            "up_sl": np.ascontiguousarray(up_w[:, r * DPC:(r + 1) * DPC]),
            "down_sl": np.ascontiguousarray(down_pad[:, r * DFPC:(r + 1) * DFPC]),
            "gate_s": np.ascontiguousarray(gate_s.reshape(NFT, 128).T).astype(np.float32),
            "up_s": np.ascontiguousarray(up_s.reshape(NFT, 128).T).astype(np.float32),
            "down_s": np.ascontiguousarray(down_s.reshape(NDCH, 128).T).astype(np.float32),
            "qa_sl": np.ascontiguousarray(w_qa[r * DPC:(r + 1) * DPC, :]).astype(np.float32),
            "qb_sl": np.ascontiguousarray(w_qb[r * QRPC:(r + 1) * QRPC, :]).astype(np.float32),
            "g_qa_sl": np.ascontiguousarray(g_qa[r * QRPC:(r + 1) * QRPC, :]).astype(np.float32),
            "kva_sl": np.ascontiguousarray(w_kva[r * DPC:(r + 1) * DPC, :]).astype(np.float32),
            "kvbN_sl": np.ascontiguousarray(kvbN[r * KVRPC:(r + 1) * KVRPC, :]).astype(np.float32),
            "kvbV_sl": np.ascontiguousarray(kvbV[r * KVRPC:(r + 1) * KVRPC, :]).astype(np.float32),
            "g_kva_sl": np.ascontiguousarray(g_kva[r * KVRPC:(r + 1) * KVRPC, :]).astype(np.float32),
            "wo_sl": np.ascontiguousarray(w_o[r * DPC:(r + 1) * DPC, :]).astype(np.float32),
            "cosq": np.ascontiguousarray(cos[:, r0:r0 + TOK]),
            "sinq": np.ascontiguousarray(sinS[:, r0:r0 + TOK]),
            "cosk": cos,
            "sink": sinS,
        }
        in_maps.append(m)
    return in_maps


def kernel(**inputs):
    in_maps = prepare_inputs(inputs)
    nc, _ = build(debug=False)
    res = run_bass_kernel_spmd(nc, in_maps, core_ids=list(range(NCORE)))
    out = np.empty((B, S, D), dtype=np.float32)
    for c in range(NCORE):
        b = c // 4
        r0 = (c % 4) * TOK
        out[b, r0:r0 + TOK, :] = res.results[c]["out_xT"].T
    return out


# version 14
# speedup vs baseline: 1.0431x; 1.0431x over previous
import numpy as np
import ml_dtypes
import concourse.bass as bass
import concourse.mybir as mybir
from concourse.bass_utils import run_bass_kernel_spmd

F32 = mybir.dt.float32
BF16 = mybir.dt.bfloat16
I32 = mybir.dt.int32
AF = mybir.ActivationFunctionType
ALU = mybir.AluOpType
AX = mybir.AxisListType.X

B, S, D = 2, 4096, 4096
NCORE = 8
TOK = 1024
NDCH = 32
DFF = 11008
NFT = 86
NH, QR, KVR, RD, ND, VD = 32, 1536, 512, 64, 128, 128
QK = RD + ND
NQRCH = QR // 128
EPS_LN = 1e-5
EPS_RMS = 1e-6
OFFS = 60.0
H1T = 29
DSUB = [(0, 29), (29, 58), (58, 86)]

DPC = D // NCORE
QRPC = QR // NCORE
KVRPC = KVR // NCORE
DFFP = 11264
DFPC = DFFP // NCORE
GU_RANGES = [(0, 2816), (2816, 5632), (5632, 8448), (8448, 11008)]
CCH = 512

ENGS = ("sync", "tensor", "vector", "scalar", "gpsimd")


class Plan:
    def __init__(self):
        self.ops = {e: [] for e in ENGS}
        self.ctr = {}
        self.seen = {e: {} for e in ENGS}
        self.sems = {}
        self.bg = []

    def emit(self, eng, fn):
        self.ops[eng].append(fn)

    def bump(self, name, amt):
        self.ctr[name] = self.ctr.get(name, 0) + amt
        return self.ctr[name]

    def cur(self, name):
        return self.ctr.get(name, 0)

    def wait(self, eng, name, val):
        if val is None or val <= 0:
            return
        if self.seen[eng].get(name, 0) >= val:
            return
        self.seen[eng][name] = val
        sem = self.sems[name]
        self.emit(eng, lambda e, sem=sem, val=val: e.wait_ge(sem, val))

    ENG_SEM = {"vector": "s_ve", "scalar": "s_ac", "gpsimd": "s_gpc"}

    def op(self, eng, fn, inc=None):
        if inc is None and eng in self.ENG_SEM:
            name = self.ENG_SEM[eng]
            self.wait(eng, name, self.cur(name))
            inc = (name, 1)
        elif inc is not None and eng in self.ENG_SEM and inc[0] == self.ENG_SEM[eng]:
            self.wait(eng, inc[0], self.cur(inc[0]))
        if inc is None:
            self.emit(eng, fn)
            return None
        name, amt = inc
        val = self.bump(name, amt)
        sem = self.sems[name]
        self.emit(eng, lambda e, fn=fn, sem=sem, amt=amt: fn(e).then_inc(sem, amt))
        return val

    def dma(self, eng, sem, out, in_, **kw):
        return self.op(eng, lambda e, out=out, in_=in_, kw=kw:
                       e.dma_start(out=out, in_=in_, **kw), inc=(sem, 16))

    def wait_tot(self, eng, *names):
        for n in names:
            self.wait(eng, n, self.cur(n))

    def cc(self, kind, groups, ins, outs):
        val = self.bump("s_cc", 1)
        sem = self.sems["s_cc"]
        self.emit("gpsimd", lambda e, kind=kind, groups=groups, ins=ins, outs=outs,
                  sem=sem: e.collective_compute(kind, ALU.bypass,
                                                replica_groups=groups,
                                                ins=ins, outs=outs).then_inc(sem))
        return val

    def add_bg(self, thunk):
        self.bg.append(thunk)

    def drain_bg(self, k=1):
        for _ in range(k):
            if not self.bg:
                return
            self.bg.pop(0)(self)

    def drain_all_bg(self):
        while self.bg:
            self.bg.pop(0)(self)

    def psum_fence(self):
        self.wait("tensor", "s_ve", self.cur("s_ve"))
        self.wait("tensor", "s_ac", self.cur("s_ac"))

    def sync_fence(self):
        self.wait("sync", "s_pe", self.cur("s_pe"))
        self.wait("sync", "s_ve", self.cur("s_ve"))
        self.wait("sync", "s_ac", self.cur("s_ac"))


def build(debug=False):
    nc = bass.Bass(target_bir_lowering=False)
    P = Plan()

    def param(name, shape, dtype, out=False):
        return nc.declare_dram_parameter(name, list(shape), dtype, isOutput=out).ap()

    def idram(name, shape, dtype):
        return nc.dram_tensor(name, list(shape), dtype, kind="Internal").ap()

    def sdram(name, shape, dtype):
        return nc.dram_tensor(name, list(shape), dtype, kind="Internal",
                              addr_space="Shared").ap()

    xT_in = param("xT_in", (D, TOK), F32)
    gate_sl = param("gate_sl", (DFF, DPC), I32)
    up_sl = param("up_sl", (DFF, DPC), I32)
    down_sl = param("down_sl", (D, DFPC), I32)
    gate_s = param("gate_s", (128, NFT), F32)
    up_s = param("up_s", (128, NFT), F32)
    down_s = param("down_s", (128, NDCH), F32)
    qa_sl = param("qa_sl", (DPC, QR), F32)
    qb_sl = param("qb_sl", (QRPC, NH * QK), F32)
    g_qa_sl = param("g_qa_sl", (QRPC, 1), F32)
    kva_sl = param("kva_sl", (DPC, KVR + RD), F32)
    kvbN_sl = param("kvbN_sl", (KVRPC, NH * ND), F32)
    kvbV_sl = param("kvbV_sl", (KVRPC, NH * VD), F32)
    g_kva_sl = param("g_kva_sl", (KVRPC, 1), F32)
    wo_sl = param("wo_sl", (DPC, D), F32)
    cosq = param("cosq", (RD, TOK), BF16)
    sinq = param("sinq", (RD, TOK), BF16)
    cosk = param("cosk", (RD, S), BF16)
    sink = param("sink", (RD, S), BF16)
    out_xT = param("out_xT", (D, TOK), F32, out=True)

    dbg = {}
    if debug:
        for name, shape, dt_ in [("d_xn0", (D, TOK), BF16), ("d_x1", (D, TOK), F32),
                                 ("d_qc", (QR, TOK), BF16), ("d_ckc", (KVR, TOK), BF16),
                                 ("d_x2", (D, TOK), F32), ("d_key", (4 * NH * ND, TOK), BF16),
                                 ("d_vv", (4 * TOK, NH * VD), BF16), ("d_oT", (D, TOK), BF16)]:
            dbg[name] = param(name, shape, dt_, out=True)

    gate_nb = idram("gate_nb", (DFF, DPC), BF16)
    up_nb = idram("up_nb", (DFF, DPC), BF16)
    down_nb = idram("down_nb", (D, DFPC), BF16)
    gu_pcA = idram("gu_pcA", (DPC, DFF), BF16)
    gu_pcB = idram("gu_pcB", (DPC, DFF), BF16)
    down_pc = idram("down_pc", (DFPC, D), BF16)
    qa_pc = idram("qa_pc", (DPC, QR), BF16)
    qb_pc = idram("qb_pc", (QRPC, NH * QK), BF16)
    kva_pc = idram("kva_pc", (DPC, KVR + RD), BF16)
    kvbN_pc = idram("kvbN_pc", (KVRPC, NH * ND), BF16)
    kvbV_pc = idram("kvbV_pc", (KVRPC, NH * VD), BF16)
    wo_pc = idram("wo_pc", (DPC, D), BF16)
    gu_TA = sdram("gu_TA", (D, DFF), BF16)
    gu_TB = sdram("gu_TB", (D, DFF), BF16)
    down_T = sdram("down_T", (DFFP, D), BF16)
    qa_T = sdram("qa_T", (D, QR), BF16)
    qb_T = sdram("qb_T", (QR, NH * QK), BF16)
    kva_T = sdram("kva_T", (D, KVR + RD), BF16)
    kvbN_T = sdram("kvbN_T", (KVR, NH * ND), BF16)
    kvbV_T = sdram("kvbV_T", (KVR, NH * VD), BF16)
    wo_T = sdram("wo_T", (D, D), BF16)
    h2_d = idram("h2_d", ((NFT - H1T) * 128, TOK), BF16)
    part_a = idram("part_a", (D, TOK), F32)
    part_b = idram("part_b", (D, TOK), F32)
    xres1 = idram("xres1", (D, TOK), F32)
    xres2 = idram("xres2", (D, TOK), F32)
    k_in0 = idram("k_in0", (NH * ND // 2, TOK), BF16)
    k_in1 = idram("k_in1", (NH * ND // 2, TOK), BF16)
    v_in0 = idram("v_in0", (TOK, NH * VD // 2), BF16)
    v_in1 = idram("v_in1", (TOK, NH * VD // 2), BF16)
    kr_in = idram("kr_in", (RD, TOK), BF16)
    km_in = idram("km_in", (1, NH), F32)
    km_T = idram("km_T", (4, NH), F32)
    k_T0 = idram("k_T0", (4 * NH * ND // 2, TOK), BF16)
    k_T1 = idram("k_T1", (4 * NH * ND // 2, TOK), BF16)
    v_T0 = idram("v_T0", (4 * TOK, NH * VD // 2), BF16)
    v_T1 = idram("v_T1", (4 * TOK, NH * VD // 2), BF16)
    kr_T = idram("kr_T", (4 * RD, TOK), BF16)

    groups8 = [list(range(8))]
    groups4 = [[0, 1, 2, 3], [4, 5, 6, 7]]

    H1C = H1T * TOK
    WG = [H1C, H1C + 4096, H1C + 8192, H1C + 12288]
    GEV = [H1C + 16384, H1C + 17408]
    HOV = [H1C + 18432, H1C + 19456]
    ST_X = [37120, 38144]
    ST_SQ = [39168, 40192]
    AT_QC = 0
    AT_KB = [12288, 16384]
    AT_VB = [20480, 24576]
    AT_SQK = 28672
    AT_ATB = [32768, 33792]
    AT_QW = 34816
    AT_QN = 37120
    AT_QA = 38144
    AT_R1 = 39168
    AT_R2 = 40192
    AT_CSQ = 41216
    AT_SSQ = 42240
    AT_KRA = 43264
    AB_COLS = 50176

    from contextlib import ExitStack
    with ExitStack() as _stack:
        xnT = _stack.enter_context(nc.sbuf_tensor("xnT", [128, NDCH * TOK], BF16))
        bc0 = _stack.enter_context(nc.sbuf_tensor("bc0", [128, TOK], F32))
        bc1 = _stack.enter_context(nc.sbuf_tensor("bc1", [128, TOK], F32))
        vrow1 = _stack.enter_context(nc.sbuf_tensor("vrow1", [128, TOK], F32))
        vrow2 = _stack.enter_context(nc.sbuf_tensor("vrow2", [128, TOK], F32))
        ones_b = _stack.enter_context(nc.sbuf_tensor("ones_b", [128, 1], BF16))
        onesrow = _stack.enter_context(nc.sbuf_tensor("onesrow", [1, 128], F32))
        scal = _stack.enter_context(nc.sbuf_tensor("scal", [32, 8], F32))
        tbuf = _stack.enter_context(nc.sbuf_tensor("tbuf", [128, 2048], BF16))
        cio_i = _stack.enter_context(nc.sbuf_tensor("cio_i", [128, 2 * CCH], I32))
        cio_o = _stack.enter_context(nc.sbuf_tensor("cio_o", [128, 2 * CCH], BF16))
        cio_f = _stack.enter_context(nc.sbuf_tensor("cio_f", [128, 2 * CCH], F32))
        sv_g = _stack.enter_context(nc.sbuf_tensor("sv_g", [128, NFT], F32))
        sv_u = _stack.enter_context(nc.sbuf_tensor("sv_u", [128, NFT], F32))
        sv_d = _stack.enter_context(nc.sbuf_tensor("sv_d", [128, NDCH], F32))
        sv_q = _stack.enter_context(nc.sbuf_tensor("sv_q", [128, 2], F32))
        sv_k = _stack.enter_context(nc.sbuf_tensor("sv_k", [128, 1], F32))
        AB = _stack.enter_context(nc.sbuf_tensor("AB", [128, AB_COLS], BF16))
        AFX = _stack.enter_context(nc.sbuf_tensor("AFX", [128, 2 * TOK], F32))
        xfb = _stack.enter_context(nc.sbuf_tensor("xfb", [128, TOK], F32))
        ps0 = _stack.enter_context(nc.psum_tensor("ps0", [128, TOK], F32))
        ps1 = _stack.enter_context(nc.psum_tensor("ps1", [128, TOK], F32))
        ps2 = _stack.enter_context(nc.psum_tensor("ps2", [128, TOK], F32))
        ps3 = _stack.enter_context(nc.psum_tensor("ps3", [128, TOK], F32))
        SEM_NAMES = ["s_cc", "s_pe", "s_ac", "s_ve",
                     "svl", "cvi", "cvo0", "cvo1", "tr", "trs",
                     "fb0", "fb1", "xfb", "w0", "w1",
                     "hst0", "hst1", "hld", "pst0", "pst1", "xw0", "xw1",
                     "kvw0", "kvw1", "vst", "krw", "tab",
                     "kv0", "kv1", "qw", "qsw", "aug", "s_gpc"]
        for _sn in SEM_NAMES:
            P.sems[_sn] = _stack.enter_context(nc.semaphore(_sn))
        block = _stack.enter_context(nc.Block())
        PS = [ps0, ps1, ps2, ps3]

        P.op("vector", lambda e: e.memset(ones_b[:, :], 1.0), inc=("s_ve", 1))
        P.op("vector", lambda e: e.memset(onesrow[:, :], 1.0), inc=("s_ve", 1))
        P.dma("gpsimd", "svl", sv_g[:, :], gate_s)
        P.dma("gpsimd", "svl", sv_u[:, :], up_s)
        P.dma("gpsimd", "svl", sv_d[:, :], down_s)
        P.dma("gpsimd", "svl", sv_q[:, 0:1], g_qa_sl[0:128, :])
        P.dma("gpsimd", "svl", sv_q[0:64, 1:2], g_qa_sl[128:192, :])
        P.dma("gpsimd", "svl", sv_k[0:64, 0:1], g_kva_sl[:, :])
        P.wait("vector", "svl", P.cur("svl"))

        conv = {"n": 0, "cast": {}, "store": {}, "tr": None}

        def conv_chain(src, dst, r0, nr, c0, cw, is_f32, scale_ap, const,
                       eng_ld="gpsimd"):
            n = conv["n"]
            conv["n"] = n + 1
            par = n % 2
            if is_f32:
                ib = cio_f[0:nr, par * CCH: par * CCH + cw]
            else:
                ib = cio_i[0:nr, par * CCH: par * CCH + cw]
            ob = cio_o[0:nr, par * CCH: par * CCH + cw]
            if n - 2 in conv["cast"]:
                P.wait(eng_ld, "s_ve", conv["cast"][n - 2])
                P.wait("vector", "cvo%d" % par, conv["store"][n - 2])
                del conv["cast"][n - 2], conv["store"][n - 2]
            ld = P.dma(eng_ld, "cvi", ib, src[r0:r0 + nr, c0:c0 + cw])
            P.wait("vector", "cvi", ld)
            if scale_ap is None and const is None:
                cv = P.op("vector", lambda e, o=ob, i=ib: e.tensor_copy(o, i),
                          inc=("s_ve", 1))
            elif scale_ap is None:
                cv = P.op("vector", lambda e, o=ob, i=ib, k=const:
                          e.tensor_scalar_mul(o, i, k), inc=("s_ve", 1))
            elif const is None:
                cv = P.op("vector", lambda e, o=ob, i=ib, s=scale_ap:
                          e.tensor_scalar_mul(o, i, s), inc=("s_ve", 1))
            else:
                cv = P.op("vector", lambda e, o=ob, i=ib, s=scale_ap, k=const:
                          e.tensor_scalar(o, i, s, k, ALU.mult, ALU.mult),
                          inc=("s_ve", 1))
            P.wait("gpsimd", "s_ve", cv)
            st = P.dma("gpsimd", "cvo%d" % par, dst[r0:r0 + nr, c0:c0 + cw], ob)
            conv["cast"][n] = cv
            conv["store"][n] = st

        def conv_transp(nb, pc, c, dstc0, rw, srcr0=None):
            if srcr0 is None:
                srcr0 = dstc0
            if conv["tr"] is not None:
                P.wait("sync", "trs", conv["tr"])
            tr = P.dma("sync", "tr", tbuf[:, 0:rw],
                       nb[srcr0:srcr0 + rw, c * 128:(c + 1) * 128], transpose=True)
            P.wait("gpsimd", "tr", tr)
            st = P.dma("gpsimd", "trs", pc[c * 128:(c + 1) * 128, dstc0:dstc0 + rw],
                       tbuf[:, 0:rw])
            conv["tr"] = st

        def cast_matrix_i32(src, dst, nrows, sv, hook=None):
            for t in range(nrows // 128):
                for c0 in range(0, src.shape[1], CCH):
                    cw = min(CCH, src.shape[1] - c0)
                    conv_chain(src, dst, t * 128, 128, c0, cw, False,
                               sv[:, t:t + 1], None, eng_ld="gpsimd")
                    if hook is not None:
                        hook()

        HDFF = (NFT // 2) * 128

        def gu_ready(t):
            if t < NFT // 2:
                return gu_ag[0]
            assert len(gu_ag) > 1, "B-half AG not registered by tile %d" % t
            return gu_ag[1]

        gu_ag = []
        down_ag_v, qa_ag_v, qb_ag_v = [], [], []
        kva_ag_v, kvbN_ag_v, kvbV_ag_v, wo_ag_v = [], [], [], []

        def bg_chain(src, dst, r0, nr, c0, cw, is_f32, scale_fn, const, fin):
            def run(_):
                sc = scale_fn() if scale_fn else None
                conv_chain(src, dst, r0, nr, c0, cw, is_f32, sc, const)
                if fin is not None:
                    fin()
            return run

        def add_f32_matrix(src, dst, nrows, ncols, aglist, out_T, scale_fn=None,
                           const=None):
            tiles = []
            nrt = (nrows + 127) // 128
            for t in range(nrt):
                r0 = t * 128
                nr = min(128, nrows - r0)
                for c0 in range(0, ncols, CCH):
                    cw = min(CCH, ncols - c0)
                    tiles.append((r0, nr, c0, cw, t))
            for i, (r0, nr, c0, cw, t) in enumerate(tiles):
                fin = None
                if i == len(tiles) - 1:
                    def fin(agl=aglist, dd=dst, oT=out_T):
                        P.wait_tot("gpsimd", "cvo0", "cvo1")
                        agl.append(P.cc("AllGather", groups8,
                                        [dd.opt()], [oT.opt()]))
                sf = (lambda t=t, nr=nr: scale_fn(t, nr)) if scale_fn else None
                P.add_bg(bg_chain(src, dst, r0, nr, c0, cw, True, sf, const, fin))

        qk_scale = float(QK) ** -0.5
        add_f32_matrix(qa_sl, qa_pc, DPC, QR, qa_ag_v, qa_T)
        add_f32_matrix(kva_sl, kva_pc, DPC, KVR + RD, kva_ag_v, kva_T)
        add_f32_matrix(qb_sl, qb_pc, QRPC, NH * QK, qb_ag_v, qb_T,
                       scale_fn=lambda t, nr: sv_q[0:nr, t:t + 1], const=qk_scale)
        add_f32_matrix(kvbN_sl, kvbN_pc, KVRPC, NH * ND, kvbN_ag_v, kvbN_T,
                       scale_fn=lambda t, nr: sv_k[0:nr, 0:1])
        add_f32_matrix(kvbV_sl, kvbV_pc, KVRPC, NH * VD, kvbV_ag_v, kvbV_T,
                       scale_fn=lambda t, nr: sv_k[0:nr, 0:1])
        add_f32_matrix(wo_sl, wo_pc, DPC, D, wo_ag_v, wo_T)

        stats_state = {"pe": 0}

        def stats_tile(xsrc_f32_sb, dt):
            par = dt % 2
            xbb = AB[:, ST_X[par]:ST_X[par] + TOK]
            sq = AB[:, ST_SQ[par]:ST_SQ[par] + TOK]
            P.wait("vector", "s_pe", stats_state["pe"] if dt >= 2 else 0)
            cv = P.op("vector", lambda e, o=xbb, i=xsrc_f32_sb:
                      e.tensor_copy(o, i), inc=("s_ve", 1))
            P.wait("scalar", "s_ve", cv)
            sv = P.op("scalar", lambda e, o=sq, i=xbb:
                      e.activation(o, i, AF.Square), inc=("s_ac", 1))
            P.wait("tensor", "s_ac", sv)
            f, l = (dt == 0), (dt == NDCH - 1)
            P.op("tensor", lambda e, i=xbb, f=f, l=l:
                 e.matmul(ps2[0:1, 0:512], ones_b[:, :], i[:, 0:512], start=f, stop=l))
            P.op("tensor", lambda e, i=xbb, f=f, l=l:
                 e.matmul(ps2[0:1, 512:1024], ones_b[:, :], i[:, 512:1024],
                          start=f, stop=l))
            P.op("tensor", lambda e, i=sq, f=f, l=l:
                 e.matmul(ps3[0:1, 0:512], ones_b[:, :], i[:, 0:512], start=f, stop=l))
            pe = P.op("tensor", lambda e, i=sq, f=f, l=l:
                      e.matmul(ps3[0:1, 512:1024], ones_b[:, :], i[:, 512:1024],
                               start=f, stop=l), inc=("s_pe", 1))
            stats_state["pe"] = pe
            return cv

        def bcast_row(src_row, psd, dst):
            P.op("tensor", lambda e: e.matmul(psd[:, 0:512], onesrow[:, :],
                                              src_row[:, 0:512], start=True, stop=True))
            pe = P.op("tensor", lambda e: e.matmul(psd[:, 512:1024], onesrow[:, :],
                                                   src_row[:, 512:1024],
                                                   start=True, stop=True),
                      inc=("s_pe", 1))
            P.wait("vector", "s_pe", pe)
            return P.op("vector", lambda e: e.tensor_copy(dst[:, :], psd[:, :]),
                        inc=("s_ve", 1))

        def finalize_ln():
            mean = bc0[0:1, :]
            var = vrow1[0:1, :]
            rstd = bc1[0:1, :]
            t = vrow2[0:1, :]
            P.wait("vector", "s_pe", P.cur("s_pe"))
            P.op("vector", lambda e: e.tensor_scalar_mul(mean, ps2[0:1, :], 1.0 / D))
            v2 = P.op("vector", lambda e: e.tensor_scalar_mul(var, ps3[0:1, :], 1.0 / D),
                      inc=("s_ve", 1))
            P.wait("vector", "s_ve", v2)
            P.op("vector", lambda e: e.tensor_mul(t, mean, mean))
            P.op("vector", lambda e: e.tensor_sub(var, var, t))
            v3 = P.op("vector", lambda e: e.tensor_scalar_add(var, var, EPS_LN),
                      inc=("s_ve", 1))
            P.wait("scalar", "s_ve", v3)
            a1 = P.op("scalar", lambda e: e.activation(rstd, var, AF.Sqrt),
                      inc=("s_ac", 1))
            P.wait("vector", "s_ac", a1)
            v4 = P.op("vector", lambda e: e.reciprocal(rstd, rstd), inc=("s_ve", 1))
            P.wait("tensor", "s_ve", v4)
            P.psum_fence()
            bcast_row(mean, ps0, bc0)
            return bcast_row(rstd, ps1, bc1)

        def ln_apply(xsrc_d):
            bdone = finalize_ln()
            P.wait_tot("sync", "xw0", "xw1")
            P.wait("sync", "s_ve", P.cur("s_ve"))
            applied = {}
            for c in range(NDCH):
                par = c % 2
                xb = AFX[:, par * TOK:(par + 1) * TOK]
                if c >= 2:
                    P.wait("sync", "s_ve", applied[c - 2])
                ld = P.dma("sync", "fb%d" % par, xb, xsrc_d[c * 128:(c + 1) * 128, :])
                P.wait("vector", "fb%d" % par, ld)
                P.wait("vector", "s_ve", bdone)
                P.op("vector", lambda e, xb=xb: e.tensor_sub(xb, xb, bc0[:, :]))
                applied[c] = P.op("vector", lambda e, xb=xb, c=c:
                                  e.tensor_mul(xnT[:, c * TOK:(c + 1) * TOK], xb,
                                               bc1[:, :]), inc=("s_ve", 1))
            return P.cur("s_ve")

        P.psum_fence()
        ln0 = {"c": 0, "cv": {}}

        def ln0_step():
            c = ln0["c"]
            if c >= NDCH:
                return
            ln0["c"] = c + 1
            par = c % 2
            xb = AFX[:, par * TOK:(par + 1) * TOK]
            if c >= 2:
                P.wait("sync", "s_ve", ln0["cv"][c - 2])
            ld = P.dma("sync", "fb%d" % par, xb, xT_in[c * 128:(c + 1) * 128, :])
            P.wait("vector", "fb%d" % par, ld)
            ln0["cv"][c] = stats_tile(xb, c)

        NHT = NFT // 2
        cast_matrix_i32(gate_sl[0:HDFF, :], gate_nb[0:HDFF, :], HDFF,
                        sv_g[:, 0:NHT], hook=ln0_step)
        while ln0["c"] < NDCH:
            ln0_step()
        g_cvo = (P.cur("cvo0"), P.cur("cvo1"))
        xn_ready = ln_apply(xT_in)
        cast_matrix_i32(up_sl[0:HDFF, :], up_nb[0:HDFF, :], HDFF, sv_u[:, 0:NHT])
        P.wait("sync", "cvo0", g_cvo[0])
        P.wait("sync", "cvo1", g_cvo[1])
        for c in range(DPC // 128):
            for r0 in range(0, HDFF, 2048):
                conv_transp(gate_nb, gu_pcA, c, r0, min(2048, HDFF - r0))
        P.wait_tot("sync", "cvo0", "cvo1")
        for c in range(DPC // 128):
            for r0 in range(0, HDFF, 2048):
                conv_transp(up_nb, gu_pcA, c, HDFF + r0, min(2048, HDFF - r0),
                            srcr0=r0)
        P.wait_tot("gpsimd", "trs")
        gu_ag.append(P.cc("AllGather", groups8, [gu_pcA.opt()], [gu_TA.opt()]))

        def fin_B():
            P.wait_tot("sync", "cvo0", "cvo1")
            for c in range(DPC // 128):
                for r0 in range(0, DFF - HDFF, 2048):
                    conv_transp(gate_nb, gu_pcB, c, r0,
                                min(2048, DFF - HDFF - r0), srcr0=HDFF + r0)
            for c in range(DPC // 128):
                for r0 in range(0, DFF - HDFF, 2048):
                    conv_transp(up_nb, gu_pcB, c, HDFF + r0,
                                min(2048, DFF - HDFF - r0), srcr0=HDFF + r0)
            P.wait_tot("gpsimd", "trs")
            gu_ag.append(P.cc("AllGather", groups8, [gu_pcB.opt()], [gu_TB.opt()]))

        def fin_down():
            P.wait_tot("sync", "cvo0", "cvo1")
            for c in range(DFPC // 128):
                conv_transp(down_nb, down_pc, c, 0, 2048)
                conv_transp(down_nb, down_pc, c, 2048, 2048)
            P.wait_tot("gpsimd", "trs")
            down_ag_v.append(P.cc("AllGather", groups8,
                                  [down_pc.opt()], [down_T.opt()]))

        def reg_i32_bg(src, dst, nrows, sv, fin):
            tiles = []
            for t in range(nrows // 128):
                for c0 in range(0, src.shape[1], CCH):
                    tiles.append((t, c0, min(CCH, src.shape[1] - c0)))
            for i, (t, c0, cw) in enumerate(tiles):
                P.add_bg(bg_chain(src, dst, t * 128, 128, c0, cw, False,
                                  (lambda t=t, sv=sv: sv[:, t:t + 1]), None,
                                  fin if i == len(tiles) - 1 else None))

        pre_bg = P.bg[:]
        P.bg = []
        reg_i32_bg(gate_sl[HDFF:DFF, :], gate_nb[HDFF:DFF, :], DFF - HDFF,
                   sv_g[:, NHT:NFT], None)
        reg_i32_bg(up_sl[HDFF:DFF, :], up_nb[HDFF:DFF, :], DFF - HDFF,
                   sv_u[:, NHT:NFT], fin_B)
        reg_i32_bg(down_sl, down_nb, D, sv_d, fin_down)
        P.bg.extend(pre_bg)
        if debug:
            P.wait("gpsimd", "s_ve", xn_ready)
            for c in range(NDCH):
                P.dma("gpsimd", "krw", dbg["d_xn0"][c * 128:(c + 1) * 128, :],
                      xnT[:, c * TOK:(c + 1) * TOK])

        def emit_ffn_A(first_pass, xn_ready):
            P.psum_fence()
            P.sync_fence()
            mm_of, ve_of, ac_of, gp_of = {}, {}, {}, {}
            for t in range(NFT):
                par = t % 2
                gbuf = AB[:, WG[par]:WG[par] + 4096]
                ubuf = AB[:, WG[2 + par]:WG[2 + par] + 4096]
                if first_pass:
                    P.wait("sync", "s_cc", gu_ready(t))
                if t >= 2:
                    P.wait("sync", "s_pe", mm_of[t - 2])
                gu_Th = gu_TA if t < NFT // 2 else gu_TB
                th = t if t < NFT // 2 else t - NFT // 2
                P.dma("sync", "w%d" % par, gbuf.rearrange("p (c f) -> p c f", c=NDCH),
                      gu_Th[:, th * 128:(th + 1) * 128].rearrange(
                          "(c p) f -> p c f", p=128))
                lu = P.dma("sync", "w%d" % par, ubuf.rearrange("p (c f) -> p c f", c=NDCH),
                           gu_Th[:, HDFF + th * 128:HDFF + (th + 1) * 128].rearrange(
                               "(c p) f -> p c f", p=128))
                psg, psu = PS[par], PS[2 + par]
                P.wait("tensor", "w%d" % par, lu)
                P.wait("tensor", "s_ve", xn_ready)
                if t >= 2:
                    P.wait("tensor", "s_ve", ve_of[t - 2])
                    P.wait("tensor", "s_ac", ac_of[t - 2])
                for c in range(NDCH):
                    lt = gbuf[:, c * 128:(c + 1) * 128]
                    f, l = (c == 0), (c == NDCH - 1)
                    P.op("tensor", lambda e, lt=lt, c=c, psg=psg, f=f, l=l:
                         e.matmul(psg[:, 0:512], lt, xnT[:, c * TOK:c * TOK + 512],
                                  start=f, stop=l))
                    P.op("tensor", lambda e, lt=lt, c=c, psg=psg, f=f, l=l:
                         e.matmul(psg[:, 512:1024], lt,
                                  xnT[:, c * TOK + 512:(c + 1) * TOK], start=f, stop=l))
                for c in range(NDCH):
                    lt = ubuf[:, c * 128:(c + 1) * 128]
                    f, l = (c == 0), (c == NDCH - 1)
                    P.op("tensor", lambda e, lt=lt, c=c, psu=psu, f=f, l=l:
                         e.matmul(psu[:, 0:512], lt, xnT[:, c * TOK:c * TOK + 512],
                                  start=f, stop=l))
                    mm = P.op("tensor", lambda e, lt=lt, c=c, psu=psu, f=f, l=l:
                              e.matmul(psu[:, 512:1024], lt,
                                       xnT[:, c * TOK + 512:(c + 1) * TOK],
                                       start=f, stop=l),
                              inc=("s_pe", 1) if l else None)
                mm_of[t] = mm
                gev = AB[:, GEV[par]:GEV[par] + TOK]
                P.wait("scalar", "s_pe", mm)
                if t >= 2:
                    P.wait("scalar", "s_ve", ve_of[t - 2])
                av = P.op("scalar", lambda e, o=gev, psg=psg:
                          e.activation(o, psg[:, :], AF.Silu), inc=("s_ac", 1))
                ac_of[t] = av
                if t < H1T:
                    hdst = AB[:, t * TOK:(t + 1) * TOK]
                else:
                    hdst = AB[:, HOV[par]:HOV[par] + TOK]
                    if t - 2 >= H1T:
                        P.wait("vector", "hst%d" % par, gp_of[t - 2])
                P.wait("vector", "s_ac", av)
                hv = P.op("vector", lambda e, o=hdst, psu=psu, g=gev:
                          e.tensor_tensor(o, psu[:, :], g, ALU.mult), inc=("s_ve", 1))
                ve_of[t] = hv
                if t >= H1T:
                    P.wait("gpsimd", "s_ve", hv)
                    gp_of[t] = P.dma("gpsimd", "hst%d" % par,
                                     h2_d[(t - H1T) * 128:(t - H1T + 1) * 128, :],
                                     hdst)
                P.drain_bg(4)
            return P.cur("s_ve")

        def emit_ffn_B(xbase_d, xdst_d, final):
            assert down_ag_v, "down AG not emitted yet"
            P.wait("sync", "s_cc", down_ag_v[0])
            P.sync_fence()
            P.psum_fence()
            SB = [H1C, H1C + 3712]
            gpev = {}
            xbv = {}
            for sub, (ca, cb) in enumerate(DSUB):
                nch = cb - ca
                if sub > 0:
                    P.wait("gpsimd", "s_pe", P.cur("s_pe"))
                    P.wait("gpsimd", "s_ve", P.cur("s_ve"))
                    P.wait("gpsimd", "s_ac", P.cur("s_ac"))
                    P.wait_tot("gpsimd", "hst0", "hst1")
                    for i in range(nch):
                        P.dma("gpsimd", "hld", AB[:, i * TOK:(i + 1) * TOK],
                              h2_d[(ca - H1T + i) * 128:(ca - H1T + i + 1) * 128, :])
                    P.wait_tot("tensor", "hld")
                    P.wait_tot("sync", "pst0", "pst1")
                mm_of, ve_of = {}, {}
                for dt in range(NDCH):
                    par = dt % 2
                    sbv = AB[:, SB[par]:SB[par] + nch * 128]
                    if dt >= 2:
                        P.wait("sync", "s_pe", mm_of[dt - 2])
                    lw = P.dma("sync", "w%d" % par, sbv.rearrange("p (c f) -> p c f", c=nch),
                               down_T[ca * 128:cb * 128,
                                      dt * 128:(dt + 1) * 128].rearrange(
                                          "(c p) f -> p c f", p=128))
                    psd = PS[par]
                    P.wait("tensor", "w%d" % par, lw)
                    if dt >= 2:
                        P.wait("tensor", "s_ve", ve_of[dt - 2])
                    for c in range(nch):
                        lt = sbv[:, c * 128:(c + 1) * 128]
                        f, l = (c == 0), (c == nch - 1)
                        P.op("tensor", lambda e, lt=lt, c=c, psd=psd, f=f, l=l:
                             e.matmul(psd[:, 0:512], lt, AB[:, c * TOK:c * TOK + 512],
                                      start=f, stop=l))
                        mm = P.op("tensor", lambda e, lt=lt, c=c, psd=psd, f=f, l=l:
                                  e.matmul(psd[:, 512:1024], lt,
                                           AB[:, c * TOK + 512:(c + 1) * TOK],
                                           start=f, stop=l),
                                  inc=("s_pe", 1) if l else None)
                    mm_of[dt] = mm
                    fb = AFX[:, par * TOK:(par + 1) * TOK]
                    if sub == 0:
                        P.wait("vector", "s_pe", mm)
                        if dt >= 2:
                            P.wait("vector", "pst%d" % par, gpev[dt - 2])
                        vv = P.op("vector", lambda e, o=fb, psd=psd:
                                  e.tensor_copy(o, psd[:, :]), inc=("s_ve", 1))
                        ve_of[dt] = vv
                        P.wait("gpsimd", "s_ve", vv)
                        gpev[dt] = P.dma("gpsimd", "pst%d" % par,
                                         part_a[dt * 128:(dt + 1) * 128, :], fb)
                    elif sub == 1:
                        if dt >= 2:
                            P.wait("sync", "s_ve", ve_of[dt - 2])
                        if dt >= 2:
                            P.wait("sync", "pst%d" % par, gpev[dt - 2])
                        ld = P.dma("sync", "fb%d" % par, fb,
                                   part_a[dt * 128:(dt + 1) * 128, :])
                        P.wait("vector", "s_pe", mm)
                        P.wait("vector", "fb%d" % par, ld)
                        if dt >= 2:
                            P.wait("vector", "pst%d" % par, gpev[dt - 2])
                        vv = P.op("vector", lambda e, o=fb, psd=psd:
                                  e.tensor_add(o, psd[:, :], o), inc=("s_ve", 1))
                        ve_of[dt] = vv
                        P.wait("gpsimd", "s_ve", vv)
                        gpev[dt] = P.dma("gpsimd", "pst%d" % par,
                                         part_b[dt * 128:(dt + 1) * 128, :], fb)
                    else:
                        if dt >= 2:
                            P.wait("sync", "s_ve", ve_of[dt - 2])
                        if dt >= 2:
                            P.wait("sync", "xw%d" % par, gpev[dt - 2])
                        ld1 = P.dma("sync", "fb%d" % par, fb,
                                    part_b[dt * 128:(dt + 1) * 128, :])
                        P.wait("sync", "s_ve", xbv.get(dt - 1, 0))
                        ld2 = P.dma("sync", "xfb", xfb[:, :],
                                    xbase_d[dt * 128:(dt + 1) * 128, :])
                        P.wait("vector", "s_pe", mm)
                        P.wait("vector", "fb%d" % par, ld1)
                        P.wait("vector", "xfb", ld2)
                        if dt >= 2:
                            P.wait("vector", "xw%d" % par, gpev[dt - 2])
                        P.op("vector", lambda e, o=fb, psd=psd:
                             e.tensor_add(o, psd[:, :], o))
                        vv = P.op("vector", lambda e, o=fb:
                                  e.scalar_tensor_tensor(o, o, 0.5, xfb[:, :],
                                                         ALU.mult, ALU.add),
                                  inc=("s_ve", 1))
                        xbv[dt] = vv
                        ve_of[dt] = vv
                        P.wait("gpsimd", "s_ve", vv)
                        dst = out_xT if final else xdst_d
                        gpev[dt] = P.dma("gpsimd", "xw%d" % par,
                                         dst[dt * 128:(dt + 1) * 128, :], fb)
                        if not final:
                            ve_of[dt] = stats_tile(fb, dt)
                    P.drain_bg(4)
            if final:
                return None
            return ln_apply(xdst_d)

        emit_ffn_A(first_pass=True, xn_ready=xn_ready)
        xn_ready = emit_ffn_B(xbase_d=xT_in, xdst_d=xres1, final=False)
        if debug:
            P.wait("gpsimd", "s_ve", xn_ready)
            for c in range(NDCH):
                P.dma("gpsimd", "krw", dbg["d_x1"][c * 128:(c + 1) * 128, :],
                      xres1[c * 128:(c + 1) * 128, :])

        P.drain_all_bg()
        assert qa_ag_v and qb_ag_v and kva_ag_v and kvbN_ag_v and kvbV_ag_v \
            and wo_ag_v

        P.psum_fence()
        P.sync_fence()
        CK = AT_VB[0]
        nrm = bc0[0:1, :]
        km_ve = {}

        P.wait("sync", "s_cc", kva_ag_v[0])
        P.wait("sync", "s_pe", P.cur("s_pe"))
        P.psum_fence()
        P.wait("tensor", "s_ve", xn_ready)
        mm_of, kv_cv = {}, {}
        for kt in range(5):
            ncols = 128 if kt < 4 else 64
            par = kt % 2
            wb = AB[:, AT_KB[0] + par * 4096: AT_KB[0] + par * 4096 + NDCH * ncols]
            if kt >= 2:
                P.wait("sync", "s_pe", mm_of[kt - 2])
            lw = P.dma("sync", "w%d" % par, wb.rearrange("p (c f) -> p c f", c=NDCH),
                       kva_T[:, kt * 128:kt * 128 + ncols].rearrange(
                           "(c p) f -> p c f", p=128))
            psq = PS[par]
            P.wait("tensor", "w%d" % par, lw)
            if kt >= 2:
                P.wait("tensor", "s_ve", kv_cv[kt - 2])
            for c in range(NDCH):
                lt = wb[:, c * ncols:(c + 1) * ncols]
                f, l = (c == 0), (c == NDCH - 1)
                P.op("tensor", lambda e, lt=lt, c=c, psq=psq, nr=ncols, f=f, l=l:
                     e.matmul(psq[0:nr, 0:512], lt, xnT[:, c * TOK:c * TOK + 512],
                              start=f, stop=l))
                mm = P.op("tensor", lambda e, lt=lt, c=c, psq=psq, nr=ncols, f=f, l=l:
                          e.matmul(psq[0:nr, 512:1024], lt,
                                   xnT[:, c * TOK + 512:(c + 1) * TOK],
                                   start=f, stop=l), inc=("s_pe", 1) if l else None)
            mm_of[kt] = mm
            P.wait("vector", "s_pe", mm)
            if kt < 4:
                kdst = AB[:, CK + kt * TOK: CK + (kt + 1) * TOK]
                cv = P.op("vector", lambda e, o=kdst, psq=psq:
                          e.tensor_copy(o, psq[:, :]), inc=("s_ve", 1))
                kv_cv[kt] = cv
                sq = AB[:, AT_R1:AT_R1 + TOK]
                P.wait("scalar", "s_ve", cv)
                svv = P.op("scalar", lambda e, o=sq, i=kdst:
                           e.activation(o, i, AF.Square), inc=("s_ac", 1))
                P.wait("tensor", "s_ac", svv)
                f, l = (kt == 0), (kt == 3)
                P.op("tensor", lambda e, i=sq, f=f, l=l:
                     e.matmul(ps3[0:1, 0:512], ones_b[:, :], i[:, 0:512],
                              start=f, stop=l))
                P.op("tensor", lambda e, i=sq, f=f, l=l:
                     e.matmul(ps3[0:1, 512:1024], ones_b[:, :], i[:, 512:1024],
                              start=f, stop=l), inc=("s_pe", 1))
            else:
                krb = AB[:, AT_R2:AT_R2 + TOK]
                cv = P.op("vector", lambda e, o=krb, psq=psq:
                          e.tensor_copy(o[0:64, :], psq[0:64, :]), inc=("s_ve", 1))
                kv_cv[kt] = cv
                P.wait("gpsimd", "s_ve", cv)
                krst = P.dma("gpsimd", "krw", kr_in, krb[0:64, :])
        P.wait("gpsimd", "krw", krst)
        ag_r = P.cc("AllGather", groups4, [kr_in.opt()], [kr_T.opt()])
        P.wait("scalar", "s_pe", P.cur("s_pe"))
        a1 = P.op("scalar", lambda e: e.activation(nrm, ps3[0:1, :], AF.Sqrt,
                                                   scale=1.0 / KVR), inc=("s_ac", 1))
        P.wait("vector", "s_ac", a1)
        P.op("vector", lambda e: e.tensor_scalar_add(nrm, nrm, EPS_RMS))
        v1 = P.op("vector", lambda e: e.reciprocal(nrm, nrm), inc=("s_ve", 1))
        P.wait("tensor", "s_ve", v1)
        P.psum_fence()
        bv = bcast_row(nrm, ps0, bc0)
        for kt in range(4):
            kd = AB[:, CK + kt * TOK: CK + (kt + 1) * TOK]
            P.wait("vector", "s_ve", bv)
            P.op("vector", lambda e, k=kd: e.tensor_mul(k, k, bc0[:, :]),
                 inc=("s_ve", 1) if kt == 3 else None)
        ck_done = P.cur("s_ve")
        if debug:
            P.wait("gpsimd", "s_ve", ck_done)
            for kt in range(4):
                P.dma("gpsimd", "krw", dbg["d_ckc"][kt * 128:(kt + 1) * 128, :],
                      AB[:, CK + kt * TOK: CK + (kt + 1) * TOK])

        krb = AB[:, AT_R2:AT_R2 + TOK]
        sqk_own = AB[:, AT_R1:AT_R1 + TOK]
        P.wait("scalar", "s_ve", kv_cv[4])
        kra_own = P.op("scalar", lambda e: e.activation(sqk_own[0:64, :], krb[0:64, :],
                                                        AF.Square), inc=("s_ac", 1))
        P.wait("tensor", "s_ac", kra_own)
        P.psum_fence()
        P.op("tensor", lambda e: e.matmul(ps2[0:1, 0:512], ones_b[0:64, :],
                                          sqk_own[0:64, 0:512], start=True, stop=True))
        krpe = P.op("tensor", lambda e:
                    e.matmul(ps2[0:1, 512:1024], ones_b[0:64, :],
                             sqk_own[0:64, 512:1024], start=True, stop=True),
                    inc=("s_pe", 1))
        P.wait("vector", "s_pe", krpe)
        kr2v = P.op("vector", lambda e: e.tensor_copy(bc1[0:1, :], ps2[0:1, :]),
                    inc=("s_ve", 1))

        P.wait("sync", "s_cc", kvbN_ag_v[0])
        P.wait("sync", "s_pe", P.cur("s_pe"))
        P.psum_fence()
        kwb = AB[:, AT_KB[0]:AT_KB[0] + 512]
        knv, kng, kn_mm = {}, {}, {}
        for h in range(NH):
            if h >= 1:
                P.wait("sync", "s_pe", kn_mm[h - 1])
            lw = P.dma("sync", "w0", kwb.rearrange("p (c f) -> p c f", c=4),
                       kvbN_T[:, h * ND:(h + 1) * ND].rearrange(
                           "(c p) f -> p c f", p=128))
            psk = PS[h % 2]
            P.wait("tensor", "w0", lw)
            P.wait("tensor", "s_ve", ck_done)
            if h >= 2:
                P.wait("tensor", "s_ve", knv[h - 2])
            for c in range(4):
                lt = kwb[:, c * 128:(c + 1) * 128]
                f, l = (c == 0), (c == 3)
                P.op("tensor", lambda e, lt=lt, c=c, psk=psk, f=f, l=l:
                     e.matmul(psk[:, 0:512], lt,
                              AB[:, CK + c * TOK: CK + c * TOK + 512], start=f, stop=l))
                mm = P.op("tensor", lambda e, lt=lt, c=c, psk=psk, f=f, l=l:
                          e.matmul(psk[:, 512:1024], lt,
                                   AB[:, CK + c * TOK + 512: CK + (c + 1) * TOK],
                                   start=f, stop=l), inc=("s_pe", 1) if l else None)
            kn_mm[h] = mm
            kout = AB[:, AT_ATB[h % 2]:AT_ATB[h % 2] + TOK]
            P.wait("vector", "s_pe", mm)
            if h >= 2:
                P.wait("vector", "kvw%d" % (h % 2), kng[h - 2])
            cv = P.op("vector", lambda e, o=kout, psk=psk:
                      e.tensor_copy(o, psk[:, :]), inc=("s_ve", 1))
            knv[h] = cv
            P.wait("gpsimd", "s_ve", cv)
            k_inh = k_in0 if h < 16 else k_in1
            kng[h] = P.dma("gpsimd", "kvw%d" % (h % 2),
                           k_inh[(h % 16) * ND:(h % 16 + 1) * ND, :], kout)
            if h == 15:
                P.wait("gpsimd", "kvw0", kng[14])
                P.wait("gpsimd", "kvw1", kng[15])
                ag_k0 = P.cc("AllGather", groups4, [k_in0.opt()], [k_T0.opt()])
            sqh = AB[:, 30976:30976 + TOK]
            P.wait("scalar", "s_ve", cv)
            if h >= 1:
                P.wait("scalar", "s_ve", km_ve[h - 1])
            sks = P.op("scalar", lambda e, o=sqh, i=kout:
                       e.activation(o, i, AF.Square), inc=("s_ac", 1))
            P.wait("tensor", "s_ac", sks)
            if h >= 1:
                P.wait("tensor", "s_ve", km_ve[h - 1])
            P.op("tensor", lambda e: e.matmul(ps2[0:1, 0:512], ones_b[:, :],
                                              sqh[:, 0:512], start=True, stop=True))
            kpe = P.op("tensor", lambda e:
                       e.matmul(ps2[0:1, 512:1024], ones_b[:, :],
                                sqh[:, 512:1024], start=True, stop=True),
                       inc=("s_pe", 1))
            P.wait("vector", "s_pe", kpe)
            P.op("vector", lambda e: e.tensor_add(vrow1[0:1, :], ps2[0:1, :],
                                                  bc1[0:1, :]))
            km_ve[h] = P.op("vector", lambda e, h=h:
                            e.reduce_max(vrow2[0:1, h:h + 1], vrow1[0:1, :],
                                         axis=AX), inc=("s_ve", 1))
            P.drain_bg(1)
        P.wait("gpsimd", "s_ve", km_ve[NH - 1])
        kmst = P.dma("gpsimd", "krw", km_in, vrow2[0:1, 0:NH])
        P.wait("gpsimd", "krw", kmst)
        ag_m = P.cc("AllGather", groups4, [km_in.opt()], [km_T.opt()])
        P.wait("gpsimd", "kvw0", kng[30])
        P.wait("gpsimd", "kvw1", kng[31])
        ag_k1 = P.cc("AllGather", groups4, [k_in1.opt()], [k_T1.opt()])

        P.wait("sync", "s_cc", kvbV_ag_v[0])
        P.wait("sync", "s_pe", P.cur("s_pe"))
        P.psum_fence()
        spots = [AT_KB[0], AT_KB[0] + 4096, AT_SQK, AT_KRA]
        vld = []
        for c in range(4):
            vld.append(P.dma("sync", "tab", AB[:, spots[c]:spots[c] + 4096],
                             kvbV_T[c * 128:(c + 1) * 128, :]))
        vev, vtg = {}, {}
        for tt in range(8):
            P.wait("tensor", "tab", vld[3])
            if tt >= 1:
                P.wait("tensor", "s_ve", vev[tt - 1])
            for c in range(4):
                lt = AB[:, CK + c * TOK + tt * 128: CK + c * TOK + (tt + 1) * 128]
                mv_ = AB[:, spots[c]:spots[c] + 4096]
                for vb in range(4):
                    f, l = (c == 0), (c == 3)
                    P.op("tensor", lambda e, lt=lt, mv=mv_, vb=vb, psx=PS[vb], f=f, l=l:
                         e.matmul(psx[:, 0:512], lt,
                                  mv[:, vb * 1024:vb * 1024 + 512], start=f, stop=l))
                    mm = P.op("tensor", lambda e, lt=lt, mv=mv_, vb=vb, psx=PS[vb],
                              f=f, l=l:
                              e.matmul(psx[:, 512:1024], lt,
                                       mv[:, vb * 1024 + 512:(vb + 1) * 1024],
                                       start=f, stop=l),
                              inc=("s_pe", 1) if (l and vb == 3) else None)
            vout = AB[:, AT_VB[1]:AT_VB[1] + 4096]
            P.wait("vector", "s_pe", mm)
            if tt >= 1:
                P.wait("vector", "vst", vtg[tt - 1])
            cvl = None
            for vb in range(4):
                cvl = P.op("vector", lambda e, o=vout[:, vb * 1024:(vb + 1) * 1024],
                           psx=PS[vb]: e.tensor_copy(o, psx[:, :]), inc=("s_ve", 1))
            vev[tt] = cvl
            P.wait("gpsimd", "s_ve", cvl)
            P.dma("gpsimd", "vst", v_in0[tt * 128:(tt + 1) * 128, :],
                  vout[:, 0:NH * VD // 2])
            vtg[tt] = P.dma("gpsimd", "vst", v_in1[tt * 128:(tt + 1) * 128, :],
                            vout[:, NH * VD // 2:])
            P.wait("tensor", "vst", vtg[tt])
            P.drain_bg(1)
        P.wait("gpsimd", "vst", vtg[7])
        ag_v0 = P.cc("AllGather", groups4, [v_in0.opt()], [v_T0.opt()])
        ag_v1 = P.cc("AllGather", groups4, [v_in1.opt()], [v_T1.opt()])
        if debug:
            P.wait("gpsimd", "s_cc", ag_v1)
            P.dma("gpsimd", "krw", dbg["d_key"][0:4 * NH * ND // 2, :], k_T0)
            P.dma("gpsimd", "krw", dbg["d_key"][4 * NH * ND // 2:, :], k_T1)
            P.dma("gpsimd", "krw", dbg["d_vv"][:, 0:NH * VD // 2], v_T0)
            P.dma("gpsimd", "krw", dbg["d_vv"][:, NH * VD // 2:], v_T1)

        P.wait("sync", "s_cc", qa_ag_v[0])
        P.wait("sync", "s_pe", P.cur("s_pe"))
        P.psum_fence()
        mm_of, cv_of = {}, {}
        for qt in range(NQRCH):
            par = qt % 2
            wb = AB[:, AT_KB[0] + par * 4096: AT_KB[0] + par * 4096 + 4096]
            if qt >= 2:
                P.wait("sync", "s_pe", mm_of[qt - 2])
            lw = P.dma("sync", "w%d" % par, wb.rearrange("p (c f) -> p c f", c=NDCH),
                       qa_T[:, qt * 128:(qt + 1) * 128].rearrange(
                           "(c p) f -> p c f", p=128))
            psq = PS[par]
            P.wait("tensor", "w%d" % par, lw)
            P.wait("tensor", "s_ve", xn_ready)
            if qt >= 2:
                P.wait("tensor", "s_ve", cv_of[qt - 2])
            for c in range(NDCH):
                lt = wb[:, c * 128:(c + 1) * 128]
                f, l = (c == 0), (c == NDCH - 1)
                P.op("tensor", lambda e, lt=lt, c=c, psq=psq, f=f, l=l:
                     e.matmul(psq[:, 0:512], lt, xnT[:, c * TOK:c * TOK + 512],
                              start=f, stop=l))
                mm = P.op("tensor", lambda e, lt=lt, c=c, psq=psq, f=f, l=l:
                          e.matmul(psq[:, 512:1024], lt,
                                   xnT[:, c * TOK + 512:(c + 1) * TOK],
                                   start=f, stop=l), inc=("s_pe", 1) if l else None)
            mm_of[qt] = mm
            qdst = AB[:, AT_QC + qt * TOK: AT_QC + (qt + 1) * TOK]
            P.wait("vector", "s_pe", mm)
            cv = P.op("vector", lambda e, o=qdst, psq=psq:
                      e.tensor_copy(o, psq[:, :]), inc=("s_ve", 1))
            cv_of[qt] = cv
            sq = AB[:, AT_R1:AT_R1 + TOK]
            P.wait("scalar", "s_ve", cv)
            svv = P.op("scalar", lambda e, o=sq, i=qdst:
                       e.activation(o, i, AF.Square), inc=("s_ac", 1))
            P.wait("tensor", "s_ac", svv)
            f, l = (qt == 0), (qt == NQRCH - 1)
            P.op("tensor", lambda e, i=sq, f=f, l=l:
                 e.matmul(ps2[0:1, 0:512], ones_b[:, :], i[:, 0:512], start=f, stop=l))
            P.op("tensor", lambda e, i=sq, f=f, l=l:
                 e.matmul(ps2[0:1, 512:1024], ones_b[:, :], i[:, 512:1024],
                          start=f, stop=l), inc=("s_pe", 1))
        P.wait("scalar", "s_pe", P.cur("s_pe"))
        nrm = bc0[0:1, :]
        a1 = P.op("scalar", lambda e: e.activation(nrm, ps2[0:1, :], AF.Sqrt,
                                                   scale=1.0 / QR), inc=("s_ac", 1))
        P.wait("vector", "s_ac", a1)
        P.op("vector", lambda e: e.tensor_scalar_add(nrm, nrm, EPS_RMS))
        v1 = P.op("vector", lambda e: e.reciprocal(nrm, nrm), inc=("s_ve", 1))
        P.wait("tensor", "s_ve", v1)
        P.psum_fence()
        bv = bcast_row(nrm, ps0, bc0)
        for qt in range(NQRCH):
            qd = AB[:, AT_QC + qt * TOK: AT_QC + (qt + 1) * TOK]
            P.wait("vector", "s_ve", bv)
            P.op("vector", lambda e, q=qd: e.tensor_mul(q, q, bc0[:, :]),
                 inc=("s_ve", 1) if qt == NQRCH - 1 else None)
        qc_done = P.cur("s_ve")

        if debug:
            P.wait("gpsimd", "s_ve", qc_done)
            for qt in range(NQRCH):
                P.dma("gpsimd", "krw", dbg["d_qc"][qt * 128:(qt + 1) * 128, :],
                      AB[:, AT_QC + qt * TOK: AT_QC + (qt + 1) * TOK])

        P.sync_fence()
        P.dma("sync", "tab", AB[0:64, AT_KB[0]:AT_KB[0] + S], cosk)
        csk_ld = P.dma("sync", "tab", AB[0:64, AT_KB[1]:AT_KB[1] + S], sink)
        P.dma("sync", "tab", AB[0:64, AT_CSQ:AT_CSQ + TOK], cosq)
        csq_ld = P.dma("sync", "tab", AB[0:64, AT_SSQ:AT_SSQ + TOK], sinq)
        P.wait("sync", "s_cc", ag_r)
        for r in range(4):
            P.dma("sync", "tab", AB[0:64, AT_KRA + r * TOK: AT_KRA + (r + 1) * TOK],
                  kr_T[r * RD:(r + 1) * RD, :])
        kr_raw_done = P.cur("tab")
        P.wait("sync", "tab", kr_raw_done)
        P.dma("sync", "qsw", AB[0:32, AT_SQK:AT_SQK + S], AB[32:64, AT_KRA:AT_KRA + S])
        krsw_ld = P.dma("sync", "qsw", AB[32:64, AT_SQK:AT_SQK + S],
                        AB[0:32, AT_KRA:AT_KRA + S])
        P.wait("vector", "qsw", krsw_ld)
        P.wait("vector", "tab", kr_raw_done)
        P.wait("vector", "s_pe", P.cur("s_pe"))
        KRG = AB[0:64, AT_KRA:AT_KRA + S]
        KRSW = AB[0:64, AT_SQK:AT_SQK + S]
        v1 = P.op("vector", lambda e: e.tensor_mul(KRG, KRG,
                                                   AB[0:64, AT_KB[0]:AT_KB[0] + S]),
                  inc=("s_ve", 1))
        v2 = P.op("vector", lambda e: e.tensor_mul(KRSW, KRSW,
                                                   AB[0:64, AT_KB[1]:AT_KB[1] + S]),
                  inc=("s_ve", 1))
        P.wait("vector", "s_ve", v2)
        v3 = P.op("vector", lambda e: e.tensor_add(KRG, KRG, KRSW), inc=("s_ve", 1))
        v4 = P.op("vector", lambda e: e.memset(AB[64:65, AT_KRA:AT_KRA + S], 1.0),
                  inc=("s_ve", 1))
        kr_ready = v4
        kr_ready = v4
        P.wait("sync", "s_cc", ag_m)
        P.dma("sync", "tab", vrow2[0:1, 0:NH], km_T[0:1, :])
        P.dma("sync", "tab", vrow1[0:1, 0:NH], km_T[1:2, :])
        P.dma("sync", "tab", vrow1[0:1, NH:2 * NH], km_T[2:3, :])
        km_rows = P.dma("sync", "tab", vrow1[0:1, 2 * NH:3 * NH], km_T[3:4, :])
        P.wait("vector", "tab", km_rows)
        P.op("vector", lambda e: e.tensor_max(vrow2[0:1, 0:NH], vrow2[0:1, 0:NH],
                                              vrow1[0:1, 0:NH]))
        P.op("vector", lambda e: e.tensor_max(vrow2[0:1, 0:NH], vrow2[0:1, 0:NH],
                                              vrow1[0:1, NH:2 * NH]))
        kmx = P.op("vector", lambda e: e.tensor_max(vrow2[0:1, 0:NH], vrow2[0:1, 0:NH],
                                                    vrow1[0:1, 2 * NH:3 * NH]),
                   inc=("s_ve", 1))

        out_allT = xnT
        head_entry_pe = P.cur("s_pe")
        AT_QN2 = 47360
        AT_QA2 = 48384
        QNs = [AT_QN, AT_QN2]
        QAs = [AT_QA, AT_QA2]
        HB = NH * ND // 2
        st = {}

        def keyb_of(h):
            return AB[:, AT_KB[h % 2]:AT_KB[h % 2] + 4096]

        def vbuf_of(h):
            return AB[:, AT_VB[h % 2]:AT_VB[h % 2] + 4096]

        def qwb_of(h):
            base = AT_QW if h % 2 == 0 else AT_SQK
            return AB[:, base:base + NQRCH * QK]

        def emit_loads(h):
            parb = h % 2
            k_Th, v_Th = (k_T0, v_T0) if h < 16 else (k_T1, v_T1)
            hh = h % 16
            st.setdefault(h, {})
            P.wait("sync", "s_cc", max(ag_k0, ag_v0) if h < 16
                   else max(ag_k1, ag_v1))
            P.wait("sync", "s_pe",
                   st[h - 2]["av_last"] if h >= 2 else head_entry_pe)
            if h < 2:
                P.wait("sync", "s_ve", kr_ready)
            for r in range(4):
                P.dma("sync", "kv%d" % parb, keyb_of(h)[:, r * TOK:(r + 1) * TOK],
                      k_Th[r * HB + hh * ND: r * HB + (hh + 1) * ND, :])
            st[h]["kv_ld"] = P.dma(
                "sync", "kv%d" % parb,
                vbuf_of(h).rearrange("p (c f) -> p c f", c=32),
                v_Th[:, hh * VD:(hh + 1) * VD].rearrange("(c p) f -> p c f", p=128))
            P.wait("sync", "s_cc", qb_ag_v[0])
            st[h]["lqw"] = P.dma(
                "sync", "qw", qwb_of(h).rearrange("p (c f) -> p c f", c=NQRCH),
                qb_T[:, h * QK:(h + 1) * QK].rearrange("(c p) f -> p c f", p=128))

        def emit_qbn(h):
            qwb = qwb_of(h)
            P.wait("tensor", "qw", st[h]["lqw"])
            P.wait("tensor", "s_ve", qc_done)
            if h >= 1:
                P.wait("tensor", "s_ac", st[h - 1]["exp"][30])
            for c in range(NQRCH):
                lt = qwb[:, c * QK:c * QK + 128]
                f, l = (c == 0), (c == NQRCH - 1)
                P.op("tensor", lambda e, lt=lt, c=c, f=f, l=l:
                     e.matmul(ps0[:, 0:512], lt,
                              AB[:, AT_QC + c * TOK:AT_QC + c * TOK + 512],
                              start=f, stop=l))
                mm = P.op("tensor", lambda e, lt=lt, c=c, f=f, l=l:
                          e.matmul(ps0[:, 512:1024], lt,
                                   AB[:, AT_QC + c * TOK + 512:AT_QC + (c + 1) * TOK],
                                   start=f, stop=l), inc=("s_pe", 1) if l else None)
            st[h]["qbn_pe"] = mm

        def emit_qbr(h):
            qwb = qwb_of(h)
            if h >= 1:
                P.wait("tensor", "s_ac", st[h - 1]["exp"][31])
            for c in range(NQRCH):
                lt = qwb[:, c * QK + 128:(c + 1) * QK]
                f, l = (c == 0), (c == NQRCH - 1)
                P.op("tensor", lambda e, lt=lt, c=c, f=f, l=l:
                     e.matmul(ps1[0:64, 0:512], lt,
                              AB[:, AT_QC + c * TOK:AT_QC + c * TOK + 512],
                              start=f, stop=l))
                mm = P.op("tensor", lambda e, lt=lt, c=c, f=f, l=l:
                          e.matmul(ps1[0:64, 512:1024], lt,
                                   AB[:, AT_QC + c * TOK + 512:AT_QC + (c + 1) * TOK],
                                   start=f, stop=l), inc=("s_pe", 1) if l else None)
            st[h]["qbr_pe"] = mm

        def emit_preA3(h):
            qn_sb = AB[:, QNs[h % 2]:QNs[h % 2] + TOK]
            QRA = AB[0:64, QAs[h % 2]:QAs[h % 2] + TOK]
            QRR = AB[0:64, AT_R1:AT_R1 + TOK]
            QSW = AB[0:64, AT_R2:AT_R2 + TOK]
            P.wait("vector", "s_pe", st[h]["qbn_pe"])
            if h >= 2:
                P.wait("vector", "s_pe", st[h - 2]["sc_last"])
            qnv = P.op("vector", lambda e, o=qn_sb: e.tensor_copy(o, ps0[:, :]),
                       inc=("s_ve", 1))
            st[h]["qnv"] = qnv
            P.wait("vector", "s_pe", st[h]["qbr_pe"])
            if h >= 1:
                P.wait("vector", "s_pe", st[h - 1]["qsq_pe"])
            P.wait("vector", "tab", csq_ld)
            rv1 = P.op("vector", lambda e: e.tensor_copy(QRR, ps1[0:64, :]),
                       inc=("s_ve", 1))
            st[h]["rv1"] = rv1
            P.wait("sync", "s_ve", rv1)
            if h >= 1:
                P.wait("sync", "qsw", st[h - 1]["swap"])
            P.dma("sync", "qsw", AB[0:32, AT_R2:AT_R2 + TOK],
                  AB[32:64, AT_R1:AT_R1 + TOK])
            d2 = P.dma("sync", "qsw", AB[32:64, AT_R2:AT_R2 + TOK],
                       AB[0:32, AT_R1:AT_R1 + TOK])
            st[h]["swap"] = d2
            P.wait("vector", "qsw", d2)
            P.op("vector", lambda e: e.tensor_mul(QSW, QSW,
                                                  AB[0:64, AT_SSQ:AT_SSQ + TOK]),
                 inc=("s_ve", 1))
            rv3 = P.op("vector", lambda e: e.tensor_mul(QRR, QRR,
                                                        AB[0:64, AT_CSQ:AT_CSQ + TOK]),
                       inc=("s_ve", 1))
            P.wait("vector", "s_ve", rv3)
            rv4 = P.op("vector", lambda e: e.tensor_add(QRA, QRR, QSW),
                       inc=("s_ve", 1))
            sqq = AB[:, AT_R1:AT_R1 + TOK]
            sqr = AB[0:64, AT_R2:AT_R2 + TOK]
            P.wait("scalar", "s_ve", rv4)
            P.op("scalar", lambda e, o=sqq, i=qn_sb: e.activation(o, i, AF.Square),
                 inc=("s_ac", 1))
            st[h]["sq_ac"] = P.op("scalar", lambda e, o=sqr, i=QRA:
                                  e.activation(o, i, AF.Square), inc=("s_ac", 1))

        def emit_partB(h):
            sqq = AB[:, AT_R1:AT_R1 + TOK]
            sqr = AB[0:64, AT_R2:AT_R2 + TOK]
            P.wait("tensor", "s_ac", st[h]["sq_ac"])
            P.wait("tensor", "s_ve", st[h]["qnv"])
            P.op("tensor", lambda e: e.matmul(ps0[0:1, 0:512], ones_b[:, :],
                                              sqq[:, 0:512], start=True, stop=False))
            P.op("tensor", lambda e: e.matmul(ps0[0:1, 512:1024], ones_b[:, :],
                                              sqq[:, 512:1024], start=True, stop=False))
            P.op("tensor", lambda e: e.matmul(ps0[0:1, 0:512], ones_b[0:64, :],
                                              sqr[:, 0:512], start=False, stop=True))
            qsq_pe = P.op("tensor", lambda e:
                          e.matmul(ps0[0:1, 512:1024], ones_b[0:64, :],
                                   sqr[:, 512:1024], start=False, stop=True),
                          inc=("s_pe", 1))
            st[h]["qsq_pe"] = qsq_pe
            P.wait("scalar", "s_pe", qsq_pe)
            P.wait("scalar", "s_ve", kmx)
            if h >= 1:
                P.wait("scalar", "aug", st[h - 1]["augd"])
            qn_a = P.op("scalar", lambda e, h=h:
                        e.activation(vrow1[0:1, :], ps0[0:1, :], AF.Sqrt,
                                     scale=vrow2[0:1, h:h + 1]),
                        inc=("s_ac", 1))
            st[h]["qn_a"] = qn_a
            P.wait("vector", "s_ac", qn_a)
            augv = P.op("vector", lambda e:
                        e.tensor_scalar(vrow1[0:1, :], vrow1[0:1, :],
                                        -1.0, OFFS, ALU.mult, ALU.add),
                        inc=("s_ve", 1))
            P.wait("gpsimd", "s_ve", augv)
            if h >= 2:
                P.wait("gpsimd", "s_pe", st[h - 2]["sc_last"])
            st[h]["augd"] = P.dma("gpsimd", "aug",
                                  AB[64:65, QAs[h % 2]:QAs[h % 2] + TOK],
                                  vrow1[0:1, :])

        def emit_pipe(h):
            parb = h % 2
            keyb = keyb_of(h)
            vbuf = vbuf_of(h)
            qn_sb = AB[:, QNs[h % 2]:QNs[h % 2] + TOK]
            qac = QAs[h % 2]
            kt_pe, exp_of, av_pe = {}, {}, {}
            st[h]["exp"] = exp_of

            def emit_score(kt):
                pss = PS[kt % 2]
                lkn = keyb[:, kt * 128:(kt + 1) * 128]
                lkr = AB[0:65, AT_KRA + kt * 128: AT_KRA + (kt + 1) * 128]
                if kt == 0:
                    P.wait("tensor", "aug", st[h]["augd"])
                    P.wait("tensor", "s_ve", kr_ready)
                    P.wait("tensor", "kv%d" % parb, st[h]["kv_ld"])
                    P.wait("tensor", "s_ac", st[h]["qn_a"])
                if kt == 1 and h >= 1:
                    P.wait("tensor", "s_ve", st[h - 1]["bc_cv"])
                if kt >= 2:
                    P.wait("tensor", "s_ac", exp_of[kt - 2])
                P.op("tensor", lambda e, lkn=lkn, pss=pss:
                     e.matmul(pss[:, 0:512], lkn, qn_sb[:, 0:512],
                              start=True, stop=False))
                P.op("tensor", lambda e, lkn=lkn, pss=pss:
                     e.matmul(pss[:, 512:1024], lkn, qn_sb[:, 512:1024],
                              start=True, stop=False))
                P.op("tensor", lambda e, lkr=lkr, pss=pss:
                     e.matmul(pss[:, 0:512], lkr, AB[0:65, qac:qac + 512],
                              start=False, stop=True))
                return P.op("tensor", lambda e, lkr=lkr, pss=pss:
                            e.matmul(pss[:, 512:1024], lkr,
                                     AB[0:65, qac + 512:qac + TOK],
                                     start=False, stop=True), inc=("s_pe", 1))

            def emit_exp(kt):
                pss = PS[kt % 2]
                atb = AB[:, AT_ATB[kt % 2]:AT_ATB[kt % 2] + TOK]
                P.wait("scalar", "s_pe", kt_pe[kt])
                if kt >= 2:
                    P.wait("scalar", "s_pe", av_pe[kt - 2])
                return P.op("scalar", lambda e, o=atb, pss=pss:
                            e.activation(o, pss[:, :], AF.Exp), inc=("s_ac", 1))

            def emit_av(kt):
                atb = AB[:, AT_ATB[kt % 2]:AT_ATB[kt % 2] + TOK]
                lv = vbuf[:, kt * 128:(kt + 1) * 128]
                f, l = (kt == 0), (kt == 31)
                P.wait("tensor", "s_ac", exp_of[kt])
                if kt == 0 and h >= 1:
                    P.wait("tensor", "s_ve", st[h - 1]["hv"])
                    P.wait("tensor", "s_ve", st[h - 1]["recip"])
                P.op("tensor", lambda e, lv=lv, atb=atb, f=f, l=l:
                     e.matmul(ps2[:, 0:512], lv, atb[:, 0:512], start=f, stop=l))
                P.op("tensor", lambda e, lv=lv, atb=atb, f=f, l=l:
                     e.matmul(ps2[:, 512:1024], lv, atb[:, 512:1024], start=f, stop=l))
                P.op("tensor", lambda e, atb=atb, f=f, l=l:
                     e.matmul(ps3[0:1, 0:512], ones_b[:, :], atb[:, 0:512],
                              start=f, stop=l))
                return P.op("tensor", lambda e, atb=atb, f=f, l=l:
                            e.matmul(ps3[0:1, 512:1024], ones_b[:, :],
                                     atb[:, 512:1024], start=f, stop=l),
                            inc=("s_pe", 1))

            if h + 1 < NH:
                emit_loads(h + 1)
            kt_pe[0] = emit_score(0)
            kt_pe[1] = emit_score(1)
            exp_of[0] = emit_exp(0)
            for kt in range(32):
                if kt + 2 < 32:
                    kt_pe[kt + 2] = emit_score(kt + 2)
                if kt + 1 < 32:
                    exp_of[kt + 1] = emit_exp(kt + 1)
                if kt == 29 and h + 1 < NH:
                    emit_qbn(h + 1)
                if kt == 30 and h + 1 < NH:
                    emit_qbr(h + 1)
                av_pe[kt] = emit_av(kt)
            st[h]["sc_last"] = kt_pe[31]
            st[h]["av_last"] = av_pe[31]
            if h + 1 < NH:
                emit_preA3(h + 1)
            P.wait("vector", "s_pe", av_pe[31])
            rv = P.op("vector", lambda e: e.reciprocal(bc1[0:1, :], ps3[0:1, :]),
                      inc=("s_ve", 1))
            st[h]["recip"] = rv
            P.wait("tensor", "s_ve", rv)
            if h + 1 < NH:
                P.wait("tensor", "s_ve", st[h + 1]["rv1"])
            else:
                P.wait("tensor", "s_ac", exp_of[31])
            bvh = bcast_row(bc1[0:1, :], ps1, bc1)
            st[h]["bc_cv"] = bvh
            P.wait("vector", "s_ve", bvh)
            st[h]["hv"] = P.op("vector", lambda e, h=h:
                               e.tensor_tensor(out_allT[:, h * TOK:(h + 1) * TOK],
                                               ps2[:, :], bc1[:, :], ALU.mult),
                               inc=("s_ve", 1))

        emit_loads(0)
        P.psum_fence()
        emit_qbn(0)
        emit_qbr(0)
        emit_preA3(0)
        for h in range(NH):
            emit_partB(h)
            emit_pipe(h)

        if debug:
            P.wait("gpsimd", "s_ve", st[NH - 1]["hv"])
            for c in range(NDCH):
                P.dma("gpsimd", "krw", dbg["d_oT"][c * 128:(c + 1) * 128, :],
                      out_allT[:, c * TOK:(c + 1) * TOK])

        P.wait("sync", "s_cc", wo_ag_v[0])
        P.wait("sync", "s_pe", P.cur("s_pe"))
        P.wait("sync", "s_ve", P.cur("s_ve"))
        P.wait_tot("sync", "xw0", "xw1")
        P.psum_fence()
        oall_done = st[NH - 1]["hv"]
        wo_mm, wo_ve, gpev2, xbv2 = {}, {}, {}, {}
        for dt in range(NDCH):
            par = dt % 2
            swb = AB[:, AT_VB[par]:AT_VB[par] + 4096]
            if dt >= 2:
                P.wait("sync", "s_pe", wo_mm[dt - 2])
            lw = P.dma("sync", "w%d" % par, swb.rearrange("p (c f) -> p c f", c=NDCH),
                       wo_T[:, dt * 128:(dt + 1) * 128].rearrange(
                           "(c p) f -> p c f", p=128))
            psd = PS[par]
            P.wait("tensor", "w%d" % par, lw)
            P.wait("tensor", "s_ve", oall_done)
            if dt >= 2:
                P.wait("tensor", "s_ve", wo_ve[dt - 2])
            for c in range(NDCH):
                lt = swb[:, c * 128:(c + 1) * 128]
                f, l = (c == 0), (c == NDCH - 1)
                P.op("tensor", lambda e, lt=lt, c=c, psd=psd, f=f, l=l:
                     e.matmul(psd[:, 0:512], lt,
                              out_allT[:, c * TOK:c * TOK + 512], start=f, stop=l))
                mm = P.op("tensor", lambda e, lt=lt, c=c, psd=psd, f=f, l=l:
                          e.matmul(psd[:, 512:1024], lt,
                                   out_allT[:, c * TOK + 512:(c + 1) * TOK],
                                   start=f, stop=l), inc=("s_pe", 1) if l else None)
            wo_mm[dt] = mm
            fb = AFX[:, par * TOK:(par + 1) * TOK]
            if dt >= 2:
                P.wait("sync", "xw%d" % par, gpev2[dt - 2])
            P.wait("sync", "s_ve", xbv2.get(dt - 1, 0))
            ld2 = P.dma("sync", "xfb", xfb[:, :], xres1[dt * 128:(dt + 1) * 128, :])
            P.wait("vector", "s_pe", mm)
            P.wait("vector", "xfb", ld2)
            if dt >= 2:
                P.wait("vector", "xw%d" % par, gpev2[dt - 2])
            vv = P.op("vector", lambda e, o=fb, psd=psd:
                      e.tensor_add(o, psd[:, :], xfb[:, :]), inc=("s_ve", 1))
            xbv2[dt] = vv
            wo_ve[dt] = vv
            P.wait("gpsimd", "s_ve", vv)
            gpev2[dt] = P.dma("gpsimd", "xw%d" % par,
                              xres2[dt * 128:(dt + 1) * 128, :], fb)
            stats_tile(fb, dt)
        xn_ready = ln_apply(xres2)
        if debug:
            P.wait("gpsimd", "s_ve", xn_ready)
            for c in range(NDCH):
                P.dma("gpsimd", "krw", dbg["d_x2"][c * 128:(c + 1) * 128, :],
                      xres2[c * 128:(c + 1) * 128, :])

        emit_ffn_A(first_pass=False, xn_ready=xn_ready)
        emit_ffn_B(xbase_d=xres2, xdst_d=None, final=True)
        P.wait_tot("sync", "xw0", "xw1")
        P.wait_tot("tensor", "xw0", "xw1")

        @block.sync
        def _(e):
            for fn in P.ops["sync"]:
                fn(e)

        @block.tensor
        def _(e):
            for fn in P.ops["tensor"]:
                fn(e)

        @block.vector
        def _(e):
            for fn in P.ops["vector"]:
                fn(e)

        @block.scalar
        def _(e):
            for fn in P.ops["scalar"]:
                fn(e)

        @block.gpsimd
        def _(e):
            for fn in P.ops["gpsimd"]:
                fn(e)

    return nc, dbg


def _rope_tables():
    inv = 1.0 / (10000.0 ** (np.arange(0, RD, 2, dtype=np.float64) / RD))
    t = np.arange(S, dtype=np.float64)
    fr = np.outer(t, inv)
    emb = np.concatenate([fr, fr], axis=1)
    cos = np.ascontiguousarray(np.cos(emb).T).astype(ml_dtypes.bfloat16)
    sin = np.sin(emb).T
    sinS = sin.copy()
    sinS[:RD // 2] *= -1.0
    sinS = np.ascontiguousarray(sinS).astype(ml_dtypes.bfloat16)
    return cos, sinS


def prepare_inputs(inputs):
    x = np.asarray(inputs["L0_attn_hidden_states"])
    w_qa = np.asarray(inputs["L0_attn_w_qa"])
    g_qa = np.asarray(inputs["L0_attn_g_qa"]).reshape(QR, 1)
    w_qb = np.asarray(inputs["L0_attn_w_qb"])
    w_kva = np.asarray(inputs["L0_attn_w_kva"])
    g_kva = np.asarray(inputs["L0_attn_g_kva"]).reshape(KVR, 1)
    w_kvb = np.asarray(inputs["L0_attn_w_kvb"])
    w_o = np.asarray(inputs["L0_attn_w_o"])
    gate_w = np.asarray(inputs["L0_ffn_gate_w"])
    gate_s = np.asarray(inputs["L0_ffn_gate_s"]).reshape(DFF, 1)
    up_w = np.asarray(inputs["L0_ffn_up_w"])
    up_s = np.asarray(inputs["L0_ffn_up_s"]).reshape(DFF, 1)
    down_w = np.asarray(inputs["L0_ffn_down_w"])
    down_s = np.asarray(inputs["L0_ffn_down_s"]).reshape(D, 1)

    kvb4 = w_kvb.reshape(KVR, NH, ND + VD)
    kvbN = np.ascontiguousarray(kvb4[:, :, :ND].reshape(KVR, NH * ND))
    kvbV = np.ascontiguousarray(kvb4[:, :, ND:].reshape(KVR, NH * VD))
    down_pad = np.zeros((D, DFFP), dtype=np.int32)
    down_pad[:, :DFF] = down_w
    cos, sinS = _rope_tables()

    in_maps = []
    for c in range(NCORE):
        b = c // 4
        r0 = (c % 4) * TOK
        r = c
        m = {
            "xT_in": np.ascontiguousarray(x[b, r0:r0 + TOK, :].T).astype(np.float32),
            "gate_sl": np.ascontiguousarray(gate_w[:, r * DPC:(r + 1) * DPC]),
            "up_sl": np.ascontiguousarray(up_w[:, r * DPC:(r + 1) * DPC]),
            "down_sl": np.ascontiguousarray(down_pad[:, r * DFPC:(r + 1) * DFPC]),
            "gate_s": np.ascontiguousarray(gate_s.reshape(NFT, 128).T).astype(np.float32),
            "up_s": np.ascontiguousarray(up_s.reshape(NFT, 128).T).astype(np.float32),
            "down_s": np.ascontiguousarray(down_s.reshape(NDCH, 128).T).astype(np.float32),
            "qa_sl": np.ascontiguousarray(w_qa[r * DPC:(r + 1) * DPC, :]).astype(np.float32),
            "qb_sl": np.ascontiguousarray(w_qb[r * QRPC:(r + 1) * QRPC, :]).astype(np.float32),
            "g_qa_sl": np.ascontiguousarray(g_qa[r * QRPC:(r + 1) * QRPC, :]).astype(np.float32),
            "kva_sl": np.ascontiguousarray(w_kva[r * DPC:(r + 1) * DPC, :]).astype(np.float32),
            "kvbN_sl": np.ascontiguousarray(kvbN[r * KVRPC:(r + 1) * KVRPC, :]).astype(np.float32),
            "kvbV_sl": np.ascontiguousarray(kvbV[r * KVRPC:(r + 1) * KVRPC, :]).astype(np.float32),
            "g_kva_sl": np.ascontiguousarray(g_kva[r * KVRPC:(r + 1) * KVRPC, :]).astype(np.float32),
            "wo_sl": np.ascontiguousarray(w_o[r * DPC:(r + 1) * DPC, :]).astype(np.float32),
            "cosq": np.ascontiguousarray(cos[:, r0:r0 + TOK]),
            "sinq": np.ascontiguousarray(sinS[:, r0:r0 + TOK]),
            "cosk": cos,
            "sink": sinS,
        }
        in_maps.append(m)
    return in_maps


def kernel(**inputs):
    in_maps = prepare_inputs(inputs)
    nc, _ = build(debug=False)
    res = run_bass_kernel_spmd(nc, in_maps, core_ids=list(range(NCORE)))
    out = np.empty((B, S, D), dtype=np.float32)
    for c in range(NCORE):
        b = c // 4
        r0 = (c % 4) * TOK
        out[b, r0:r0 + TOK, :] = res.results[c]["out_xT"].T
    return out


# version 15 (final)
# speedup vs baseline: 1.0511x; 1.0076x over previous
import numpy as np
import ml_dtypes
import concourse.bass as bass
import concourse.mybir as mybir
from concourse.bass_utils import run_bass_kernel_spmd

F32 = mybir.dt.float32
BF16 = mybir.dt.bfloat16
I32 = mybir.dt.int32
AF = mybir.ActivationFunctionType
ALU = mybir.AluOpType
AX = mybir.AxisListType.X

B, S, D = 2, 4096, 4096
NCORE = 8
TOK = 1024
NDCH = 32
DFF = 11008
NFT = 86
NH, QR, KVR, RD, ND, VD = 32, 1536, 512, 64, 128, 128
QK = RD + ND
NQRCH = QR // 128
EPS_LN = 1e-5
EPS_RMS = 1e-6
OFFS = 60.0
H1T = 29
DSUB = [(0, 29), (29, 58), (58, 86)]

DPC = D // NCORE
QRPC = QR // NCORE
KVRPC = KVR // NCORE
DFFP = 11264
DFPC = DFFP // NCORE
GU_RANGES = [(0, 2816), (2816, 5632), (5632, 8448), (8448, 11008)]
CCH = 512

ENGS = ("sync", "tensor", "vector", "scalar", "gpsimd")


class Plan:
    def __init__(self):
        self.ops = {e: [] for e in ENGS}
        self.ctr = {}
        self.seen = {e: {} for e in ENGS}
        self.sems = {}
        self.bg = []

    def emit(self, eng, fn):
        self.ops[eng].append(fn)

    def bump(self, name, amt):
        self.ctr[name] = self.ctr.get(name, 0) + amt
        return self.ctr[name]

    def cur(self, name):
        return self.ctr.get(name, 0)

    def wait(self, eng, name, val):
        if val is None or val <= 0:
            return
        if self.seen[eng].get(name, 0) >= val:
            return
        self.seen[eng][name] = val
        sem = self.sems[name]
        self.emit(eng, lambda e, sem=sem, val=val: e.wait_ge(sem, val))

    ENG_SEM = {"vector": "s_ve", "scalar": "s_ac", "gpsimd": "s_gpc"}

    def op(self, eng, fn, inc=None):
        if inc is None and eng in self.ENG_SEM:
            name = self.ENG_SEM[eng]
            self.wait(eng, name, self.cur(name))
            inc = (name, 1)
        elif inc is not None and eng in self.ENG_SEM and inc[0] == self.ENG_SEM[eng]:
            self.wait(eng, inc[0], self.cur(inc[0]))
        if inc is None:
            self.emit(eng, fn)
            return None
        name, amt = inc
        val = self.bump(name, amt)
        sem = self.sems[name]
        self.emit(eng, lambda e, fn=fn, sem=sem, amt=amt: fn(e).then_inc(sem, amt))
        return val

    def dma(self, eng, sem, out, in_, **kw):
        return self.op(eng, lambda e, out=out, in_=in_, kw=kw:
                       e.dma_start(out=out, in_=in_, **kw), inc=(sem, 16))

    def wait_tot(self, eng, *names):
        for n in names:
            self.wait(eng, n, self.cur(n))

    def cc(self, kind, groups, ins, outs):
        val = self.bump("s_cc", 1)
        sem = self.sems["s_cc"]
        self.emit("gpsimd", lambda e, kind=kind, groups=groups, ins=ins, outs=outs,
                  sem=sem: e.collective_compute(kind, ALU.bypass,
                                                replica_groups=groups,
                                                ins=ins, outs=outs).then_inc(sem))
        return val

    def add_bg(self, thunk):
        self.bg.append(thunk)

    def drain_bg(self, k=1):
        for _ in range(k):
            if not self.bg:
                return
            self.bg.pop(0)(self)

    def drain_all_bg(self):
        while self.bg:
            self.bg.pop(0)(self)

    def psum_fence(self):
        self.wait("tensor", "s_ve", self.cur("s_ve"))
        self.wait("tensor", "s_ac", self.cur("s_ac"))

    def sync_fence(self):
        self.wait("sync", "s_pe", self.cur("s_pe"))
        self.wait("sync", "s_ve", self.cur("s_ve"))
        self.wait("sync", "s_ac", self.cur("s_ac"))


def build(debug=False):
    nc = bass.Bass(target_bir_lowering=False)
    P = Plan()

    def param(name, shape, dtype, out=False):
        return nc.declare_dram_parameter(name, list(shape), dtype, isOutput=out).ap()

    def idram(name, shape, dtype):
        return nc.dram_tensor(name, list(shape), dtype, kind="Internal").ap()

    def sdram(name, shape, dtype):
        return nc.dram_tensor(name, list(shape), dtype, kind="Internal",
                              addr_space="Shared").ap()

    xT_in = param("xT_in", (D, TOK), F32)
    gate_sl = param("gate_sl", (DFF, DPC), I32)
    up_sl = param("up_sl", (DFF, DPC), I32)
    down_sl = param("down_sl", (D, DFPC), I32)
    gate_s = param("gate_s", (128, NFT), F32)
    up_s = param("up_s", (128, NFT), F32)
    down_s = param("down_s", (128, NDCH), F32)
    qa_sl = param("qa_sl", (DPC, QR), F32)
    qb_sl = param("qb_sl", (QRPC, NH * QK), F32)
    g_qa_sl = param("g_qa_sl", (QRPC, 1), F32)
    kva_sl = param("kva_sl", (DPC, KVR + RD), F32)
    kvbN_sl = param("kvbN_sl", (KVRPC, NH * ND), F32)
    kvbV_sl = param("kvbV_sl", (KVRPC, NH * VD), F32)
    g_kva_sl = param("g_kva_sl", (KVRPC, 1), F32)
    wo_sl = param("wo_sl", (DPC, D), F32)
    cosq = param("cosq", (RD, TOK), BF16)
    sinq = param("sinq", (RD, TOK), BF16)
    cosk = param("cosk", (RD, S), BF16)
    sink = param("sink", (RD, S), BF16)
    out_xT = param("out_xT", (D, TOK), F32, out=True)

    dbg = {}
    if debug:
        for name, shape, dt_ in [("d_xn0", (D, TOK), BF16), ("d_x1", (D, TOK), F32),
                                 ("d_qc", (QR, TOK), BF16), ("d_ckc", (KVR, TOK), BF16),
                                 ("d_x2", (D, TOK), F32), ("d_key", (4 * NH * ND, TOK), BF16),
                                 ("d_vv", (4 * TOK, NH * VD), BF16), ("d_oT", (D, TOK), BF16)]:
            dbg[name] = param(name, shape, dt_, out=True)

    gate_nb = idram("gate_nb", (DFF, DPC), BF16)
    up_nb = idram("up_nb", (DFF, DPC), BF16)
    down_nb = idram("down_nb", (D, DFPC), BF16)
    gu_pcA = idram("gu_pcA", (DPC, DFF), BF16)
    gu_pcB = idram("gu_pcB", (DPC, DFF), BF16)
    down_pc = idram("down_pc", (DFPC, D), BF16)
    qa_pc = idram("qa_pc", (DPC, QR), BF16)
    qb_pc = idram("qb_pc", (QRPC, NH * QK), BF16)
    kva_pc = idram("kva_pc", (DPC, KVR + RD), BF16)
    kvbN_pc = idram("kvbN_pc", (KVRPC, NH * ND), BF16)
    kvbV_pc = idram("kvbV_pc", (KVRPC, NH * VD), BF16)
    wo_pc = idram("wo_pc", (DPC, D), BF16)
    gu_TA = sdram("gu_TA", (D, DFF), BF16)
    gu_TB = sdram("gu_TB", (D, DFF), BF16)
    down_T = sdram("down_T", (DFFP, D), BF16)
    qa_T = sdram("qa_T", (D, QR), BF16)
    qb_T = sdram("qb_T", (QR, NH * QK), BF16)
    kva_T = sdram("kva_T", (D, KVR + RD), BF16)
    kvbN_T = sdram("kvbN_T", (KVR, NH * ND), BF16)
    kvbV_T = sdram("kvbV_T", (KVR, NH * VD), BF16)
    wo_T = sdram("wo_T", (D, D), BF16)
    h2_d = idram("h2_d", ((NFT - H1T) * 128, TOK), BF16)
    part_a = idram("part_a", (D, TOK), F32)
    part_b = idram("part_b", (D, TOK), F32)
    xres1 = idram("xres1", (D, TOK), F32)
    xres2 = idram("xres2", (D, TOK), F32)
    k_in0 = idram("k_in0", (NH * ND // 2, TOK), BF16)
    k_in1 = idram("k_in1", (NH * ND // 2, TOK), BF16)
    v_in0 = idram("v_in0", (TOK, NH * VD // 2), BF16)
    v_in1 = idram("v_in1", (TOK, NH * VD // 2), BF16)
    kr_in = idram("kr_in", (RD, TOK), BF16)
    km_in = idram("km_in", (1, NH), F32)
    km_T = idram("km_T", (4, NH), F32)
    k_T0 = idram("k_T0", (4 * NH * ND // 2, TOK), BF16)
    k_T1 = idram("k_T1", (4 * NH * ND // 2, TOK), BF16)
    v_T0 = idram("v_T0", (4 * TOK, NH * VD // 2), BF16)
    v_T1 = idram("v_T1", (4 * TOK, NH * VD // 2), BF16)
    kr_T = idram("kr_T", (4 * RD, TOK), BF16)

    groups8 = [list(range(8))]
    groups4 = [[0, 1, 2, 3], [4, 5, 6, 7]]

    H1C = H1T * TOK
    WG = [H1C, H1C + 4096, H1C + 8192, H1C + 12288]
    GEV = [H1C + 16384, H1C + 17408]
    HOV = [H1C + 18432, H1C + 19456]
    ST_X = [37120, 38144]
    ST_SQ = [39168, 40192]
    AT_QC = 0
    AT_KB = [12288, 16384]
    AT_VB = [20480, 24576]
    AT_SQK = 28672
    AT_ATB = [32768, 33792]
    AT_QW = 34816
    AT_QN = 37120
    AT_QA = 38144
    AT_R1 = 39168
    AT_R2 = 40192
    AT_CSQ = 41216
    AT_SSQ = 42240
    AT_KRA = 43264
    AB_COLS = 50176

    from contextlib import ExitStack
    with ExitStack() as _stack:
        xnT = _stack.enter_context(nc.sbuf_tensor("xnT", [128, NDCH * TOK], BF16))
        bc0 = _stack.enter_context(nc.sbuf_tensor("bc0", [128, TOK], F32))
        bc1 = _stack.enter_context(nc.sbuf_tensor("bc1", [128, TOK], F32))
        vrow1 = _stack.enter_context(nc.sbuf_tensor("vrow1", [128, TOK], F32))
        vrow2 = _stack.enter_context(nc.sbuf_tensor("vrow2", [128, TOK], F32))
        ones_b = _stack.enter_context(nc.sbuf_tensor("ones_b", [128, 1], BF16))
        onesrow = _stack.enter_context(nc.sbuf_tensor("onesrow", [1, 128], F32))
        scal = _stack.enter_context(nc.sbuf_tensor("scal", [32, 8], F32))
        tbuf = _stack.enter_context(nc.sbuf_tensor("tbuf", [128, 2048], BF16))
        cio_i = _stack.enter_context(nc.sbuf_tensor("cio_i", [128, 2 * CCH], I32))
        cio_o = _stack.enter_context(nc.sbuf_tensor("cio_o", [128, 2 * CCH], BF16))
        cio_f = _stack.enter_context(nc.sbuf_tensor("cio_f", [128, 2 * CCH], F32))
        sv_g = _stack.enter_context(nc.sbuf_tensor("sv_g", [128, NFT], F32))
        sv_u = _stack.enter_context(nc.sbuf_tensor("sv_u", [128, NFT], F32))
        sv_d = _stack.enter_context(nc.sbuf_tensor("sv_d", [128, NDCH], F32))
        sv_q = _stack.enter_context(nc.sbuf_tensor("sv_q", [128, 2], F32))
        sv_k = _stack.enter_context(nc.sbuf_tensor("sv_k", [128, 1], F32))
        AB = _stack.enter_context(nc.sbuf_tensor("AB", [128, AB_COLS], BF16))
        AFX = _stack.enter_context(nc.sbuf_tensor("AFX", [128, 2 * TOK], F32))
        xfb = _stack.enter_context(nc.sbuf_tensor("xfb", [128, TOK], F32))
        ps0 = _stack.enter_context(nc.psum_tensor("ps0", [128, TOK], F32))
        ps1 = _stack.enter_context(nc.psum_tensor("ps1", [128, TOK], F32))
        ps2 = _stack.enter_context(nc.psum_tensor("ps2", [128, TOK], F32))
        ps3 = _stack.enter_context(nc.psum_tensor("ps3", [128, TOK], F32))
        SEM_NAMES = ["s_cc", "s_pe", "s_ac", "s_ve",
                     "svl", "cvi", "cvo0", "cvo1", "tr", "trs",
                     "fb0", "fb1", "xfb", "w0", "w1",
                     "hst0", "hst1", "hld", "pst0", "pst1", "xw0", "xw1",
                     "kvw0", "kvw1", "vst", "krw", "tab",
                     "kv0", "kv1", "qw", "qsw", "aug", "s_gpc"]
        for _sn in SEM_NAMES:
            P.sems[_sn] = _stack.enter_context(nc.semaphore(_sn))
        block = _stack.enter_context(nc.Block())
        PS = [ps0, ps1, ps2, ps3]

        P.op("vector", lambda e: e.memset(ones_b[:, :], 1.0), inc=("s_ve", 1))
        P.op("vector", lambda e: e.memset(onesrow[:, :], 1.0), inc=("s_ve", 1))
        P.dma("gpsimd", "svl", sv_g[:, :], gate_s)
        P.dma("gpsimd", "svl", sv_u[:, :], up_s)
        P.dma("gpsimd", "svl", sv_d[:, :], down_s)
        P.dma("gpsimd", "svl", sv_q[:, 0:1], g_qa_sl[0:128, :])
        P.dma("gpsimd", "svl", sv_q[0:64, 1:2], g_qa_sl[128:192, :])
        P.dma("gpsimd", "svl", sv_k[0:64, 0:1], g_kva_sl[:, :])
        P.wait("vector", "svl", P.cur("svl"))

        conv = {"n": 0, "cast": {}, "store": {}, "tr": None}

        def conv_chain(src, dst, r0, nr, c0, cw, is_f32, scale_ap, const,
                       eng_ld="gpsimd"):
            n = conv["n"]
            conv["n"] = n + 1
            par = n % 2
            if is_f32:
                ib = cio_f[0:nr, par * CCH: par * CCH + cw]
            else:
                ib = cio_i[0:nr, par * CCH: par * CCH + cw]
            ob = cio_o[0:nr, par * CCH: par * CCH + cw]
            if n - 2 in conv["cast"]:
                P.wait(eng_ld, "s_ve", conv["cast"][n - 2])
                P.wait("vector", "cvo%d" % par, conv["store"][n - 2])
                del conv["cast"][n - 2], conv["store"][n - 2]
            ld = P.dma(eng_ld, "cvi", ib, src[r0:r0 + nr, c0:c0 + cw])
            P.wait("vector", "cvi", ld)
            if scale_ap is None and const is None:
                cv = P.op("vector", lambda e, o=ob, i=ib: e.tensor_copy(o, i),
                          inc=("s_ve", 1))
            elif scale_ap is None:
                cv = P.op("vector", lambda e, o=ob, i=ib, k=const:
                          e.tensor_scalar_mul(o, i, k), inc=("s_ve", 1))
            elif const is None:
                cv = P.op("vector", lambda e, o=ob, i=ib, s=scale_ap:
                          e.tensor_scalar_mul(o, i, s), inc=("s_ve", 1))
            else:
                cv = P.op("vector", lambda e, o=ob, i=ib, s=scale_ap, k=const:
                          e.tensor_scalar(o, i, s, k, ALU.mult, ALU.mult),
                          inc=("s_ve", 1))
            P.wait("gpsimd", "s_ve", cv)
            st = P.dma("gpsimd", "cvo%d" % par, dst[r0:r0 + nr, c0:c0 + cw], ob)
            conv["cast"][n] = cv
            conv["store"][n] = st

        def conv_transp(nb, pc, c, dstc0, rw, srcr0=None):
            if srcr0 is None:
                srcr0 = dstc0
            if conv["tr"] is not None:
                P.wait("sync", "trs", conv["tr"])
            tr = P.dma("sync", "tr", tbuf[:, 0:rw],
                       nb[srcr0:srcr0 + rw, c * 128:(c + 1) * 128], transpose=True)
            P.wait("gpsimd", "tr", tr)
            st = P.dma("gpsimd", "trs", pc[c * 128:(c + 1) * 128, dstc0:dstc0 + rw],
                       tbuf[:, 0:rw])
            conv["tr"] = st

        def cast_matrix_i32(src, dst, nrows, sv, hook=None):
            for t in range(nrows // 128):
                for c0 in range(0, src.shape[1], CCH):
                    cw = min(CCH, src.shape[1] - c0)
                    conv_chain(src, dst, t * 128, 128, c0, cw, False,
                               sv[:, t:t + 1], None, eng_ld="gpsimd")
                    if hook is not None:
                        hook()

        HDFF = (NFT // 2) * 128

        def gu_ready(t):
            if t < NFT // 2:
                return gu_ag[0]
            assert len(gu_ag) > 1, "B-half AG not registered by tile %d" % t
            return gu_ag[1]

        gu_ag = []
        down_ag_v, qa_ag_v, qb_ag_v = [], [], []
        kva_ag_v, kvbN_ag_v, kvbV_ag_v, wo_ag_v = [], [], [], []

        def bg_chain(src, dst, r0, nr, c0, cw, is_f32, scale_fn, const, fin):
            def run(_):
                sc = scale_fn() if scale_fn else None
                conv_chain(src, dst, r0, nr, c0, cw, is_f32, sc, const)
                if fin is not None:
                    fin()
            return run

        def add_f32_matrix(src, dst, nrows, ncols, aglist, out_T, scale_fn=None,
                           const=None):
            tiles = []
            nrt = (nrows + 127) // 128
            for t in range(nrt):
                r0 = t * 128
                nr = min(128, nrows - r0)
                for c0 in range(0, ncols, CCH):
                    cw = min(CCH, ncols - c0)
                    tiles.append((r0, nr, c0, cw, t))
            for i, (r0, nr, c0, cw, t) in enumerate(tiles):
                fin = None
                if i == len(tiles) - 1:
                    def fin(agl=aglist, dd=dst, oT=out_T):
                        P.wait_tot("gpsimd", "cvo0", "cvo1")
                        agl.append(P.cc("AllGather", groups8,
                                        [dd.opt()], [oT.opt()]))
                sf = (lambda t=t, nr=nr: scale_fn(t, nr)) if scale_fn else None
                P.add_bg(bg_chain(src, dst, r0, nr, c0, cw, True, sf, const, fin))

        qk_scale = float(QK) ** -0.5
        add_f32_matrix(qa_sl, qa_pc, DPC, QR, qa_ag_v, qa_T)
        add_f32_matrix(kva_sl, kva_pc, DPC, KVR + RD, kva_ag_v, kva_T)
        add_f32_matrix(qb_sl, qb_pc, QRPC, NH * QK, qb_ag_v, qb_T,
                       scale_fn=lambda t, nr: sv_q[0:nr, t:t + 1], const=qk_scale)
        add_f32_matrix(kvbN_sl, kvbN_pc, KVRPC, NH * ND, kvbN_ag_v, kvbN_T,
                       scale_fn=lambda t, nr: sv_k[0:nr, 0:1])
        add_f32_matrix(kvbV_sl, kvbV_pc, KVRPC, NH * VD, kvbV_ag_v, kvbV_T,
                       scale_fn=lambda t, nr: sv_k[0:nr, 0:1])
        add_f32_matrix(wo_sl, wo_pc, DPC, D, wo_ag_v, wo_T)

        stats_state = {"pe": 0}

        def stats_tile(xsrc_f32_sb, dt):
            par = dt % 2
            xbb = AB[:, ST_X[par]:ST_X[par] + TOK]
            sq = AB[:, ST_SQ[par]:ST_SQ[par] + TOK]
            P.wait("vector", "s_pe", stats_state["pe"] if dt >= 2 else 0)
            cv = P.op("vector", lambda e, o=xbb, i=xsrc_f32_sb:
                      e.tensor_copy(o, i), inc=("s_ve", 1))
            P.wait("scalar", "s_ve", cv)
            sv = P.op("scalar", lambda e, o=sq, i=xbb:
                      e.activation(o, i, AF.Square), inc=("s_ac", 1))
            P.wait("tensor", "s_ac", sv)
            f, l = (dt == 0), (dt == NDCH - 1)
            P.op("tensor", lambda e, i=xbb, f=f, l=l:
                 e.matmul(ps2[0:1, 0:512], ones_b[:, :], i[:, 0:512], start=f, stop=l))
            P.op("tensor", lambda e, i=xbb, f=f, l=l:
                 e.matmul(ps2[0:1, 512:1024], ones_b[:, :], i[:, 512:1024],
                          start=f, stop=l))
            P.op("tensor", lambda e, i=sq, f=f, l=l:
                 e.matmul(ps3[0:1, 0:512], ones_b[:, :], i[:, 0:512], start=f, stop=l))
            pe = P.op("tensor", lambda e, i=sq, f=f, l=l:
                      e.matmul(ps3[0:1, 512:1024], ones_b[:, :], i[:, 512:1024],
                               start=f, stop=l), inc=("s_pe", 1))
            stats_state["pe"] = pe
            return cv

        def bcast_row(src_row, psd, dst):
            P.op("tensor", lambda e: e.matmul(psd[:, 0:512], onesrow[:, :],
                                              src_row[:, 0:512], start=True, stop=True))
            pe = P.op("tensor", lambda e: e.matmul(psd[:, 512:1024], onesrow[:, :],
                                                   src_row[:, 512:1024],
                                                   start=True, stop=True),
                      inc=("s_pe", 1))
            P.wait("vector", "s_pe", pe)
            return P.op("vector", lambda e: e.tensor_copy(dst[:, :], psd[:, :]),
                        inc=("s_ve", 1))

        def finalize_ln():
            mean = bc0[0:1, :]
            var = vrow1[0:1, :]
            rstd = bc1[0:1, :]
            t = vrow2[0:1, :]
            P.wait("vector", "s_pe", P.cur("s_pe"))
            P.op("vector", lambda e: e.tensor_scalar_mul(mean, ps2[0:1, :], 1.0 / D))
            v2 = P.op("vector", lambda e: e.tensor_scalar_mul(var, ps3[0:1, :], 1.0 / D),
                      inc=("s_ve", 1))
            P.wait("vector", "s_ve", v2)
            P.op("vector", lambda e: e.tensor_mul(t, mean, mean))
            P.op("vector", lambda e: e.tensor_sub(var, var, t))
            v3 = P.op("vector", lambda e: e.tensor_scalar_add(var, var, EPS_LN),
                      inc=("s_ve", 1))
            P.wait("scalar", "s_ve", v3)
            a1 = P.op("scalar", lambda e: e.activation(rstd, var, AF.Sqrt),
                      inc=("s_ac", 1))
            P.wait("vector", "s_ac", a1)
            v4 = P.op("vector", lambda e: e.reciprocal(rstd, rstd), inc=("s_ve", 1))
            P.wait("tensor", "s_ve", v4)
            P.psum_fence()
            bcast_row(mean, ps0, bc0)
            return bcast_row(rstd, ps1, bc1)

        def ln_apply(xsrc_d):
            bdone = finalize_ln()
            P.wait_tot("sync", "xw0", "xw1")
            P.wait("sync", "s_ve", P.cur("s_ve"))
            applied = {}
            for c in range(NDCH):
                par = c % 2
                xb = AFX[:, par * TOK:(par + 1) * TOK]
                if c >= 2:
                    P.wait("sync", "s_ve", applied[c - 2])
                ld = P.dma("sync", "fb%d" % par, xb, xsrc_d[c * 128:(c + 1) * 128, :])
                P.wait("vector", "fb%d" % par, ld)
                P.wait("vector", "s_ve", bdone)
                P.op("vector", lambda e, xb=xb: e.tensor_sub(xb, xb, bc0[:, :]))
                applied[c] = P.op("vector", lambda e, xb=xb, c=c:
                                  e.tensor_mul(xnT[:, c * TOK:(c + 1) * TOK], xb,
                                               bc1[:, :]), inc=("s_ve", 1))
            return P.cur("s_ve")

        P.psum_fence()
        ln0 = {"c": 0, "cv": {}}

        def ln0_step():
            c = ln0["c"]
            if c >= NDCH:
                return
            ln0["c"] = c + 1
            par = c % 2
            xb = AFX[:, par * TOK:(par + 1) * TOK]
            if c >= 2:
                P.wait("sync", "s_ve", ln0["cv"][c - 2])
            ld = P.dma("sync", "fb%d" % par, xb, xT_in[c * 128:(c + 1) * 128, :])
            P.wait("vector", "fb%d" % par, ld)
            ln0["cv"][c] = stats_tile(xb, c)

        NHT = NFT // 2
        cast_matrix_i32(gate_sl[0:HDFF, :], gate_nb[0:HDFF, :], HDFF,
                        sv_g[:, 0:NHT], hook=ln0_step)
        while ln0["c"] < NDCH:
            ln0_step()
        g_cvo = (P.cur("cvo0"), P.cur("cvo1"))
        xn_ready = ln_apply(xT_in)
        cast_matrix_i32(up_sl[0:HDFF, :], up_nb[0:HDFF, :], HDFF, sv_u[:, 0:NHT])
        P.wait("sync", "cvo0", g_cvo[0])
        P.wait("sync", "cvo1", g_cvo[1])
        for c in range(DPC // 128):
            for r0 in range(0, HDFF, 2048):
                conv_transp(gate_nb, gu_pcA, c, r0, min(2048, HDFF - r0))
        P.wait_tot("sync", "cvo0", "cvo1")
        for c in range(DPC // 128):
            for r0 in range(0, HDFF, 2048):
                conv_transp(up_nb, gu_pcA, c, HDFF + r0, min(2048, HDFF - r0),
                            srcr0=r0)
        P.wait_tot("gpsimd", "trs")
        gu_ag.append(P.cc("AllGather", groups8, [gu_pcA.opt()], [gu_TA.opt()]))

        def fin_B():
            P.wait_tot("sync", "cvo0", "cvo1")
            for c in range(DPC // 128):
                for r0 in range(0, DFF - HDFF, 2048):
                    conv_transp(gate_nb, gu_pcB, c, r0,
                                min(2048, DFF - HDFF - r0), srcr0=HDFF + r0)
            for c in range(DPC // 128):
                for r0 in range(0, DFF - HDFF, 2048):
                    conv_transp(up_nb, gu_pcB, c, HDFF + r0,
                                min(2048, DFF - HDFF - r0), srcr0=HDFF + r0)
            P.wait_tot("gpsimd", "trs")
            gu_ag.append(P.cc("AllGather", groups8, [gu_pcB.opt()], [gu_TB.opt()]))

        def fin_down():
            P.wait_tot("sync", "cvo0", "cvo1")
            for c in range(DFPC // 128):
                conv_transp(down_nb, down_pc, c, 0, 2048)
                conv_transp(down_nb, down_pc, c, 2048, 2048)
            P.wait_tot("gpsimd", "trs")
            down_ag_v.append(P.cc("AllGather", groups8,
                                  [down_pc.opt()], [down_T.opt()]))

        def reg_i32_bg(src, dst, nrows, sv, fin):
            tiles = []
            for t in range(nrows // 128):
                for c0 in range(0, src.shape[1], CCH):
                    tiles.append((t, c0, min(CCH, src.shape[1] - c0)))
            for i, (t, c0, cw) in enumerate(tiles):
                P.add_bg(bg_chain(src, dst, t * 128, 128, c0, cw, False,
                                  (lambda t=t, sv=sv: sv[:, t:t + 1]), None,
                                  fin if i == len(tiles) - 1 else None))

        pre_bg = P.bg[:]
        P.bg = []
        reg_i32_bg(gate_sl[HDFF:DFF, :], gate_nb[HDFF:DFF, :], DFF - HDFF,
                   sv_g[:, NHT:NFT], None)
        reg_i32_bg(up_sl[HDFF:DFF, :], up_nb[HDFF:DFF, :], DFF - HDFF,
                   sv_u[:, NHT:NFT], fin_B)
        reg_i32_bg(down_sl, down_nb, D, sv_d, fin_down)
        P.bg.extend(pre_bg)
        if debug:
            P.wait("gpsimd", "s_ve", xn_ready)
            for c in range(NDCH):
                P.dma("gpsimd", "krw", dbg["d_xn0"][c * 128:(c + 1) * 128, :],
                      xnT[:, c * TOK:(c + 1) * TOK])

        def emit_ffn_A(first_pass, xn_ready):
            P.psum_fence()
            P.sync_fence()
            mm_of, ve_of, ac_of, gp_of = {}, {}, {}, {}
            for t in range(NFT):
                par = t % 2
                gbuf = AB[:, WG[par]:WG[par] + 4096]
                ubuf = AB[:, WG[2 + par]:WG[2 + par] + 4096]
                if first_pass:
                    P.wait("sync", "s_cc", gu_ready(t))
                if t >= 2:
                    P.wait("sync", "s_pe", mm_of[t - 2])
                gu_Th = gu_TA if t < NFT // 2 else gu_TB
                th = t if t < NFT // 2 else t - NFT // 2
                P.dma("sync", "w%d" % par, gbuf.rearrange("p (c f) -> p c f", c=NDCH),
                      gu_Th[:, th * 128:(th + 1) * 128].rearrange(
                          "(c p) f -> p c f", p=128))
                lu = P.dma("sync", "w%d" % par, ubuf.rearrange("p (c f) -> p c f", c=NDCH),
                           gu_Th[:, HDFF + th * 128:HDFF + (th + 1) * 128].rearrange(
                               "(c p) f -> p c f", p=128))
                psg, psu = PS[par], PS[2 + par]
                P.wait("tensor", "w%d" % par, lu)
                P.wait("tensor", "s_ve", xn_ready)
                if t >= 2:
                    P.wait("tensor", "s_ve", ve_of[t - 2])
                    P.wait("tensor", "s_ac", ac_of[t - 2])
                for c in range(NDCH):
                    lt = gbuf[:, c * 128:(c + 1) * 128]
                    f, l = (c == 0), (c == NDCH - 1)
                    P.op("tensor", lambda e, lt=lt, c=c, psg=psg, f=f, l=l:
                         e.matmul(psg[:, 0:512], lt, xnT[:, c * TOK:c * TOK + 512],
                                  start=f, stop=l))
                    P.op("tensor", lambda e, lt=lt, c=c, psg=psg, f=f, l=l:
                         e.matmul(psg[:, 512:1024], lt,
                                  xnT[:, c * TOK + 512:(c + 1) * TOK], start=f, stop=l))
                for c in range(NDCH):
                    lt = ubuf[:, c * 128:(c + 1) * 128]
                    f, l = (c == 0), (c == NDCH - 1)
                    P.op("tensor", lambda e, lt=lt, c=c, psu=psu, f=f, l=l:
                         e.matmul(psu[:, 0:512], lt, xnT[:, c * TOK:c * TOK + 512],
                                  start=f, stop=l))
                    mm = P.op("tensor", lambda e, lt=lt, c=c, psu=psu, f=f, l=l:
                              e.matmul(psu[:, 512:1024], lt,
                                       xnT[:, c * TOK + 512:(c + 1) * TOK],
                                       start=f, stop=l),
                              inc=("s_pe", 1) if l else None)
                mm_of[t] = mm
                gev = AB[:, GEV[par]:GEV[par] + TOK]
                P.wait("scalar", "s_pe", mm)
                if t >= 2:
                    P.wait("scalar", "s_ve", ve_of[t - 2])
                av = P.op("scalar", lambda e, o=gev, psg=psg:
                          e.activation(o, psg[:, :], AF.Silu), inc=("s_ac", 1))
                ac_of[t] = av
                if t < H1T:
                    hdst = AB[:, t * TOK:(t + 1) * TOK]
                else:
                    hdst = AB[:, HOV[par]:HOV[par] + TOK]
                    if t - 2 >= H1T:
                        P.wait("vector", "hst%d" % par, gp_of[t - 2])
                P.wait("vector", "s_ac", av)
                hv = P.op("vector", lambda e, o=hdst, psu=psu, g=gev:
                          e.tensor_tensor(o, psu[:, :], g, ALU.mult), inc=("s_ve", 1))
                ve_of[t] = hv
                if t >= H1T:
                    P.wait("gpsimd", "s_ve", hv)
                    gp_of[t] = P.dma("gpsimd", "hst%d" % par,
                                     h2_d[(t - H1T) * 128:(t - H1T + 1) * 128, :],
                                     hdst)
                P.drain_bg(4)
            return P.cur("s_ve")

        def emit_ffn_B(xbase_d, xdst_d, final):
            assert down_ag_v, "down AG not emitted yet"
            P.wait("sync", "s_cc", down_ag_v[0])
            P.sync_fence()
            P.psum_fence()
            SB = [H1C, H1C + 3712]
            gpev = {}
            xbv = {}
            for sub, (ca, cb) in enumerate(DSUB):
                nch = cb - ca
                if sub > 0:
                    P.wait("gpsimd", "s_pe", P.cur("s_pe"))
                    P.wait("gpsimd", "s_ve", P.cur("s_ve"))
                    P.wait("gpsimd", "s_ac", P.cur("s_ac"))
                    P.wait_tot("gpsimd", "hst0", "hst1")
                    for i in range(nch):
                        P.dma("gpsimd", "hld", AB[:, i * TOK:(i + 1) * TOK],
                              h2_d[(ca - H1T + i) * 128:(ca - H1T + i + 1) * 128, :])
                    P.wait_tot("tensor", "hld")
                    P.wait_tot("sync", "pst0", "pst1")
                mm_of, ve_of = {}, {}
                for dt in range(NDCH):
                    par = dt % 2
                    sbv = AB[:, SB[par]:SB[par] + nch * 128]
                    if dt >= 2:
                        P.wait("sync", "s_pe", mm_of[dt - 2])
                    lw = P.dma("sync", "w%d" % par, sbv.rearrange("p (c f) -> p c f", c=nch),
                               down_T[ca * 128:cb * 128,
                                      dt * 128:(dt + 1) * 128].rearrange(
                                          "(c p) f -> p c f", p=128))
                    psd = PS[par]
                    P.wait("tensor", "w%d" % par, lw)
                    if dt >= 2:
                        P.wait("tensor", "s_ve", ve_of[dt - 2])
                    for c in range(nch):
                        lt = sbv[:, c * 128:(c + 1) * 128]
                        f, l = (c == 0), (c == nch - 1)
                        P.op("tensor", lambda e, lt=lt, c=c, psd=psd, f=f, l=l:
                             e.matmul(psd[:, 0:512], lt, AB[:, c * TOK:c * TOK + 512],
                                      start=f, stop=l))
                        mm = P.op("tensor", lambda e, lt=lt, c=c, psd=psd, f=f, l=l:
                                  e.matmul(psd[:, 512:1024], lt,
                                           AB[:, c * TOK + 512:(c + 1) * TOK],
                                           start=f, stop=l),
                                  inc=("s_pe", 1) if l else None)
                    mm_of[dt] = mm
                    fb = AFX[:, par * TOK:(par + 1) * TOK]
                    if sub == 0:
                        P.wait("vector", "s_pe", mm)
                        if dt >= 2:
                            P.wait("vector", "pst%d" % par, gpev[dt - 2])
                        vv = P.op("vector", lambda e, o=fb, psd=psd:
                                  e.tensor_copy(o, psd[:, :]), inc=("s_ve", 1))
                        ve_of[dt] = vv
                        P.wait("gpsimd", "s_ve", vv)
                        gpev[dt] = P.dma("gpsimd", "pst%d" % par,
                                         part_a[dt * 128:(dt + 1) * 128, :], fb)
                    elif sub == 1:
                        if dt >= 2:
                            P.wait("sync", "s_ve", ve_of[dt - 2])
                        if dt >= 2:
                            P.wait("sync", "pst%d" % par, gpev[dt - 2])
                        ld = P.dma("sync", "fb%d" % par, fb,
                                   part_a[dt * 128:(dt + 1) * 128, :])
                        P.wait("vector", "s_pe", mm)
                        P.wait("vector", "fb%d" % par, ld)
                        if dt >= 2:
                            P.wait("vector", "pst%d" % par, gpev[dt - 2])
                        vv = P.op("vector", lambda e, o=fb, psd=psd:
                                  e.tensor_add(o, psd[:, :], o), inc=("s_ve", 1))
                        ve_of[dt] = vv
                        P.wait("gpsimd", "s_ve", vv)
                        gpev[dt] = P.dma("gpsimd", "pst%d" % par,
                                         part_b[dt * 128:(dt + 1) * 128, :], fb)
                    else:
                        if dt >= 2:
                            P.wait("sync", "s_ve", ve_of[dt - 2])
                        if dt >= 2:
                            P.wait("sync", "xw%d" % par, gpev[dt - 2])
                        ld1 = P.dma("sync", "fb%d" % par, fb,
                                    part_b[dt * 128:(dt + 1) * 128, :])
                        P.wait("sync", "s_ve", xbv.get(dt - 1, 0))
                        ld2 = P.dma("sync", "xfb", xfb[:, :],
                                    xbase_d[dt * 128:(dt + 1) * 128, :])
                        P.wait("vector", "s_pe", mm)
                        P.wait("vector", "fb%d" % par, ld1)
                        P.wait("vector", "xfb", ld2)
                        if dt >= 2:
                            P.wait("vector", "xw%d" % par, gpev[dt - 2])
                        P.op("vector", lambda e, o=fb, psd=psd:
                             e.tensor_add(o, psd[:, :], o))
                        vv = P.op("vector", lambda e, o=fb:
                                  e.scalar_tensor_tensor(o, o, 0.5, xfb[:, :],
                                                         ALU.mult, ALU.add),
                                  inc=("s_ve", 1))
                        xbv[dt] = vv
                        ve_of[dt] = vv
                        P.wait("gpsimd", "s_ve", vv)
                        dst = out_xT if final else xdst_d
                        gpev[dt] = P.dma("gpsimd", "xw%d" % par,
                                         dst[dt * 128:(dt + 1) * 128, :], fb)
                        if not final:
                            ve_of[dt] = stats_tile(fb, dt)
                    P.drain_bg(4)
            if final:
                return None
            return ln_apply(xdst_d)

        emit_ffn_A(first_pass=True, xn_ready=xn_ready)
        xn_ready = emit_ffn_B(xbase_d=xT_in, xdst_d=xres1, final=False)
        if debug:
            P.wait("gpsimd", "s_ve", xn_ready)
            for c in range(NDCH):
                P.dma("gpsimd", "krw", dbg["d_x1"][c * 128:(c + 1) * 128, :],
                      xres1[c * 128:(c + 1) * 128, :])

        P.drain_all_bg()
        assert qa_ag_v and qb_ag_v and kva_ag_v and kvbN_ag_v and kvbV_ag_v \
            and wo_ag_v

        P.psum_fence()
        P.sync_fence()
        CK = AT_VB[0]
        nrm = bc0[0:1, :]
        km_ve = {}

        P.wait("sync", "s_cc", kva_ag_v[0])
        P.wait("sync", "s_pe", P.cur("s_pe"))
        P.psum_fence()
        P.wait("tensor", "s_ve", xn_ready)
        mm_of, kv_cv = {}, {}
        for kt in range(5):
            ncols = 128 if kt < 4 else 64
            par = kt % 2
            wb = AB[:, AT_KB[0] + par * 4096: AT_KB[0] + par * 4096 + NDCH * ncols]
            if kt >= 2:
                P.wait("sync", "s_pe", mm_of[kt - 2])
            lw = P.dma("sync", "w%d" % par, wb.rearrange("p (c f) -> p c f", c=NDCH),
                       kva_T[:, kt * 128:kt * 128 + ncols].rearrange(
                           "(c p) f -> p c f", p=128))
            psq = PS[par]
            P.wait("tensor", "w%d" % par, lw)
            if kt >= 2:
                P.wait("tensor", "s_ve", kv_cv[kt - 2])
            for c in range(NDCH):
                lt = wb[:, c * ncols:(c + 1) * ncols]
                f, l = (c == 0), (c == NDCH - 1)
                P.op("tensor", lambda e, lt=lt, c=c, psq=psq, nr=ncols, f=f, l=l:
                     e.matmul(psq[0:nr, 0:512], lt, xnT[:, c * TOK:c * TOK + 512],
                              start=f, stop=l))
                mm = P.op("tensor", lambda e, lt=lt, c=c, psq=psq, nr=ncols, f=f, l=l:
                          e.matmul(psq[0:nr, 512:1024], lt,
                                   xnT[:, c * TOK + 512:(c + 1) * TOK],
                                   start=f, stop=l), inc=("s_pe", 1) if l else None)
            mm_of[kt] = mm
            P.wait("vector", "s_pe", mm)
            if kt < 4:
                kdst = AB[:, CK + kt * TOK: CK + (kt + 1) * TOK]
                cv = P.op("vector", lambda e, o=kdst, psq=psq:
                          e.tensor_copy(o, psq[:, :]), inc=("s_ve", 1))
                kv_cv[kt] = cv
                sq = AB[:, AT_R1:AT_R1 + TOK]
                P.wait("scalar", "s_ve", cv)
                svv = P.op("scalar", lambda e, o=sq, i=kdst:
                           e.activation(o, i, AF.Square), inc=("s_ac", 1))
                P.wait("tensor", "s_ac", svv)
                f, l = (kt == 0), (kt == 3)
                P.op("tensor", lambda e, i=sq, f=f, l=l:
                     e.matmul(ps3[0:1, 0:512], ones_b[:, :], i[:, 0:512],
                              start=f, stop=l))
                P.op("tensor", lambda e, i=sq, f=f, l=l:
                     e.matmul(ps3[0:1, 512:1024], ones_b[:, :], i[:, 512:1024],
                              start=f, stop=l), inc=("s_pe", 1))
            else:
                krb = AB[:, AT_R2:AT_R2 + TOK]
                cv = P.op("vector", lambda e, o=krb, psq=psq:
                          e.tensor_copy(o[0:64, :], psq[0:64, :]), inc=("s_ve", 1))
                kv_cv[kt] = cv
                P.wait("gpsimd", "s_ve", cv)
                krst = P.dma("gpsimd", "krw", kr_in, krb[0:64, :])
        P.wait("gpsimd", "krw", krst)
        ag_r = P.cc("AllGather", groups4, [kr_in.opt()], [kr_T.opt()])
        P.wait("scalar", "s_pe", P.cur("s_pe"))
        a1 = P.op("scalar", lambda e: e.activation(nrm, ps3[0:1, :], AF.Sqrt,
                                                   scale=1.0 / KVR), inc=("s_ac", 1))
        P.wait("vector", "s_ac", a1)
        P.op("vector", lambda e: e.tensor_scalar_add(nrm, nrm, EPS_RMS))
        v1 = P.op("vector", lambda e: e.reciprocal(nrm, nrm), inc=("s_ve", 1))
        P.wait("tensor", "s_ve", v1)
        P.psum_fence()
        bv = bcast_row(nrm, ps0, bc0)
        for kt in range(4):
            kd = AB[:, CK + kt * TOK: CK + (kt + 1) * TOK]
            P.wait("vector", "s_ve", bv)
            P.op("vector", lambda e, k=kd: e.tensor_mul(k, k, bc0[:, :]),
                 inc=("s_ve", 1) if kt == 3 else None)
        ck_done = P.cur("s_ve")
        if debug:
            P.wait("gpsimd", "s_ve", ck_done)
            for kt in range(4):
                P.dma("gpsimd", "krw", dbg["d_ckc"][kt * 128:(kt + 1) * 128, :],
                      AB[:, CK + kt * TOK: CK + (kt + 1) * TOK])

        krb = AB[:, AT_R2:AT_R2 + TOK]
        sqk_own = AB[:, AT_R1:AT_R1 + TOK]
        P.wait("scalar", "s_ve", kv_cv[4])
        kra_own = P.op("scalar", lambda e: e.activation(sqk_own[0:64, :], krb[0:64, :],
                                                        AF.Square), inc=("s_ac", 1))
        P.wait("tensor", "s_ac", kra_own)
        P.psum_fence()
        P.op("tensor", lambda e: e.matmul(ps2[0:1, 0:512], ones_b[0:64, :],
                                          sqk_own[0:64, 0:512], start=True, stop=True))
        krpe = P.op("tensor", lambda e:
                    e.matmul(ps2[0:1, 512:1024], ones_b[0:64, :],
                             sqk_own[0:64, 512:1024], start=True, stop=True),
                    inc=("s_pe", 1))
        P.wait("vector", "s_pe", krpe)
        kr2v = P.op("vector", lambda e: e.tensor_copy(bc1[0:1, :], ps2[0:1, :]),
                    inc=("s_ve", 1))

        P.wait("sync", "s_cc", kvbN_ag_v[0])
        P.wait("sync", "s_pe", P.cur("s_pe"))
        P.psum_fence()
        knv, kng, kn_mm = {}, {}, {}
        for h in range(NH):
            parw = h % 2
            kwb = AB[:, AT_KB[0] + parw * 512:AT_KB[0] + parw * 512 + 512]
            if h >= 2:
                P.wait("sync", "s_pe", kn_mm[h - 2])
            lw = P.dma("sync", "w%d" % parw, kwb.rearrange("p (c f) -> p c f", c=4),
                       kvbN_T[:, h * ND:(h + 1) * ND].rearrange(
                           "(c p) f -> p c f", p=128))
            psk = PS[h % 2]
            P.wait("tensor", "w%d" % parw, lw)
            P.wait("tensor", "s_ve", ck_done)
            if h >= 2:
                P.wait("tensor", "s_ve", knv[h - 2])
            for c in range(4):
                lt = kwb[:, c * 128:(c + 1) * 128]
                f, l = (c == 0), (c == 3)
                P.op("tensor", lambda e, lt=lt, c=c, psk=psk, f=f, l=l:
                     e.matmul(psk[:, 0:512], lt,
                              AB[:, CK + c * TOK: CK + c * TOK + 512], start=f, stop=l))
                mm = P.op("tensor", lambda e, lt=lt, c=c, psk=psk, f=f, l=l:
                          e.matmul(psk[:, 512:1024], lt,
                                   AB[:, CK + c * TOK + 512: CK + (c + 1) * TOK],
                                   start=f, stop=l), inc=("s_pe", 1) if l else None)
            kn_mm[h] = mm
            kout = AB[:, AT_ATB[h % 2]:AT_ATB[h % 2] + TOK]
            P.wait("vector", "s_pe", mm)
            if h >= 2:
                P.wait("vector", "kvw%d" % (h % 2), kng[h - 2])
            cv = P.op("vector", lambda e, o=kout, psk=psk:
                      e.tensor_copy(o, psk[:, :]), inc=("s_ve", 1))
            knv[h] = cv
            P.wait("gpsimd", "s_ve", cv)
            k_inh = k_in0 if h < 16 else k_in1
            kng[h] = P.dma("gpsimd", "kvw%d" % (h % 2),
                           k_inh[(h % 16) * ND:(h % 16 + 1) * ND, :], kout)
            if h == 15:
                P.wait("gpsimd", "kvw0", kng[14])
                P.wait("gpsimd", "kvw1", kng[15])
                ag_k0 = P.cc("AllGather", groups4, [k_in0.opt()], [k_T0.opt()])
            sqh = AB[:, 30976:30976 + TOK]
            P.wait("scalar", "s_ve", cv)
            if h >= 1:
                P.wait("scalar", "s_ve", km_ve[h - 1])
            sks = P.op("scalar", lambda e, o=sqh, i=kout:
                       e.activation(o, i, AF.Square), inc=("s_ac", 1))
            P.wait("tensor", "s_ac", sks)
            if h >= 1:
                P.wait("tensor", "s_ve", km_ve[h - 1])
            P.op("tensor", lambda e: e.matmul(ps2[0:1, 0:512], ones_b[:, :],
                                              sqh[:, 0:512], start=True, stop=True))
            kpe = P.op("tensor", lambda e:
                       e.matmul(ps2[0:1, 512:1024], ones_b[:, :],
                                sqh[:, 512:1024], start=True, stop=True),
                       inc=("s_pe", 1))
            P.wait("vector", "s_pe", kpe)
            P.op("vector", lambda e: e.tensor_add(vrow1[0:1, :], ps2[0:1, :],
                                                  bc1[0:1, :]))
            km_ve[h] = P.op("vector", lambda e, h=h:
                            e.reduce_max(vrow2[0:1, h:h + 1], vrow1[0:1, :],
                                         axis=AX), inc=("s_ve", 1))
            P.drain_bg(1)
        P.wait("gpsimd", "s_ve", km_ve[NH - 1])
        kmst = P.dma("gpsimd", "krw", km_in, vrow2[0:1, 0:NH])
        P.wait("gpsimd", "krw", kmst)
        ag_m = P.cc("AllGather", groups4, [km_in.opt()], [km_T.opt()])
        P.wait("gpsimd", "kvw0", kng[30])
        P.wait("gpsimd", "kvw1", kng[31])
        ag_k1 = P.cc("AllGather", groups4, [k_in1.opt()], [k_T1.opt()])

        P.wait("sync", "s_cc", kvbV_ag_v[0])
        P.wait("sync", "s_pe", P.cur("s_pe"))
        P.psum_fence()
        spots = [AT_KB[0], AT_KB[0] + 4096, AT_SQK, AT_KRA]
        vld = []
        for c in range(4):
            vld.append(P.dma("sync", "tab", AB[:, spots[c]:spots[c] + 4096],
                             kvbV_T[c * 128:(c + 1) * 128, :]))
        vev, vtg = {}, {}
        for tt in range(8):
            P.wait("tensor", "tab", vld[3])
            if tt >= 1:
                P.wait("tensor", "s_ve", vev[tt - 1])
            for c in range(4):
                lt = AB[:, CK + c * TOK + tt * 128: CK + c * TOK + (tt + 1) * 128]
                mv_ = AB[:, spots[c]:spots[c] + 4096]
                for vb in range(4):
                    f, l = (c == 0), (c == 3)
                    P.op("tensor", lambda e, lt=lt, mv=mv_, vb=vb, psx=PS[vb], f=f, l=l:
                         e.matmul(psx[:, 0:512], lt,
                                  mv[:, vb * 1024:vb * 1024 + 512], start=f, stop=l))
                    mm = P.op("tensor", lambda e, lt=lt, mv=mv_, vb=vb, psx=PS[vb],
                              f=f, l=l:
                              e.matmul(psx[:, 512:1024], lt,
                                       mv[:, vb * 1024 + 512:(vb + 1) * 1024],
                                       start=f, stop=l),
                              inc=("s_pe", 1) if (l and vb == 3) else None)
            vout = AB[:, AT_VB[1]:AT_VB[1] + 4096]
            P.wait("vector", "s_pe", mm)
            if tt >= 1:
                P.wait("vector", "vst", vtg[tt - 1])
            cvl = None
            for vb in range(4):
                cvl = P.op("vector", lambda e, o=vout[:, vb * 1024:(vb + 1) * 1024],
                           psx=PS[vb]: e.tensor_copy(o, psx[:, :]), inc=("s_ve", 1))
            vev[tt] = cvl
            P.wait("gpsimd", "s_ve", cvl)
            P.dma("gpsimd", "vst", v_in0[tt * 128:(tt + 1) * 128, :],
                  vout[:, 0:NH * VD // 2])
            vtg[tt] = P.dma("gpsimd", "vst", v_in1[tt * 128:(tt + 1) * 128, :],
                            vout[:, NH * VD // 2:])
            P.drain_bg(1)
        P.wait("gpsimd", "vst", vtg[7])
        ag_v0 = P.cc("AllGather", groups4, [v_in0.opt()], [v_T0.opt()])
        ag_v1 = P.cc("AllGather", groups4, [v_in1.opt()], [v_T1.opt()])
        if debug:
            P.wait("gpsimd", "s_cc", ag_v1)
            P.dma("gpsimd", "krw", dbg["d_key"][0:4 * NH * ND // 2, :], k_T0)
            P.dma("gpsimd", "krw", dbg["d_key"][4 * NH * ND // 2:, :], k_T1)
            P.dma("gpsimd", "krw", dbg["d_vv"][:, 0:NH * VD // 2], v_T0)
            P.dma("gpsimd", "krw", dbg["d_vv"][:, NH * VD // 2:], v_T1)

        P.wait("sync", "s_cc", qa_ag_v[0])
        P.wait("sync", "s_pe", P.cur("s_pe"))
        P.psum_fence()
        mm_of, cv_of = {}, {}
        for qt in range(NQRCH):
            par = qt % 2
            wb = AB[:, AT_KB[0] + par * 4096: AT_KB[0] + par * 4096 + 4096]
            if qt >= 2:
                P.wait("sync", "s_pe", mm_of[qt - 2])
            lw = P.dma("sync", "w%d" % par, wb.rearrange("p (c f) -> p c f", c=NDCH),
                       qa_T[:, qt * 128:(qt + 1) * 128].rearrange(
                           "(c p) f -> p c f", p=128))
            psq = PS[par]
            P.wait("tensor", "w%d" % par, lw)
            P.wait("tensor", "s_ve", xn_ready)
            if qt >= 2:
                P.wait("tensor", "s_ve", cv_of[qt - 2])
            for c in range(NDCH):
                lt = wb[:, c * 128:(c + 1) * 128]
                f, l = (c == 0), (c == NDCH - 1)
                P.op("tensor", lambda e, lt=lt, c=c, psq=psq, f=f, l=l:
                     e.matmul(psq[:, 0:512], lt, xnT[:, c * TOK:c * TOK + 512],
                              start=f, stop=l))
                mm = P.op("tensor", lambda e, lt=lt, c=c, psq=psq, f=f, l=l:
                          e.matmul(psq[:, 512:1024], lt,
                                   xnT[:, c * TOK + 512:(c + 1) * TOK],
                                   start=f, stop=l), inc=("s_pe", 1) if l else None)
            mm_of[qt] = mm
            qdst = AB[:, AT_QC + qt * TOK: AT_QC + (qt + 1) * TOK]
            P.wait("vector", "s_pe", mm)
            cv = P.op("vector", lambda e, o=qdst, psq=psq:
                      e.tensor_copy(o, psq[:, :]), inc=("s_ve", 1))
            cv_of[qt] = cv
            sq = AB[:, AT_R1:AT_R1 + TOK]
            P.wait("scalar", "s_ve", cv)
            svv = P.op("scalar", lambda e, o=sq, i=qdst:
                       e.activation(o, i, AF.Square), inc=("s_ac", 1))
            P.wait("tensor", "s_ac", svv)
            f, l = (qt == 0), (qt == NQRCH - 1)
            P.op("tensor", lambda e, i=sq, f=f, l=l:
                 e.matmul(ps2[0:1, 0:512], ones_b[:, :], i[:, 0:512], start=f, stop=l))
            P.op("tensor", lambda e, i=sq, f=f, l=l:
                 e.matmul(ps2[0:1, 512:1024], ones_b[:, :], i[:, 512:1024],
                          start=f, stop=l), inc=("s_pe", 1))
        P.wait("scalar", "s_pe", P.cur("s_pe"))
        nrm = bc0[0:1, :]
        a1 = P.op("scalar", lambda e: e.activation(nrm, ps2[0:1, :], AF.Sqrt,
                                                   scale=1.0 / QR), inc=("s_ac", 1))
        P.wait("vector", "s_ac", a1)
        P.op("vector", lambda e: e.tensor_scalar_add(nrm, nrm, EPS_RMS))
        v1 = P.op("vector", lambda e: e.reciprocal(nrm, nrm), inc=("s_ve", 1))
        P.wait("tensor", "s_ve", v1)
        P.psum_fence()
        bv = bcast_row(nrm, ps0, bc0)
        for qt in range(NQRCH):
            qd = AB[:, AT_QC + qt * TOK: AT_QC + (qt + 1) * TOK]
            P.wait("vector", "s_ve", bv)
            P.op("vector", lambda e, q=qd: e.tensor_mul(q, q, bc0[:, :]),
                 inc=("s_ve", 1) if qt == NQRCH - 1 else None)
        qc_done = P.cur("s_ve")

        if debug:
            P.wait("gpsimd", "s_ve", qc_done)
            for qt in range(NQRCH):
                P.dma("gpsimd", "krw", dbg["d_qc"][qt * 128:(qt + 1) * 128, :],
                      AB[:, AT_QC + qt * TOK: AT_QC + (qt + 1) * TOK])

        P.sync_fence()
        P.dma("sync", "tab", AB[0:64, AT_KB[0]:AT_KB[0] + S], cosk)
        csk_ld = P.dma("sync", "tab", AB[0:64, AT_KB[1]:AT_KB[1] + S], sink)
        P.dma("sync", "tab", AB[0:64, AT_CSQ:AT_CSQ + TOK], cosq)
        csq_ld = P.dma("sync", "tab", AB[0:64, AT_SSQ:AT_SSQ + TOK], sinq)
        P.wait("sync", "s_cc", ag_r)
        for r in range(4):
            P.dma("sync", "tab", AB[0:64, AT_KRA + r * TOK: AT_KRA + (r + 1) * TOK],
                  kr_T[r * RD:(r + 1) * RD, :])
        kr_raw_done = P.cur("tab")
        P.wait("sync", "tab", kr_raw_done)
        P.dma("sync", "qsw", AB[0:32, AT_SQK:AT_SQK + S], AB[32:64, AT_KRA:AT_KRA + S])
        krsw_ld = P.dma("sync", "qsw", AB[32:64, AT_SQK:AT_SQK + S],
                        AB[0:32, AT_KRA:AT_KRA + S])
        P.wait("vector", "qsw", krsw_ld)
        P.wait("vector", "tab", kr_raw_done)
        P.wait("vector", "s_pe", P.cur("s_pe"))
        KRG = AB[0:64, AT_KRA:AT_KRA + S]
        KRSW = AB[0:64, AT_SQK:AT_SQK + S]
        v1 = P.op("vector", lambda e: e.tensor_mul(KRG, KRG,
                                                   AB[0:64, AT_KB[0]:AT_KB[0] + S]),
                  inc=("s_ve", 1))
        v2 = P.op("vector", lambda e: e.tensor_mul(KRSW, KRSW,
                                                   AB[0:64, AT_KB[1]:AT_KB[1] + S]),
                  inc=("s_ve", 1))
        P.wait("vector", "s_ve", v2)
        v3 = P.op("vector", lambda e: e.tensor_add(KRG, KRG, KRSW), inc=("s_ve", 1))
        v4 = P.op("vector", lambda e: e.memset(AB[64:65, AT_KRA:AT_KRA + S], 1.0),
                  inc=("s_ve", 1))
        kr_ready = v4
        kr_ready = v4
        P.wait("sync", "s_cc", ag_m)
        P.dma("sync", "tab", vrow2[0:1, 0:NH], km_T[0:1, :])
        P.dma("sync", "tab", vrow1[0:1, 0:NH], km_T[1:2, :])
        P.dma("sync", "tab", vrow1[0:1, NH:2 * NH], km_T[2:3, :])
        km_rows = P.dma("sync", "tab", vrow1[0:1, 2 * NH:3 * NH], km_T[3:4, :])
        P.wait("vector", "tab", km_rows)
        P.op("vector", lambda e: e.tensor_max(vrow2[0:1, 0:NH], vrow2[0:1, 0:NH],
                                              vrow1[0:1, 0:NH]))
        P.op("vector", lambda e: e.tensor_max(vrow2[0:1, 0:NH], vrow2[0:1, 0:NH],
                                              vrow1[0:1, NH:2 * NH]))
        kmx = P.op("vector", lambda e: e.tensor_max(vrow2[0:1, 0:NH], vrow2[0:1, 0:NH],
                                                    vrow1[0:1, 2 * NH:3 * NH]),
                   inc=("s_ve", 1))

        out_allT = xnT
        head_entry_pe = P.cur("s_pe")
        AT_QN2 = 47360
        AT_QA2 = 48384
        QNs = [AT_QN, AT_QN2]
        QAs = [AT_QA, AT_QA2]
        HB = NH * ND // 2
        st = {}

        def keyb_of(h):
            return AB[:, AT_KB[h % 2]:AT_KB[h % 2] + 4096]

        def vbuf_of(h):
            return AB[:, AT_VB[h % 2]:AT_VB[h % 2] + 4096]

        def qwb_of(h):
            base = AT_QW if h % 2 == 0 else AT_SQK
            return AB[:, base:base + NQRCH * QK]

        def emit_loads(h):
            parb = h % 2
            k_Th, v_Th = (k_T0, v_T0) if h < 16 else (k_T1, v_T1)
            hh = h % 16
            st.setdefault(h, {})
            P.wait("sync", "s_cc", max(ag_k0, ag_v0) if h < 16
                   else max(ag_k1, ag_v1))
            P.wait("sync", "s_pe",
                   st[h - 2]["av_last"] if h >= 2 else head_entry_pe)
            if h < 2:
                P.wait("sync", "s_ve", kr_ready)
            for r in range(4):
                P.dma("sync", "kv%d" % parb, keyb_of(h)[:, r * TOK:(r + 1) * TOK],
                      k_Th[r * HB + hh * ND: r * HB + (hh + 1) * ND, :])
            st[h]["kv_ld"] = P.dma(
                "sync", "kv%d" % parb,
                vbuf_of(h).rearrange("p (c f) -> p c f", c=32),
                v_Th[:, hh * VD:(hh + 1) * VD].rearrange("(c p) f -> p c f", p=128))
            P.wait("sync", "s_cc", qb_ag_v[0])
            st[h]["lqw"] = P.dma(
                "sync", "qw", qwb_of(h).rearrange("p (c f) -> p c f", c=NQRCH),
                qb_T[:, h * QK:(h + 1) * QK].rearrange("(c p) f -> p c f", p=128))

        def emit_qbn(h):
            qwb = qwb_of(h)
            P.wait("tensor", "qw", st[h]["lqw"])
            P.wait("tensor", "s_ve", qc_done)
            if h >= 1:
                P.wait("tensor", "s_ac", st[h - 1]["exp"][30])
            for c in range(NQRCH):
                lt = qwb[:, c * QK:c * QK + 128]
                f, l = (c == 0), (c == NQRCH - 1)
                P.op("tensor", lambda e, lt=lt, c=c, f=f, l=l:
                     e.matmul(ps0[:, 0:512], lt,
                              AB[:, AT_QC + c * TOK:AT_QC + c * TOK + 512],
                              start=f, stop=l))
                mm = P.op("tensor", lambda e, lt=lt, c=c, f=f, l=l:
                          e.matmul(ps0[:, 512:1024], lt,
                                   AB[:, AT_QC + c * TOK + 512:AT_QC + (c + 1) * TOK],
                                   start=f, stop=l), inc=("s_pe", 1) if l else None)
            st[h]["qbn_pe"] = mm

        def emit_qbr(h):
            qwb = qwb_of(h)
            if h >= 1:
                P.wait("tensor", "s_ac", st[h - 1]["exp"][31])
            for c in range(NQRCH):
                lt = qwb[:, c * QK + 128:(c + 1) * QK]
                f, l = (c == 0), (c == NQRCH - 1)
                P.op("tensor", lambda e, lt=lt, c=c, f=f, l=l:
                     e.matmul(ps1[0:64, 0:512], lt,
                              AB[:, AT_QC + c * TOK:AT_QC + c * TOK + 512],
                              start=f, stop=l))
                mm = P.op("tensor", lambda e, lt=lt, c=c, f=f, l=l:
                          e.matmul(ps1[0:64, 512:1024], lt,
                                   AB[:, AT_QC + c * TOK + 512:AT_QC + (c + 1) * TOK],
                                   start=f, stop=l), inc=("s_pe", 1) if l else None)
            st[h]["qbr_pe"] = mm

        def emit_preA3(h):
            qn_sb = AB[:, QNs[h % 2]:QNs[h % 2] + TOK]
            QRA = AB[0:64, QAs[h % 2]:QAs[h % 2] + TOK]
            QRR = AB[0:64, AT_R1:AT_R1 + TOK]
            QSW = AB[0:64, AT_R2:AT_R2 + TOK]
            P.wait("vector", "s_pe", st[h]["qbn_pe"])
            if h >= 2:
                P.wait("vector", "s_pe", st[h - 2]["sc_last"])
            qnv = P.op("vector", lambda e, o=qn_sb: e.tensor_copy(o, ps0[:, :]),
                       inc=("s_ve", 1))
            st[h]["qnv"] = qnv
            P.wait("vector", "s_pe", st[h]["qbr_pe"])
            if h >= 1:
                P.wait("vector", "s_pe", st[h - 1]["qsq_pe"])
            P.wait("vector", "tab", csq_ld)
            rv1 = P.op("vector", lambda e: e.tensor_copy(QRR, ps1[0:64, :]),
                       inc=("s_ve", 1))
            st[h]["rv1"] = rv1
            P.wait("sync", "s_ve", rv1)
            if h >= 1:
                P.wait("sync", "qsw", st[h - 1]["swap"])
            P.dma("sync", "qsw", AB[0:32, AT_R2:AT_R2 + TOK],
                  AB[32:64, AT_R1:AT_R1 + TOK])
            d2 = P.dma("sync", "qsw", AB[32:64, AT_R2:AT_R2 + TOK],
                       AB[0:32, AT_R1:AT_R1 + TOK])
            st[h]["swap"] = d2
            P.wait("vector", "qsw", d2)
            P.op("vector", lambda e: e.tensor_mul(QSW, QSW,
                                                  AB[0:64, AT_SSQ:AT_SSQ + TOK]),
                 inc=("s_ve", 1))
            rv3 = P.op("vector", lambda e: e.tensor_mul(QRR, QRR,
                                                        AB[0:64, AT_CSQ:AT_CSQ + TOK]),
                       inc=("s_ve", 1))
            P.wait("vector", "s_ve", rv3)
            rv4 = P.op("vector", lambda e: e.tensor_add(QRA, QRR, QSW),
                       inc=("s_ve", 1))
            sqq = AB[:, AT_R1:AT_R1 + TOK]
            sqr = AB[0:64, AT_R2:AT_R2 + TOK]
            P.wait("scalar", "s_ve", rv4)
            P.op("scalar", lambda e, o=sqq, i=qn_sb: e.activation(o, i, AF.Square),
                 inc=("s_ac", 1))
            st[h]["sq_ac"] = P.op("scalar", lambda e, o=sqr, i=QRA:
                                  e.activation(o, i, AF.Square), inc=("s_ac", 1))

        def emit_partB(h):
            sqq = AB[:, AT_R1:AT_R1 + TOK]
            sqr = AB[0:64, AT_R2:AT_R2 + TOK]
            P.wait("tensor", "s_ac", st[h]["sq_ac"])
            P.wait("tensor", "s_ve", st[h]["qnv"])
            P.op("tensor", lambda e: e.matmul(ps0[0:1, 0:512], ones_b[:, :],
                                              sqq[:, 0:512], start=True, stop=False))
            P.op("tensor", lambda e: e.matmul(ps0[0:1, 512:1024], ones_b[:, :],
                                              sqq[:, 512:1024], start=True, stop=False))
            P.op("tensor", lambda e: e.matmul(ps0[0:1, 0:512], ones_b[0:64, :],
                                              sqr[:, 0:512], start=False, stop=True))
            qsq_pe = P.op("tensor", lambda e:
                          e.matmul(ps0[0:1, 512:1024], ones_b[0:64, :],
                                   sqr[:, 512:1024], start=False, stop=True),
                          inc=("s_pe", 1))
            st[h]["qsq_pe"] = qsq_pe
            P.wait("scalar", "s_pe", qsq_pe)
            P.wait("scalar", "s_ve", kmx)
            if h >= 1:
                P.wait("scalar", "aug", st[h - 1]["augd"])
            qn_a = P.op("scalar", lambda e, h=h:
                        e.activation(vrow1[0:1, :], ps0[0:1, :], AF.Sqrt,
                                     scale=vrow2[0:1, h:h + 1]),
                        inc=("s_ac", 1))
            st[h]["qn_a"] = qn_a
            P.wait("vector", "s_ac", qn_a)
            augv = P.op("vector", lambda e:
                        e.tensor_scalar(vrow1[0:1, :], vrow1[0:1, :],
                                        -1.0, OFFS, ALU.mult, ALU.add),
                        inc=("s_ve", 1))
            P.wait("gpsimd", "s_ve", augv)
            if h >= 2:
                P.wait("gpsimd", "s_pe", st[h - 2]["sc_last"])
            st[h]["augd"] = P.dma("gpsimd", "aug",
                                  AB[64:65, QAs[h % 2]:QAs[h % 2] + TOK],
                                  vrow1[0:1, :])

        def emit_pipe(h):
            parb = h % 2
            keyb = keyb_of(h)
            vbuf = vbuf_of(h)
            qn_sb = AB[:, QNs[h % 2]:QNs[h % 2] + TOK]
            qac = QAs[h % 2]
            kt_pe, exp_of, av_pe = {}, {}, {}
            st[h]["exp"] = exp_of

            def emit_score(kt):
                pss = PS[kt % 2]
                lkn = keyb[:, kt * 128:(kt + 1) * 128]
                lkr = AB[0:65, AT_KRA + kt * 128: AT_KRA + (kt + 1) * 128]
                if kt == 0:
                    P.wait("tensor", "aug", st[h]["augd"])
                    P.wait("tensor", "s_ve", kr_ready)
                    P.wait("tensor", "kv%d" % parb, st[h]["kv_ld"])
                    P.wait("tensor", "s_ac", st[h]["qn_a"])
                if kt == 1 and h >= 1:
                    P.wait("tensor", "s_ve", st[h - 1]["bc_cv"])
                if kt >= 2:
                    P.wait("tensor", "s_ac", exp_of[kt - 2])
                P.op("tensor", lambda e, lkn=lkn, pss=pss:
                     e.matmul(pss[:, 0:512], lkn, qn_sb[:, 0:512],
                              start=True, stop=False))
                P.op("tensor", lambda e, lkn=lkn, pss=pss:
                     e.matmul(pss[:, 512:1024], lkn, qn_sb[:, 512:1024],
                              start=True, stop=False))
                P.op("tensor", lambda e, lkr=lkr, pss=pss:
                     e.matmul(pss[:, 0:512], lkr, AB[0:65, qac:qac + 512],
                              start=False, stop=True))
                return P.op("tensor", lambda e, lkr=lkr, pss=pss:
                            e.matmul(pss[:, 512:1024], lkr,
                                     AB[0:65, qac + 512:qac + TOK],
                                     start=False, stop=True), inc=("s_pe", 1))

            def emit_exp(kt):
                pss = PS[kt % 2]
                atb = AB[:, AT_ATB[kt % 2]:AT_ATB[kt % 2] + TOK]
                P.wait("scalar", "s_pe", kt_pe[kt])
                if kt >= 2:
                    P.wait("scalar", "s_pe", av_pe[kt - 2])
                return P.op("scalar", lambda e, o=atb, pss=pss:
                            e.activation(o, pss[:, :], AF.Exp), inc=("s_ac", 1))

            def emit_av(kt):
                atb = AB[:, AT_ATB[kt % 2]:AT_ATB[kt % 2] + TOK]
                lv = vbuf[:, kt * 128:(kt + 1) * 128]
                f, l = (kt == 0), (kt == 31)
                P.wait("tensor", "s_ac", exp_of[kt])
                if kt == 0 and h >= 1:
                    P.wait("tensor", "s_ve", st[h - 1]["hv"])
                    P.wait("tensor", "s_ve", st[h - 1]["recip"])
                P.op("tensor", lambda e, lv=lv, atb=atb, f=f, l=l:
                     e.matmul(ps2[:, 0:512], lv, atb[:, 0:512], start=f, stop=l))
                P.op("tensor", lambda e, lv=lv, atb=atb, f=f, l=l:
                     e.matmul(ps2[:, 512:1024], lv, atb[:, 512:1024], start=f, stop=l))
                P.op("tensor", lambda e, atb=atb, f=f, l=l:
                     e.matmul(ps3[0:1, 0:512], ones_b[:, :], atb[:, 0:512],
                              start=f, stop=l))
                return P.op("tensor", lambda e, atb=atb, f=f, l=l:
                            e.matmul(ps3[0:1, 512:1024], ones_b[:, :],
                                     atb[:, 512:1024], start=f, stop=l),
                            inc=("s_pe", 1))

            if h + 1 < NH:
                emit_loads(h + 1)
            kt_pe[0] = emit_score(0)
            kt_pe[1] = emit_score(1)
            exp_of[0] = emit_exp(0)
            for kt in range(32):
                if kt + 2 < 32:
                    kt_pe[kt + 2] = emit_score(kt + 2)
                if kt + 1 < 32:
                    exp_of[kt + 1] = emit_exp(kt + 1)
                if kt == 29 and h + 1 < NH:
                    emit_qbn(h + 1)
                if kt == 30 and h + 1 < NH:
                    emit_qbr(h + 1)
                av_pe[kt] = emit_av(kt)
            st[h]["sc_last"] = kt_pe[31]
            st[h]["av_last"] = av_pe[31]
            if h + 1 < NH:
                emit_preA3(h + 1)
            P.wait("vector", "s_pe", av_pe[31])
            rv = P.op("vector", lambda e: e.reciprocal(bc1[0:1, :], ps3[0:1, :]),
                      inc=("s_ve", 1))
            st[h]["recip"] = rv
            P.wait("tensor", "s_ve", rv)
            if h + 1 < NH:
                P.wait("tensor", "s_ve", st[h + 1]["rv1"])
            else:
                P.wait("tensor", "s_ac", exp_of[31])
            bvh = bcast_row(bc1[0:1, :], ps1, bc1)
            st[h]["bc_cv"] = bvh
            P.wait("vector", "s_ve", bvh)
            st[h]["hv"] = P.op("vector", lambda e, h=h:
                               e.tensor_tensor(out_allT[:, h * TOK:(h + 1) * TOK],
                                               ps2[:, :], bc1[:, :], ALU.mult),
                               inc=("s_ve", 1))

        emit_loads(0)
        P.psum_fence()
        emit_qbn(0)
        emit_qbr(0)
        emit_preA3(0)
        for h in range(NH):
            emit_partB(h)
            emit_pipe(h)

        if debug:
            P.wait("gpsimd", "s_ve", st[NH - 1]["hv"])
            for c in range(NDCH):
                P.dma("gpsimd", "krw", dbg["d_oT"][c * 128:(c + 1) * 128, :],
                      out_allT[:, c * TOK:(c + 1) * TOK])

        P.wait("sync", "s_cc", wo_ag_v[0])
        P.wait("sync", "s_pe", P.cur("s_pe"))
        P.wait("sync", "s_ve", P.cur("s_ve"))
        P.wait_tot("sync", "xw0", "xw1")
        P.psum_fence()
        oall_done = st[NH - 1]["hv"]
        wo_mm, wo_ve, gpev2, xbv2 = {}, {}, {}, {}
        for dt in range(NDCH):
            par = dt % 2
            swb = AB[:, AT_VB[par]:AT_VB[par] + 4096]
            if dt >= 2:
                P.wait("sync", "s_pe", wo_mm[dt - 2])
            lw = P.dma("sync", "w%d" % par, swb.rearrange("p (c f) -> p c f", c=NDCH),
                       wo_T[:, dt * 128:(dt + 1) * 128].rearrange(
                           "(c p) f -> p c f", p=128))
            psd = PS[par]
            P.wait("tensor", "w%d" % par, lw)
            P.wait("tensor", "s_ve", oall_done)
            if dt >= 2:
                P.wait("tensor", "s_ve", wo_ve[dt - 2])
            for c in range(NDCH):
                lt = swb[:, c * 128:(c + 1) * 128]
                f, l = (c == 0), (c == NDCH - 1)
                P.op("tensor", lambda e, lt=lt, c=c, psd=psd, f=f, l=l:
                     e.matmul(psd[:, 0:512], lt,
                              out_allT[:, c * TOK:c * TOK + 512], start=f, stop=l))
                mm = P.op("tensor", lambda e, lt=lt, c=c, psd=psd, f=f, l=l:
                          e.matmul(psd[:, 512:1024], lt,
                                   out_allT[:, c * TOK + 512:(c + 1) * TOK],
                                   start=f, stop=l), inc=("s_pe", 1) if l else None)
            wo_mm[dt] = mm
            fb = AFX[:, par * TOK:(par + 1) * TOK]
            if dt >= 2:
                P.wait("sync", "xw%d" % par, gpev2[dt - 2])
            P.wait("sync", "s_ve", xbv2.get(dt - 1, 0))
            ld2 = P.dma("sync", "xfb", xfb[:, :], xres1[dt * 128:(dt + 1) * 128, :])
            P.wait("vector", "s_pe", mm)
            P.wait("vector", "xfb", ld2)
            if dt >= 2:
                P.wait("vector", "xw%d" % par, gpev2[dt - 2])
            vv = P.op("vector", lambda e, o=fb, psd=psd:
                      e.tensor_add(o, psd[:, :], xfb[:, :]), inc=("s_ve", 1))
            xbv2[dt] = vv
            wo_ve[dt] = vv
            P.wait("gpsimd", "s_ve", vv)
            gpev2[dt] = P.dma("gpsimd", "xw%d" % par,
                              xres2[dt * 128:(dt + 1) * 128, :], fb)
            stats_tile(fb, dt)
        xn_ready = ln_apply(xres2)
        if debug:
            P.wait("gpsimd", "s_ve", xn_ready)
            for c in range(NDCH):
                P.dma("gpsimd", "krw", dbg["d_x2"][c * 128:(c + 1) * 128, :],
                      xres2[c * 128:(c + 1) * 128, :])

        emit_ffn_A(first_pass=False, xn_ready=xn_ready)
        emit_ffn_B(xbase_d=xres2, xdst_d=None, final=True)
        P.wait_tot("sync", "xw0", "xw1")
        P.wait_tot("tensor", "xw0", "xw1")

        @block.sync
        def _(e):
            for fn in P.ops["sync"]:
                fn(e)

        @block.tensor
        def _(e):
            for fn in P.ops["tensor"]:
                fn(e)

        @block.vector
        def _(e):
            for fn in P.ops["vector"]:
                fn(e)

        @block.scalar
        def _(e):
            for fn in P.ops["scalar"]:
                fn(e)

        @block.gpsimd
        def _(e):
            for fn in P.ops["gpsimd"]:
                fn(e)

    return nc, dbg


def _rope_tables():
    inv = 1.0 / (10000.0 ** (np.arange(0, RD, 2, dtype=np.float64) / RD))
    t = np.arange(S, dtype=np.float64)
    fr = np.outer(t, inv)
    emb = np.concatenate([fr, fr], axis=1)
    cos = np.ascontiguousarray(np.cos(emb).T).astype(ml_dtypes.bfloat16)
    sin = np.sin(emb).T
    sinS = sin.copy()
    sinS[:RD // 2] *= -1.0
    sinS = np.ascontiguousarray(sinS).astype(ml_dtypes.bfloat16)
    return cos, sinS


def prepare_inputs(inputs):
    x = np.asarray(inputs["L0_attn_hidden_states"])
    w_qa = np.asarray(inputs["L0_attn_w_qa"])
    g_qa = np.asarray(inputs["L0_attn_g_qa"]).reshape(QR, 1)
    w_qb = np.asarray(inputs["L0_attn_w_qb"])
    w_kva = np.asarray(inputs["L0_attn_w_kva"])
    g_kva = np.asarray(inputs["L0_attn_g_kva"]).reshape(KVR, 1)
    w_kvb = np.asarray(inputs["L0_attn_w_kvb"])
    w_o = np.asarray(inputs["L0_attn_w_o"])
    gate_w = np.asarray(inputs["L0_ffn_gate_w"])
    gate_s = np.asarray(inputs["L0_ffn_gate_s"]).reshape(DFF, 1)
    up_w = np.asarray(inputs["L0_ffn_up_w"])
    up_s = np.asarray(inputs["L0_ffn_up_s"]).reshape(DFF, 1)
    down_w = np.asarray(inputs["L0_ffn_down_w"])
    down_s = np.asarray(inputs["L0_ffn_down_s"]).reshape(D, 1)

    kvb4 = w_kvb.reshape(KVR, NH, ND + VD)
    kvbN = np.ascontiguousarray(kvb4[:, :, :ND].reshape(KVR, NH * ND))
    kvbV = np.ascontiguousarray(kvb4[:, :, ND:].reshape(KVR, NH * VD))
    down_pad = np.zeros((D, DFFP), dtype=np.int32)
    down_pad[:, :DFF] = down_w
    cos, sinS = _rope_tables()

    in_maps = []
    for c in range(NCORE):
        b = c // 4
        r0 = (c % 4) * TOK
        r = c
        m = {
            "xT_in": np.ascontiguousarray(x[b, r0:r0 + TOK, :].T).astype(np.float32),
            "gate_sl": np.ascontiguousarray(gate_w[:, r * DPC:(r + 1) * DPC]),
            "up_sl": np.ascontiguousarray(up_w[:, r * DPC:(r + 1) * DPC]),
            "down_sl": np.ascontiguousarray(down_pad[:, r * DFPC:(r + 1) * DFPC]),
            "gate_s": np.ascontiguousarray(gate_s.reshape(NFT, 128).T).astype(np.float32),
            "up_s": np.ascontiguousarray(up_s.reshape(NFT, 128).T).astype(np.float32),
            "down_s": np.ascontiguousarray(down_s.reshape(NDCH, 128).T).astype(np.float32),
            "qa_sl": np.ascontiguousarray(w_qa[r * DPC:(r + 1) * DPC, :]).astype(np.float32),
            "qb_sl": np.ascontiguousarray(w_qb[r * QRPC:(r + 1) * QRPC, :]).astype(np.float32),
            "g_qa_sl": np.ascontiguousarray(g_qa[r * QRPC:(r + 1) * QRPC, :]).astype(np.float32),
            "kva_sl": np.ascontiguousarray(w_kva[r * DPC:(r + 1) * DPC, :]).astype(np.float32),
            "kvbN_sl": np.ascontiguousarray(kvbN[r * KVRPC:(r + 1) * KVRPC, :]).astype(np.float32),
            "kvbV_sl": np.ascontiguousarray(kvbV[r * KVRPC:(r + 1) * KVRPC, :]).astype(np.float32),
            "g_kva_sl": np.ascontiguousarray(g_kva[r * KVRPC:(r + 1) * KVRPC, :]).astype(np.float32),
            "wo_sl": np.ascontiguousarray(w_o[r * DPC:(r + 1) * DPC, :]).astype(np.float32),
            "cosq": np.ascontiguousarray(cos[:, r0:r0 + TOK]),
            "sinq": np.ascontiguousarray(sinS[:, r0:r0 + TOK]),
            "cosk": cos,
            "sink": sinS,
        }
        in_maps.append(m)
    return in_maps


def kernel(**inputs):
    in_maps = prepare_inputs(inputs)
    nc, _ = build(debug=False)
    res = run_bass_kernel_spmd(nc, in_maps, core_ids=list(range(NCORE)))
    out = np.empty((B, S, D), dtype=np.float32)
    for c in range(NCORE):
        b = c // 4
        r0 = (c % 4) * TOK
        out[b, r0:r0 + TOK, :] = res.results[c]["out_xT"].T
    return out
